# version 19
# baseline (speedup 1.0000x reference)
import math
from contextlib import ExitStack, contextmanager

import numpy as np
import concourse.bass as bass
import concourse.mybir as mybir
from concourse.bass_utils import run_bass_kernel_spmd

F32 = mybir.dt.float32
BF16 = mybir.dt.bfloat16
I32 = mybir.dt.int32
AF = mybir.ActivationFunctionType
ALU = mybir.AluOpType
AXX = mybir.AxisListType.X

Z1 = 1.2815515655446004
PHI_Z1 = 0.17549833193248682


def make_cfg(S=2048, D=1024, HF=4096, NH=16, NB=2):
    cfg = dict(S=S, D=D, HF=HF, NH=NH, NB=NB, DK=64)
    cfg["TOPK"] = max(1, int((1.0 - 0.9) * S))
    cfg["NT"] = S // 128
    cfg["DCH"] = D // 128
    cfg["HCH"] = HF // 128
    cfg["QCH"] = 3 * D // 128
    cfg["KCH"] = S // 128
    cfg["QW"] = min(256, S)
    cfg["NQ"] = S // cfg["QW"]
    cfg["GQ"] = min(4, cfg["NT"])
    cfg["NG"] = cfg["NT"] // cfg["GQ"]
    return cfg


class Ctx:
    def __init__(self, nc, es):
        self.nc = nc
        self.sem = {}
        self.cnt = {}
        self.uid = 0
        for name in ("V", "A", "P", "G", "Q", "W"):
            self.sem[name] = es.enter_context(nc.semaphore("sem" + name))
            self.cnt[name] = 0

    def name(self, n):
        self.uid += 1
        return f"{n}_{self.uid}"

    def _mark(self, name, instr, n):
        instr.then_inc(self.sem[name], n)
        self.cnt[name] += n
        return instr

    def v(self, i):
        return self._mark("V", i, 1)

    def a(self, i):
        return self._mark("A", i, 1)

    def p(self, i):
        return self._mark("P", i, 1)

    def g(self, i):
        return self._mark("G", i, 1)

    def qd(self, i):
        return self._mark("Q", i, 16)

    def wd(self, i):
        return self._mark("W", i, 16)

    def snap(self):
        return dict(self.cnt)

    def wait(self, h, snap, *names):
        for n in names:
            if snap[n] > 0:
                h.wait_ge(self.sem[n], snap[n])

    def waitv(self, h, name, val):
        if val > 0:
            h.wait_ge(self.sem[name], val)

    def gp(self, h, name="V"):
        h.wait_ge(self.sem[name], self.cnt[name])

    def ga(self, a):
        self.a(a.activation(out=self.adum[0:1, 0:1], in_=self.adum[0:1, 1:2],
                            func=AF.Identity, scale=1.0))

    def pre6(self, a):
        if self.set6 is not None:
            a.add_instruction(mybir.InstLoadActFuncSet(
                name=self.nc.get_next_instruction_name(),
                act_func_set_id=self.set6, ins=[], outs=[]))



def emit_rint(ctx, v, cfg, out_bf, in_f32, i32scr):
    if cfg.get("rint", "cast") == "cast":
        ctx.v(v.tensor_copy(out=i32scr, in_=in_f32))
        ctx.gp(v)
        ctx.v(v.tensor_copy(out=out_bf, in_=i32scr))
    else:
        ctx.v(v.tensor_scalar(out=out_bf, in0=in_f32, scalar1=12582912.0,
                              scalar2=12582912.0, op0=ALU.add, op1=ALU.subtract))


def sp512(n):
    return [(o, min(512, n - o)) for o in range(0, n, 512)]


@contextmanager
def scope():
    es = ExitStack()
    try:
        yield es
    finally:
        es.close()



def build_program(cfg):
    nc = bass.Bass("TRN2", target_bir_lowering=False, detect_race_conditions=False)
    S, D, HF, NB = cfg["S"], cfg["D"], cfg["HF"], cfg["NB"]
    QCH, HCH, NT = cfg["QCH"], cfg["HCH"], cfg["NT"]

    dt = nc.dram_tensor
    t_in = dict(
        x_in=dt("x_in", [NB, S, D], F32, kind="ExternalInput").ap(),
        Wqkv=dt("Wqkv", [3 * D, D], F32, kind="ExternalInput").ap(),
        bqkv=dt("bqkv", [3 * D], F32, kind="ExternalInput").ap(),
        Wo=dt("Wo", [D, D], F32, kind="ExternalInput").ap(),
        bo=dt("bo", [D], F32, kind="ExternalInput").ap(),
        w1=dt("w1", [HF, D], F32, kind="ExternalInput").ap(),
        cw=dt("cw", [HF, 3], F32, kind="ExternalInput").ap(),
        cb=dt("cb", [HF], F32, kind="ExternalInput").ap(),
        w2=dt("w2", [D, HF], F32, kind="ExternalInput").ap(),
    )
    out = dt("out", [NB, S, D], F32, kind="ExternalOutput").ap()
    scr = dict(
        xres=dt("xres", [NB, S, D], F32, kind="Internal").ap(),
        wqkvT=dt("wqkvT", [D, 3 * D], BF16, kind="Internal").ap(),
        woT=dt("woT", [D, D], BF16, kind="Internal").ap(),
        w1qT=dt("w1qT", [D, HF], BF16, kind="Internal").ap(),
        w2qT=dt("w2qT", [HF, D], BF16, kind="Internal").ap(),
        aoT=dt("aoT", [D, S], BF16, kind="Internal").ap(),
        drow=dt("drow", [8, max(S, 128)], F32, kind="Internal").ap(),
        dbg=(dt("dbg", [128, 4096], F32, kind="ExternalOutput").ap()
             if cfg.get("dbg") else None),
        dsc=dt("dsc", [1, 8], F32, kind="Internal").ap(),
    )

    es = ExitStack()
    ctx = Ctx(nc, es)

    def sb(name, shape, d):
        return es.enter_context(nc.sbuf_tensor(name, shape, d))

    per = dict(
        ident=sb("ident", [128, 128], BF16),
        identf=sb("identf", [128, 128], F32),
        epsc=sb("epsc", [128, 3], F32),
        ident2=sb("ident2", [128, 64], BF16),
        onescol=sb("onescol", [128, 1], BF16),
        e65row=sb("e65row", [1, cfg["DK"] + 1], F32),
        iota16=sb("iota16", [128, 16], F32),
        bobc=sb("bobc", [128, D], F32),
        bqkvcol=sb("bqkvcol", [128, QCH], F32),
        cwcol=sb("cwcol", [128, 3 * HCH], F32),
        cbcol=sb("cbcol", [128, HCH], F32),
        wscal=sb("wscal", [128, 4], F32),
        adum=sb("adum", [1, 2], F32),
    )
    ctx.adum = per["adum"]
    ctx.set6 = None
    try:
        from concourse.hw_specs import get_activation_tables
        for _i, _fs in enumerate(get_activation_tables(nc.m.arch).values()):
            if AF.Exp in _fs and AF.Ln in _fs and AF.Sign in _fs:
                ctx.set6 = _i
                break
    except Exception:
        ctx.set6 = 6

    _init_consts(ctx, cfg, per, t_in)
    _prep_weights(ctx, cfg, t_in, scr, per)
    _copy_x(ctx, cfg, t_in["x_in"], scr["xres"])

    stages = cfg.get("stages", ("f1", "at", "f2"))
    for b in range(NB):
        if "f1" in stages:
            with nc.named_scope(f"f1_b{b}"):
                _ffn_sublayer(ctx, cfg, b, scr, per,
                              dst=(out if stages[-1] == "f1" else scr["xres"]))
        if "at" in stages:
            with nc.named_scope(f"at_b{b}"):
                _attn_sublayer(ctx, cfg, b, scr, per)
        if "f2" in stages:
            with nc.named_scope(f"f2_b{b}"):
                _ffn_sublayer(ctx, cfg, b, scr, per, dst=out)

    with nc.Block() as blk:
        fin = ctx.snap()

        @blk.gpsimd
        def _(g):
            ctx.wait(g, fin, "Q", "W", "V", "A", "P", "G")

    es.close()
    return nc



def _init_consts(ctx, cfg, per, t_in):
    nc = ctx.nc
    DK = cfg["DK"]
    with nc.Block() as blk:
        @blk.gpsimd
        def _(g):
            ctx.g(g.memset(per["adum"][:], 0.0))
            ctx.g(g.iota(per["ident"][:], pattern=[[1, 128]], base=0,
                         channel_multiplier=-1,
                         allow_small_or_imprecise_dtypes=True))
            ctx.g(g.iota(per["iota16"][:, 0:8], pattern=[[1, 8]], base=1,
                         channel_multiplier=0,
                         allow_small_or_imprecise_dtypes=True))
            ctx.g(g.iota(per["iota16"][:, 8:16], pattern=[[-1, 8]], base=0,
                         channel_multiplier=0,
                         allow_small_or_imprecise_dtypes=True))
            ctx.g(g.memset(per["onescol"][:], 1.0))
            ctx.g(g.memset(per["e65row"][:], 0.0))
            ctx.g(g.memset(per["e65row"][0:1, DK:DK + 1], 1.0))

        s0 = ctx.snap()

        @blk.vector
        def _(v):
            ctx.wait(v, s0, "G")
            ctx.v(v.tensor_scalar(out=per["ident"][:], in0=per["ident"][:],
                                  scalar1=0.0, scalar2=None, op0=ALU.is_equal))
            ctx.gp(v)
            ctx.v(v.tensor_copy(out=per["identf"][:], in_=per["ident"][:]))
            ctx.v(v.memset(per["epsc"][:, 0:1], 1e-5))
            ctx.v(v.memset(per["epsc"][:, 1:2], 1e-30))
            ctx.v(v.memset(per["epsc"][:, 2:3], 1e-37))

        s1 = ctx.snap()

        @blk.sync
        def _(q):
            ctx.wait(q, s1, "V")
            ctx.qd(q.dma_start(out=per["ident2"][0:64, :],
                               in_=per["ident"][0:64, 0:64]))
            ctx.qd(q.dma_start(out=per["ident2"][64:128, :],
                               in_=per["ident"][0:64, 0:64]))
            ctx.qd(q.dma_start(out=per["bobc"][:],
                               in_=t_in["bo"][None, :].partition_broadcast(128)))
            ctx.qd(q.dma_start(
                out=per["cwcol"][:].rearrange("p (c k) -> p c k", k=3),
                in_=t_in["cw"].rearrange("(c p) k -> p c k", p=128)))

        @blk.gpsimd
        def _(g):
            ctx.wd(g.dma_start(out=per["bqkvcol"][:],
                               in_=t_in["bqkv"].rearrange("(c p) -> p c", p=128),
                               allow_slow_non_contiguous=True))
            ctx.wd(g.dma_start(out=per["cbcol"][:],
                               in_=t_in["cb"].rearrange("(c p) -> p c", p=128),
                               allow_slow_non_contiguous=True))


def _copy_x(ctx, cfg, x_in, xres):
    nc = ctx.nc
    NB, NT, D = cfg["NB"], cfg["NT"], cfg["D"]
    with scope() as es:
        xt = es.enter_context(nc.sbuf_tensor(ctx.name("xcopy"), [128, 2 * D], F32))
        with nc.Block() as blk:
            @blk.sync
            def _(q):
                for k in range(NB * NT):
                    bb, j = divmod(k, NT)
                    buf = xt[:, (k % 2) * D:(k % 2) * D + D]
                    ctx.qd(q.dma_start(out=buf, in_=x_in[bb, 128 * j:128 * (j + 1), :]))
                    ctx.waitv(q, "Q", ctx.cnt["Q"])
                    ctx.qd(q.dma_start(out=xres[bb, 128 * j:128 * (j + 1), :], in_=buf))



def _prep_weights(ctx, cfg, t_in, scr, per):
    nc = ctx.nc
    D, HF, DCH, HCH = cfg["D"], cfg["HF"], cfg["DCH"], cfg["HCH"]
    NC = max(D, HF)

    with scope() as es:
        wtmp = es.enter_context(nc.sbuf_tensor("wtmp", [128, NC], F32))
        wint = es.enter_context(nc.sbuf_tensor("wrm", [128, NC], I32))
        wbf = es.enter_context(nc.sbuf_tensor("wbf", [128, NC], BF16))
        wtr = es.enter_context(nc.sbuf_tensor("wtr", [128, NC], BF16))
        wabs = es.enter_context(nc.sbuf_tensor("wabs", [128, max(HCH, DCH)], F32))
        wsum = es.enter_context(nc.sbuf_tensor("wsum", [128, 1], F32))
        wmisc = es.enter_context(nc.sbuf_tensor("wmisc", [1, 8], F32))
        onef = es.enter_context(nc.sbuf_tensor("onef", [128, 1], F32))
        scal_ps = es.enter_context(nc.psum_tensor("scal_ps", [1, 2], F32))
        tr_ps = es.enter_context(nc.psum_tensor("wtr_ps", [128, NC], BF16))

        with nc.Block() as blk:
            @blk.vector
            def _(v):
                ctx.v(v.memset(onef[:], 1.0))

        jobs = [("qkv", t_in["Wqkv"], scr["wqkvT"], 3 * D // 128, DCH, None),
                ("wo", t_in["Wo"], scr["woT"], DCH, DCH, None),
                ("w1", t_in["w1"], scr["w1qT"], HCH, DCH, 0),
                ("w2", t_in["w2"], scr["w2qT"], DCH, HCH, 2)]

        for (name, src, dst, nrow_ch, ncol_ch, scol) in jobs:
            ncol = ncol_ch * 128
            quant = scol is not None
            if quant:
                for r in range(nrow_ch):
                    with nc.Block() as blk:
                        s0 = ctx.snap()

                        @blk.sync
                        def _(q):
                            ctx.wait(q, s0, "V")
                            ctx.qd(q.dma_start(out=wtmp[:, 0:ncol],
                                               in_=src[128 * r:128 * (r + 1), :]))

                        s1 = ctx.snap()

                        @blk.vector
                        def _(v):
                            ctx.wait(v, s1, "Q")
                            ctx.v(v.tensor_reduce(out=wabs[:, r:r + 1],
                                                  in_=wtmp[:, 0:ncol], axis=AXX,
                                                  op=ALU.add,
                                                  apply_absolute_value=True))
                with nc.Block() as blk:
                    s0 = ctx.snap()

                    @blk.vector
                    def _(v):
                        ctx.wait(v, s0, "Q")
                        ctx.gp(v)
                        ctx.v(v.tensor_reduce(out=wsum[:], in_=wabs[:, 0:nrow_ch],
                                              axis=AXX, op=ALU.add))

                    s1 = ctx.snap()

                    @blk.tensor
                    def _(p):
                        ctx.wait(p, s1, "V")
                        ctx.p(p.matmul(out=scal_ps[0:1, 0:1], lhsT=wsum[:],
                                       rhs=onef[:], start=True, stop=True))

                    s2 = ctx.snap()

                    @blk.vector
                    def _(v):
                        ctx.wait(v, s2, "P")
                        n_el = float(nrow_ch * 128 * ncol)
                        ctx.v(v.tensor_scalar(out=wmisc[0:1, 0:1],
                                              in0=scal_ps[0:1, 0:1],
                                              scalar1=1.0 / n_el, scalar2=1e-5,
                                              op0=ALU.mult, op1=ALU.max))
                        ctx.gp(v)
                        ctx.v(v.reciprocal(out=wmisc[0:1, 1:2], in_=wmisc[0:1, 0:1]))
                        ctx.v(v.tensor_scalar(out=wmisc[0:1, 2:3], in0=wmisc[0:1, 0:1],
                                              scalar1=1.0 / 127.0, scalar2=None,
                                              op0=ALU.mult))

                    s3 = ctx.snap()

                    @blk.sync
                    def _(q):
                        ctx.wait(q, s3, "V")
                        ctx.qd(q.dma_start(out=scr["dsc"][0:1, 0:2],
                                           in_=wmisc[0:1, 1:3]))
                        ctx.waitv(q, "Q", ctx.cnt["Q"])
                        ctx.qd(q.dma_start(
                            out=per["wscal"][:, scol:scol + 2],
                            in_=scr["dsc"][0:1, 0:2].partition_broadcast(128)))
                        if scr.get("dbg") is not None:
                            ctx.waitv(q, "Q", ctx.cnt["Q"])
                            ctx.qd(q.dma_start(
                                out=scr["dbg"][:, 8 * scol:8 * scol + 4],
                                in_=per["wscal"][:, 0:4]))
                            ctx.qd(q.dma_start(
                                out=scr["dbg"][:, 20 + 40 * scol:20 + 40 * scol + nrow_ch],
                                in_=wabs[:, 0:nrow_ch]))

            for r in range(nrow_ch):
                with nc.Block() as blk:
                    s0 = ctx.snap()

                    @blk.sync
                    def _(q):
                        ctx.wait(q, s0, "V", "P")
                        ctx.qd(q.dma_start(out=wtmp[:, 0:ncol],
                                           in_=src[128 * r:128 * (r + 1), :]))

                    s1 = ctx.snap()

                    @blk.vector
                    def _(v):
                        ctx.wait(v, s1, "Q", "W")
                        if quant:
                            ctx.v(v.tensor_scalar(out=wtmp[:, 0:ncol],
                                                  in0=wtmp[:, 0:ncol],
                                                  scalar1=per["wscal"][:, scol:scol + 1],
                                                  scalar2=None, op0=ALU.mult))
                            ctx.gp(v)
                            if cfg.get("rint", "cast") == "cast":
                                ctx.v(v.tensor_copy(out=wint[:, 0:ncol],
                                                    in_=wtmp[:, 0:ncol]))
                            else:
                                ctx.v(v.tensor_scalar(
                                    out=wint[:, 0:ncol].bitcast(F32),
                                    in0=wtmp[:, 0:ncol], scalar1=12582912.0,
                                    scalar2=12582912.0, op0=ALU.add,
                                    op1=ALU.subtract))
                            wq_in = (wint[:, 0:ncol] if cfg.get("rint", "cast") == "cast"
                                     else wint[:, 0:ncol].bitcast(F32))
                            ctx.gp(v)
                            ctx.v(v.tensor_scalar(out=wbf[:, 0:ncol],
                                                  in0=wq_in,
                                                  scalar1=-1.0, scalar2=1.0,
                                                  op0=ALU.max, op1=ALU.min))
                        else:
                            ctx.v(v.tensor_copy(out=wbf[:, 0:ncol],
                                                in_=wtmp[:, 0:ncol]))

                    s2 = ctx.snap()

                    @blk.tensor
                    def _(p):
                        ctx.wait(p, s2, "V", "A")
                        for c in range(ncol_ch):
                            ctx.p(p.transpose(out=tr_ps[:, 128 * c:128 * (c + 1)],
                                              in_=wbf[:, 128 * c:128 * (c + 1)],
                                              identity=per["ident"][:]))

                    s3 = ctx.snap()

                    @blk.scalar
                    def _(a):
                        ctx.wait(a, s3, "P", "W")
                        ctx.a(a.copy(out=wtr[:, 0:ncol], in_=tr_ps[:, 0:ncol]))

                    s4 = ctx.snap()

                    @blk.gpsimd
                    def _(g):
                        ctx.wait(g, s4, "A")
                        for c in range(ncol_ch):
                            ctx.wd(g.dma_start(
                                out=dst[128 * c:128 * (c + 1), 128 * r:128 * (r + 1)],
                                in_=wtr[:, 128 * c:128 * (c + 1)]))



def _ffn_sublayer(ctx, cfg, b, scr, per, dst):
    nc = ctx.nc
    S, D, HF = cfg["S"], cfg["D"], cfg["HF"]
    NT, DCH, HCH = cfg["NT"], cfg["DCH"], cfg["HCH"]
    QW, NQ = cfg["QW"], cfg["NQ"]
    SP = S + 2
    xres = scr["xres"]
    drow = scr["drow"]

    with scope() as es:
        t = lambda n, s, d: es.enter_context(nc.sbuf_tensor(ctx.name(n), s, d))
        xtile = t("f_x", [128, 2 * D], F32)
        xh = t("f_xh", [128, D], F32)
        xint = t("f_xrm", [128, D], I32)
        xqtm = t("f_xqtm", [128, D], BF16)
        st6 = t("f_st6", [128, 24], F32)
        st2 = t("f_st2", [128, 2], F32)
        tc = t("f_tc", [128, 12], F32)
        inv1col = t("f_inv1c", [128, NT], F32)
        inv2col = t("f_inv2c", [128, NT], F32)
        s2tok = t("f_s2tok", [128, NT], F32)
        xqT = t("f_xqT", [128, DCH * SP], BF16)
        w1all = t("f_w1all", [128, DCH * HF], BF16)
        w2s = t("f_w2s", [128, 2 * D], BF16)
        inv1bc = t("f_inv1bc", [128, SP], F32)
        s2bc = t("f_s2bc", [128, QW], F32)
        ubuf = t("f_ubuf", [128, 2 * (QW + 2)], F32)
        cbuf = t("f_cbuf", [128, 2 * QW], F32)
        hbuf = t("f_hbuf", [128, HCH * QW], F32)
        hmax = t("f_hmax", [128, QW], F32)
        hmin = t("f_hmin", [128, QW], F32)
        hqbuf = t("f_hq", [128, HCH * QW], BF16)
        qtmp = t("f_qtmp", [128, QW], F32)
        hint = t("f_rmq", [128, QW], I32)
        resv = t("f_resv", [128, D], F32)
        h_ps = es.enter_context(nc.psum_tensor(ctx.name("f_hps"), [128, 1024], F32))
        tr_ps = es.enter_context(nc.psum_tensor(ctx.name("f_trps"), [128, D], BF16))
        trf_ps = es.enter_context(nc.psum_tensor(ctx.name("f_trfps"), [128, 2 * 128], F32))
        y2_psl = [es.enter_context(nc.psum_tensor(ctx.name("f_y2ps"), [128, D], F32))
                  for _ in range(QW // 128)]

        ENTRY = ctx.snap()

        with nc.Block() as blk:
            @blk.sync
            def _(q):
                ctx.wait(q, ENTRY, "V", "A", "P", "G", "W")
                ctx.qd(q.dma_start(
                    out=w1all[:].rearrange("p (c h) -> p c h", c=DCH),
                    in_=scr["w1qT"].rearrange("(c p) h -> p c h", p=128)))

            @blk.vector
            def _(v):
                ctx.wait(v, ENTRY, "Q", "A", "P", "G", "W")
                xqT3 = xqT[:].rearrange("p (c s) -> p c s", s=SP)
                ctx.v(v.memset(xqT3[:, :, 0:1], 0.0))
                ctx.v(v.memset(xqT3[:, :, SP - 1:SP], 0.0))
                ctx.v(v.memset(inv1bc[:, 0:1], 0.0))
                ctx.v(v.memset(inv1bc[:, SP - 1:SP], 0.0))

        a_evac = None
        for j in range(NT):
            with nc.Block() as blk:
                s0 = ctx.snap()

                @blk.sync
                def _(q):
                    ctx.wait(q, s0, "V")
                    ctx.qd(q.dma_start(out=xtile[:, (j % 2) * D:(j % 2) * D + D],
                                       in_=xres[b, 128 * j:128 * (j + 1), :]))

                s1 = ctx.snap()
                xt = xtile[:, (j % 2) * D:(j % 2) * D + D]

                @blk.vector
                def _(v):
                    ctx.wait(v, s1, "Q")
                    for ii, (o, wd) in enumerate(sp512(D)):
                        ctx.v(v.bn_stats(out=st6[:, 6 * ii:6 * (ii + 1)],
                                         in_=xt[:, o:o + wd]))
                    ctx.gp(v)
                    ctx.v(v.bn_aggr(out=st2[:], in_=st6[:, 0:6 * len(sp512(D))]))

                s2 = ctx.snap()

                @blk.scalar
                def _(a):
                    ctx.wait(a, s2, "V")
                    ctx.a(a.activation(out=tc[:, 0:1], in_=st2[:, 1:2], func=AF.Ln,
                                       bias=per["epsc"][:, 0:1], scale=1.0))
                    ctx.ga(a)
                    ctx.a(a.activation(out=tc[:, 1:2], in_=tc[:, 0:1], func=AF.Exp,
                                       scale=-0.5))

                s3 = ctx.snap()

                @blk.vector
                def _(v):
                    ctx.wait(v, s3, "A")
                    ctx.v(v.tensor_scalar(out=xh[:], in0=xt,
                                          scalar1=st2[:, 0:1], scalar2=tc[:, 1:2],
                                          op0=ALU.subtract, op1=ALU.mult))
                    ctx.gp(v)
                    ctx.v(v.tensor_reduce(out=tc[:, 2:3], in_=xh[:], axis=AXX,
                                          op=ALU.max, apply_absolute_value=True))
                    ctx.gp(v)
                    ctx.v(v.tensor_scalar(out=tc[:, 3:4], in0=tc[:, 2:3],
                                          scalar1=1e-5, scalar2=None, op0=ALU.max))
                    ctx.gp(v)
                    ctx.v(v.reciprocal(out=tc[:, 4:5], in_=tc[:, 3:4]))
                    ctx.gp(v)
                    ctx.v(v.tensor_scalar(out=tc[:, 5:6], in0=tc[:, 4:5],
                                          scalar1=127.0, scalar2=None, op0=ALU.mult))
                    ctx.v(v.tensor_scalar(out=inv1col[:, j:j + 1], in0=tc[:, 3:4],
                                          scalar1=per["wscal"][:, 1:2], scalar2=None,
                                          op0=ALU.mult))
                    ctx.v(v.tensor_scalar(out=xh[:], in0=xh[:],
                                          scalar1=tc[:, 5:6], scalar2=None,
                                          op0=ALU.mult))
                    ctx.gp(v)
                    emit_rint(ctx, v, cfg, xqtm[:], xh[:], xint[:])

                s4 = ctx.snap()

                @blk.tensor
                def _(p):
                    ctx.wait(p, s4, "V")
                    if a_evac:
                        ctx.waitv(p, "A", a_evac)
                    for c in range(DCH):
                        ctx.p(p.transpose(out=tr_ps[:, 128 * c:128 * (c + 1)],
                                          in_=xqtm[:, 128 * c:128 * (c + 1)],
                                          identity=per["ident"][:]))

                s5 = ctx.snap()

                @blk.scalar
                def _(a):
                    ctx.wait(a, s5, "P")
                    o3 = xqT[:].rearrange("p (c s) -> p c s", s=SP)[
                        :, :, 1 + 128 * j:1 + 128 * (j + 1)]
                    i3 = tr_ps[:].rearrange("p (c w) -> p c w", w=128)
                    ctx.a(a.copy(out=o3, in_=i3))
                a_evac = ctx.cnt["A"]

        with nc.Block() as blk:
            s0 = ctx.snap()

            @blk.gpsimd
            def _(g):
                ctx.wait(g, s0, "V")
                ctx.wd(g.dma_start(out=drow[0, 0:S].rearrange("(t p) -> p t", p=128),
                                   in_=inv1col[:, 0:NT],
                                   allow_slow_non_contiguous=True))

            s1 = ctx.snap()

            @blk.sync
            def _(q):
                ctx.wait(q, s1, "W")
                ctx.qd(q.dma_start(out=inv1bc[:, 1:S + 1],
                                   in_=drow[0:1, 0:S].partition_broadcast(128)))

        for qw in range(NQ):
            base = qw * QW
            with nc.Block() as blk:
                s0 = ctx.snap()

                @blk.vector
                def _(v):
                    ctx.wait(v, s0, "Q", "P")
                    ctx.v(v.memset(hmax[:], -3.0e38))
                    ctx.v(v.memset(hmin[:], 3.0e38))

            with nc.Block() as blk:
                ent = ctx.snap()
                p0, v0, a0, q0 = ent["P"], ent["V"], ent["A"], ent["Q"]
                P_PER_M = DCH
                V_PER_M = 6
                A_PER_M = 1
                Q_PER_M = 16

                @blk.tensor
                def _(p):
                    for m in range(HCH):
                        if m == 0:
                            ctx.wait(p, ent, "V", "A")
                        if m >= 2:
                            ctx.waitv(p, "V", v0 + (m - 2) * V_PER_M + 1)
                        hp = h_ps[:, (m % 2) * 512:(m % 2) * 512 + QW + 2]
                        for c in range(DCH):
                            ctx.p(p.matmul(
                                out=hp,
                                lhsT=w1all[:, c * HF + 128 * m:c * HF + 128 * (m + 1)],
                                rhs=xqT[:, c * SP + base:c * SP + base + QW + 2],
                                start=(c == 0), stop=(c == DCH - 1)))

                @blk.vector
                def _(v):
                    for m in range(HCH):
                        ctx.waitv(v, "P", p0 + (m + 1) * P_PER_M)
                        if m >= 2:
                            ctx.waitv(v, "A", a0 + (m - 1) * A_PER_M)
                        hp = h_ps[:, (m % 2) * 512:(m % 2) * 512 + QW + 2]
                        ub = ubuf[:, (m % 2) * (QW + 2):(m % 2) * (QW + 2) + QW + 2]
                        cb2 = cbuf[:, (m % 2) * QW:(m % 2) * QW + QW]
                        ctx.v(v.tensor_tensor(out=ub, in0=hp,
                                              in1=inv1bc[:, base:base + QW + 2],
                                              op=ALU.mult))
                        ctx.gp(v)
                        ctx.v(v.tensor_scalar(out=cb2, in0=ub[:, 0:QW],
                                              scalar1=per["cwcol"][:, 3 * m:3 * m + 1],
                                              scalar2=None, op0=ALU.mult))
                        ctx.gp(v)
                        ctx.v(v.scalar_tensor_tensor(
                            out=cb2, in0=ub[:, 1:QW + 1],
                            scalar=per["cwcol"][:, 3 * m + 1:3 * m + 2],
                            in1=cb2, op0=ALU.mult, op1=ALU.add))
                        ctx.gp(v)
                        ctx.v(v.scalar_tensor_tensor(
                            out=cb2, in0=ub[:, 2:QW + 2],
                            scalar=per["cwcol"][:, 3 * m + 2:3 * m + 3],
                            in1=cb2, op0=ALU.mult, op1=ALU.add))
                        ctx.waitv(v, "A", a0 + (m + 1) * A_PER_M)
                        hb = hbuf[:, m * QW:(m + 1) * QW]
                        ctx.v(v.tensor_tensor(out=hmax[:], in0=hmax[:], in1=hb,
                                              op=ALU.max))
                        ctx.v(v.tensor_tensor(out=hmin[:], in0=hmin[:], in1=hb,
                                              op=ALU.min))

                @blk.scalar
                def _(a):
                    for m in range(HCH):
                        ctx.waitv(a, "V", v0 + m * V_PER_M + 4)
                        cb2 = cbuf[:, (m % 2) * QW:(m % 2) * QW + QW]
                        ctx.a(a.activation(out=hbuf[:, m * QW:(m + 1) * QW], in_=cb2,
                                           func=AF.Silu,
                                           bias=per["cbcol"][:, m:m + 1], scale=1.0))

            with nc.Block() as blk:
                s0 = ctx.snap()

                @blk.vector
                def _(v):
                    ctx.v(v.tensor_scalar(out=hmin[:], in0=hmin[:], scalar1=-1.0,
                                          scalar2=None, op0=ALU.mult))
                    ctx.gp(v)
                    ctx.v(v.tensor_tensor(out=hmax[:], in0=hmax[:], in1=hmin[:],
                                          op=ALU.max))

                s1 = ctx.snap()

                @blk.tensor
                def _(p):
                    ctx.wait(p, s1, "V")
                    for w in range(QW // 128):
                        ctx.p(p.transpose(out=trf_ps[:, 128 * w:128 * (w + 1)],
                                          in_=hmax[:, 128 * w:128 * (w + 1)],
                                          identity=per["identf"][:]))

                s2 = ctx.snap()

                @blk.vector
                def _(v):
                    ctx.wait(v, s2, "P")
                    nw = QW // 128
                    for w in range(nw):
                        jt = (base // 128) + w
                        ctx.v(v.tensor_reduce(out=s2tok[:, jt:jt + 1],
                                              in_=trf_ps[:, 128 * w:128 * (w + 1)],
                                              axis=AXX, op=ALU.max))
                        ctx.gp(v)
                        ctx.v(v.tensor_scalar(out=s2tok[:, jt:jt + 1],
                                              in0=s2tok[:, jt:jt + 1],
                                              scalar1=1e-5, scalar2=None,
                                              op0=ALU.max))
                        ctx.gp(v)
                        ctx.v(v.tensor_scalar(out=inv2col[:, jt:jt + 1],
                                              in0=s2tok[:, jt:jt + 1],
                                              scalar1=per["wscal"][:, 3:4],
                                              scalar2=0.5, op0=ALU.mult,
                                              op1=ALU.mult))
                        ctx.v(v.reciprocal(out=tc[:, 6:7], in_=s2tok[:, jt:jt + 1]))
                        ctx.gp(v)
                        ctx.v(v.tensor_scalar(out=s2tok[:, jt:jt + 1], in0=tc[:, 6:7],
                                              scalar1=127.0, scalar2=None,
                                              op0=ALU.mult))

                s3 = ctx.snap()

                @blk.gpsimd
                def _(g):
                    ctx.wait(g, s3, "V")
                    nw = QW // 128
                    jt = base // 128
                    ctx.wd(g.dma_start(
                        out=drow[1, base:base + QW].rearrange("(t p) -> p t", p=128),
                        in_=s2tok[:, jt:jt + nw],
                        allow_slow_non_contiguous=True))

                s4 = ctx.snap()

                @blk.sync
                def _(q):
                    ctx.wait(q, s4, "W")
                    ctx.qd(q.dma_start(
                        out=s2bc[:],
                        in_=drow[1:2, base:base + QW].partition_broadcast(128)))

            with nc.Block() as blk:
                s0 = ctx.snap()
                vq0 = ctx.cnt["V"]

                @blk.vector
                def _(v):
                    ctx.wait(v, s0, "Q")
                    for m in range(HCH):
                        hb = hbuf[:, m * QW:(m + 1) * QW]
                        ctx.v(v.tensor_tensor(out=qtmp[:], in0=hb, in1=s2bc[:],
                                              op=ALU.mult))
                        ctx.gp(v)
                        emit_rint(ctx, v, cfg, hqbuf[:, m * QW:(m + 1) * QW],
                                  qtmp[:], hint[:])

            NW = QW // 128
            with nc.Block() as blk:
                ent2 = ctx.snap()
                q2, p2 = ent2["Q"], ent2["P"]
                P_PER_M = NW * len(sp512(D))

                @blk.sync
                def _(q):
                    for m in range(HCH):
                        if m >= 2:
                            ctx.waitv(q, "P", p2 + (m - 1) * P_PER_M)
                        ctx.qd(q.dma_start(
                            out=w2s[:, (m % 2) * D:(m % 2) * D + D],
                            in_=scr["w2qT"][128 * m:128 * (m + 1), :]))

                V_QPM = 3 if cfg.get("rint", "cast") == "cast" else 2

                @blk.tensor
                def _(p):
                    for m in range(HCH):
                        ctx.waitv(p, "V", vq0 + (m + 1) * V_QPM)
                        ctx.waitv(p, "Q", q2 + 16 * (m + 1))
                        for w in range(NW):
                            for (o, wd) in sp512(D):
                                ctx.p(p.matmul(
                                    out=y2_psl[w][:, o:o + wd],
                                    lhsT=hqbuf[:, m * QW + 128 * w:m * QW + 128 * (w + 1)],
                                    rhs=w2s[:, (m % 2) * D + o:(m % 2) * D + o + wd],
                                    start=(m == 0), stop=(m == HCH - 1)))

            for w in range(NW):
                jt = (base // 128) + w
                with nc.Block() as blk:
                    s0 = ctx.snap()

                    @blk.sync
                    def _(q):
                        ctx.wait(q, s0, "V")
                        ctx.qd(q.dma_start(out=xtile[:, 0:D],
                                           in_=xres[b, 128 * jt:128 * (jt + 1), :]))

                    s2 = ctx.snap()

                    @blk.vector
                    def _(v):
                        ctx.wait(v, s2, "P", "Q")
                        ctx.v(v.scalar_tensor_tensor(
                            out=resv[:], in0=y2_psl[w][:, 0:D],
                            scalar=inv2col[:, jt:jt + 1], in1=xtile[:, 0:D],
                            op0=ALU.mult, op1=ALU.add))

                    s3 = ctx.snap()

                    @blk.sync
                    def _(q):
                        ctx.wait(q, s3, "V")
                        ctx.qd(q.dma_start(out=dst[b, 128 * jt:128 * (jt + 1), :],
                                           in_=resv[:]))



def _attn_sublayer(ctx, cfg, b, scr, per):
    nc = ctx.nc
    S, D = cfg["S"], cfg["D"]
    NT, DCH, QCH = cfg["NT"], cfg["DCH"], cfg["QCH"]
    xres = scr["xres"]

    with scope() as es_q:
        qkvT = es_q.enter_context(nc.sbuf_tensor(ctx.name("a_qkvT"), [128, QCH * S], BF16))

        with scope() as es:
            t = lambda n, s, d: es.enter_context(nc.sbuf_tensor(ctx.name(n), s, d))
            xtile = t("a_x", [128, 2 * D], F32)
            xh = t("a_xh", [128, D], BF16)
            st6 = t("a_st6", [128, 24], F32)
            st2 = t("a_st2", [128, 2], F32)
            tc = t("a_tc", [128, 4], F32)
            xhT = t("a_xhT", [128, DCH * S], BF16)
            wq_all = t("a_wq", [128, DCH * 3 * D], BF16)
            tr_ps = es.enter_context(nc.psum_tensor(ctx.name("a_trps"), [128, D], BF16))
            qkv_ps = es.enter_context(nc.psum_tensor(ctx.name("a_qkvps"), [128, S], F32))

            ENTRY = ctx.snap()
            with nc.Block() as blk:
                @blk.sync
                def _(q):
                    ctx.wait(q, ENTRY, "V", "A", "P", "G", "W")
                    ctx.qd(q.dma_start(
                        out=wq_all[:].rearrange("p (c h) -> p c h", c=DCH),
                        in_=scr["wqkvT"].rearrange("(c p) h -> p c h", p=128)))

            a_evac = None
            for j in range(NT):
                with nc.Block() as blk:
                    s0 = ctx.snap()

                    @blk.sync
                    def _(q):
                        ctx.wait(q, s0, "V")
                        ctx.qd(q.dma_start(out=xtile[:, (j % 2) * D:(j % 2) * D + D],
                                           in_=xres[b, 128 * j:128 * (j + 1), :]))

                    s1 = ctx.snap()
                    xt = xtile[:, (j % 2) * D:(j % 2) * D + D]

                    @blk.vector
                    def _(v):
                        ctx.wait(v, s1, "Q")
                        for ii, (o, wd) in enumerate(sp512(D)):
                            ctx.v(v.bn_stats(out=st6[:, 6 * ii:6 * (ii + 1)],
                                             in_=xt[:, o:o + wd]))
                        ctx.gp(v)
                        ctx.v(v.bn_aggr(out=st2[:], in_=st6[:, 0:6 * len(sp512(D))]))

                    s2 = ctx.snap()

                    @blk.scalar
                    def _(a):
                        ctx.wait(a, s2, "V")
                        ctx.a(a.activation(out=tc[:, 0:1], in_=st2[:, 1:2],
                                           func=AF.Ln, bias=per["epsc"][:, 0:1],
                                           scale=1.0))
                        ctx.ga(a)
                        ctx.a(a.activation(out=tc[:, 1:2], in_=tc[:, 0:1],
                                           func=AF.Exp, scale=-0.5))

                    s3 = ctx.snap()

                    @blk.vector
                    def _(v):
                        ctx.wait(v, s3, "A")
                        ctx.v(v.tensor_scalar(out=xh[:], in0=xt,
                                              scalar1=st2[:, 0:1], scalar2=tc[:, 1:2],
                                              op0=ALU.subtract, op1=ALU.mult))

                    s4 = ctx.snap()

                    @blk.tensor
                    def _(p):
                        ctx.wait(p, s4, "V")
                        if a_evac:
                            ctx.waitv(p, "A", a_evac)
                        for c in range(DCH):
                            ctx.p(p.transpose(out=tr_ps[:, 128 * c:128 * (c + 1)],
                                              in_=xh[:, 128 * c:128 * (c + 1)],
                                              identity=per["ident"][:]))

                    s5 = ctx.snap()

                    @blk.scalar
                    def _(a):
                        ctx.wait(a, s5, "P")
                        o3 = xhT[:].rearrange("p (c s) -> p c s", s=S)[
                            :, :, 128 * j:128 * (j + 1)]
                        i3 = tr_ps[:].rearrange("p (c w) -> p c w", w=128)
                        ctx.a(a.copy(out=o3, in_=i3))
                    a_evac = ctx.cnt["A"]

            for r in range(QCH):
                with nc.Block() as blk:
                    s0 = ctx.snap()

                    @blk.tensor
                    def _(p):
                        ctx.wait(p, s0, "Q", "A")
                        for c in range(DCH):
                            for (o, wd) in sp512(S):
                                ctx.p(p.matmul(
                                    out=qkv_ps[:, o:o + wd],
                                    lhsT=wq_all[:, c * 3 * D + 128 * r:c * 3 * D + 128 * (r + 1)],
                                    rhs=xhT[:, c * S + o:c * S + o + wd],
                                    start=(c == 0), stop=(c == DCH - 1)))

                    s1 = ctx.snap()

                    @blk.scalar
                    def _(a):
                        ctx.wait(a, s1, "P")
                        ctx.a(a.activation(out=qkvT[:, r * S:(r + 1) * S],
                                           in_=qkv_ps[:, 0:S], func=AF.Identity,
                                           bias=per["bqkvcol"][:, r:r + 1],
                                           scale=1.0))

        with scope() as es:
            _attn_heads(ctx, cfg, b, qkvT, scr, per, es)

    with scope() as es:
        t = lambda n, s, d: es.enter_context(nc.sbuf_tensor(ctx.name(n), s, d))
        wo_all = t("o_wo", [128, DCH * D], BF16)
        ao_t = t("o_ao", [128, 2 * D], BF16)
        xtile = t("o_x", [128, D], F32)
        resv = t("o_res", [128, D], F32)
        wo_ps = es.enter_context(nc.psum_tensor(ctx.name("o_wops"), [128, D], F32))

        with nc.Block() as blk:
            s0 = ctx.snap()

            @blk.sync
            def _(q):
                ctx.wait(q, s0, "V", "A", "P", "W")
                ctx.qd(q.dma_start(
                    out=wo_all[:].rearrange("p (c h) -> p c h", c=DCH),
                    in_=scr["woT"].rearrange("(c p) h -> p c h", p=128)))

        v_res = None
        for j in range(NT):
            with nc.Block() as blk:
                s0 = ctx.snap()

                @blk.sync
                def _(q):
                    if j > 1:
                        ctx.waitv(q, "P", s0["P"])
                    ctx.qd(q.dma_start(
                        out=ao_t[:, (j % 2) * D:(j % 2) * D + D].rearrange(
                            "p (c w) -> p c w", w=128),
                        in_=scr["aoT"].rearrange("(c p) s -> p c s", p=128)[
                            :, :, 128 * j:128 * (j + 1)]))
                    ctx.waitv(q, "V", s0["V"])
                    ctx.qd(q.dma_start(out=xtile[:],
                                       in_=xres[b, 128 * j:128 * (j + 1), :]))

                s1 = ctx.snap()

                @blk.tensor
                def _(p):
                    ctx.wait(p, s1, "Q")
                    if v_res:
                        ctx.waitv(p, "V", v_res)
                    for c in range(DCH):
                        for (o, wd) in sp512(D):
                            ctx.p(p.matmul(
                                out=wo_ps[:, o:o + wd],
                                lhsT=ao_t[:, (j % 2) * D + 128 * c:(j % 2) * D + 128 * (c + 1)],
                                rhs=wo_all[:, c * D + o:c * D + o + wd],
                                start=(c == 0), stop=(c == DCH - 1)))

                s2 = ctx.snap()

                @blk.vector
                def _(v):
                    ctx.wait(v, s2, "P", "Q")
                    ctx.v(v.scalar_tensor_tensor(out=resv[:], in0=wo_ps[:],
                                                 scalar=0.0, in1=per["bobc"][:],
                                                 op0=ALU.add, op1=ALU.add))
                    ctx.gp(v)
                    ctx.v(v.tensor_tensor(out=resv[:], in0=resv[:], in1=xtile[:],
                                          op=ALU.add))
                v_res = ctx.cnt["V"]

                s3 = ctx.snap()

                @blk.sync
                def _(q):
                    ctx.wait(q, s3, "V")
                    ctx.qd(q.dma_start(out=xres[b, 128 * j:128 * (j + 1), :],
                                       in_=resv[:]))


def _attn_heads(ctx, cfg, b, qkvT, scr, per, es):
    nc = ctx.nc
    S, D, NH, DK = cfg["S"], cfg["D"], cfg["NH"], cfg["DK"]
    NT, KCH, GQ, NG = cfg["NT"], cfg["KCH"], cfg["GQ"], cfg["NG"]
    GW = GQ * 128
    TOPK = cfg["TOPK"]
    NPR = float(S)
    DCH = D // 128
    drow, aoT = scr["drow"], scr["aoT"]

    (RM, NM, BET, LT, LTP, SLP, CA, CB, ZC, ZP, TST, RHO, TA, TB, TC, TD,
     TBS, CBS, TBG) = range(19)
    NR = 7
    cslots = list(range(19, 19 + NR))
    tslots = list(range(19 + NR, 19 + 2 * NR))
    cslotsB = list(range(19 + 2 * NR, 19 + 3 * NR))
    NSLOT = 19 + 3 * NR
    TGT = float(TOPK) - 0.5
    SSZ = S // 32
    islot = min(7, max(0, int(round(TOPK / 32.0)) - 1))
    aslot = max(0, islot - 2)
    bslot = min(7, islot + 2)
    def _zof(c):
        return 2.0 * (c - TGT) / (c + TGT)
    sden = _zof((bslot + 1) * 32.0) - _zof((aslot + 1) * 32.0)
    ZF = 4.0 / float(TOPK)

    t = lambda n, s, d: es.enter_context(nc.sbuf_tensor(ctx.name(n), s, d))
    vtm2 = t("h_vtm", [128, 2 * KCH * (DK + 1)], BF16)
    ebuf2 = t("h_e", [128, 2 * GQ * S], BF16)
    gbuf = t("h_g", [128, GQ * S], BF16)
    junk = t("h_junk", [128, S], BF16)
    scrt = t("h_scrt", [128, S], BF16)
    gTbuf = t("h_gT", [128, KCH * GW], BF16)
    candb = t("h_cand", [128, GQ * 16], F32)
    ohb = t("h_ohb", [128, GQ * 16], F32)
    kscr = t("h_kscr", [128, 8], BF16)
    smp82 = t("h_smp8", [128, 2 * 8 * GQ], BF16)
    gcol2 = t("h_gcol", [128, 2 * 40 * GQ], F32)
    invsc = t("h_invsc", [DK + 1, GW], F32)
    invbc = t("h_invbc", [64, GW], F32)
    aot_t = t("h_aot", [64, GW], BF16)
    sc_ps = es.enter_context(nc.psum_tensor(ctx.name("h_scps"), [128, S], F32))
    tr_ps2 = es.enter_context(nc.psum_tensor(ctx.name("h_trps2"), [128, S], BF16))
    pv_ps = es.enter_context(nc.psum_tensor(ctx.name("h_pvps"), [DK + 1, GW], F32))

    blk_cm = nc.Block()
    blk = blk_cm.__enter__()

    groups = [(h, gi) for h in range(NH) for gi in range(NG)]
    NGR = len(groups)

    def P(k):
        return (k % 2)

    def eb(k, tt):
        return ebuf2[:, (P(k) * GQ + tt) * S:(P(k) * GQ + tt + 1) * S]

    def gc(k, i):
        o = P(k) * 40 * GQ
        return gcol2[:, o + i * GQ:o + (i + 1) * GQ]

    def gc1(k, i, tt):
        o = P(k) * 40 * GQ
        return gcol2[:, o + i * GQ + tt:o + i * GQ + tt + 1]

    def vtm(k):
        h = groups[k][0]
        o = (h % 2) * KCH * (DK + 1)
        return vtm2[:, o:o + KCH * (DK + 1)]

    def smp8(k):
        return smp82[:, P(k) * 8 * GQ:(P(k) + 1) * 8 * GQ]

    def qkslice(k):
        h, gi = groups[k]
        ch_q = h // 2
        off = 64 * (h % 2)
        qT = qkvT[off:off + 64, ch_q * S:(ch_q + 1) * S]
        kT = qkvT[off:off + 64, (DCH + ch_q) * S:(DCH + ch_q + 1) * S]
        vT = qkvT[off:off + 64, (2 * DCH + ch_q) * S:(2 * DCH + ch_q + 1) * S]
        return qT, kT, vT

    s0 = ctx.snap()

    @blk.vector
    def _(v):
        ctx.wait(v, s0, "A", "P")
        ctx.v(v.memset(vtm2[:].rearrange("p (c w) -> p c w", w=DK + 1)[
            :, :, DK:DK + 1], 1.0))

    def head_prep(k):
        h, _ = groups[k]
        off = 64 * (h % 2)
        _, _, vT = qkslice(k)
        vt = vtm(k)
        sp0 = ctx.snap()

        @blk.tensor
        def _(p):
            ctx.wait(p, sp0, "A", "V")
            for c in range(KCH):
                ctx.p(p.transpose(out=tr_ps2[:, DK * c:DK * (c + 1)],
                                  in_=vT[:, 128 * c:128 * (c + 1)],
                                  identity=per["ident2"][off:off + 64, :]))

        sp1 = ctx.snap()

        @blk.scalar
        def _(a):
            ctx.wait(a, sp1, "P")
            ctx.a(a.copy(
                out=vt.rearrange("p (c w) -> p c w", w=DK + 1)[:, :, 0:DK],
                in_=tr_ps2[:, 0:DK * KCH].rearrange("p (c w) -> p c w",
                                                    w=DK)))

    P_QK = len(sp512(S))
    pstate = {"exp_a": 0}

    def s1_qk(k, tt):
        h, gi = groups[k]
        qT, kT, _ = qkslice(k)
        jt = gi * GQ + tt
        @blk.tensor
        def _(p):
            ctx.waitv(p, "A", pstate["exp_a"])
            for (o, wd) in sp512(S):
                ctx.p(p.matmul(out=sc_ps[:, o:o + wd],
                               lhsT=qT[:, 128 * jt:128 * (jt + 1)],
                               rhs=kT[:, o:o + wd],
                               start=True, stop=True))

    def s1_rx(k, tt):
        st1 = ctx.snap()

        @blk.vector
        def _(v):
            ctx.wait(v, st1, "P")
            ctx.v(v.tensor_reduce(out=gc1(k, RM, tt), in_=sc_ps[:, 0:S],
                                  axis=AXX, op=ALU.max))
            ctx.gp(v)
            ctx.v(v.tensor_scalar(out=gc1(k, NM, tt), in0=gc1(k, RM, tt),
                                  scalar1=-0.125, scalar2=None,
                                  op0=ALU.mult))

    def s1_exp(k, tt):
        st2 = ctx.snap()

        @blk.scalar
        def _(a):
            ctx.wait(a, st2, "V")
            ctx.a(a.activation(out=eb(k, tt), in_=sc_ps[:, 0:S],
                               func=AF.Exp, bias=gc1(k, NM, tt),
                               scale=0.125))
        pstate["exp_a"] = ctx.cnt["A"]

    def s1_smp(k):
        sm0 = ctx.snap()

        @blk.scalar
        def _(a):
            ctx.wait(a, sm0, "V")
            ctx.a(a.activation(out=gc(k, BET), in_=gc(k, NM),
                               func=AF.Exp, scale=1.0))

        sm1 = ctx.snap()

        @blk.vector
        def _(v):
            ctx.wait(v, sm1, "A")
            for tt in range(GQ):
                ctx.v(v.tensor_copy(
                    out=scrt[:, SSZ * tt:SSZ * (tt + 1)].unsqueeze(2),
                    in_=eb(k, tt).rearrange("p (a b) -> p a b",
                                            b=32)[:, :, 0:1]))
            for tt in range(GQ):
                ctx.v(v.max(out=smp8(k)[:, 8 * tt:8 * (tt + 1)],
                            in_=scrt[:, SSZ * tt:SSZ * (tt + 1)]))
            l3 = smp8(k)[:].rearrange("p (t k) -> p t k", k=8)
            ctx.gp(v)
            ctx.v(v.tensor_copy(out=gc(k, LT).unsqueeze(2),
                                in_=l3[:, :, islot:islot + 1]))
            ctx.v(v.tensor_tensor(out=gc(k, SLP).unsqueeze(2),
                                  in0=l3[:, :, aslot:aslot + 1],
                                  in1=l3[:, :, bslot:bslot + 1],
                                  op=ALU.subtract))
            ctx.v(v.memset(gc(k, CA), 0.0))
            ctx.v(v.memset(gc(k, CB), 1.01))
            ctx.v(v.memset(gc(k, ZC), 0.0))
            ctx.v(v.memset(gc(k, LTP), 0.0))
            ctx.v(v.tensor_scalar(out=gc(k, SLP), in0=gc(k, SLP),
                                  scalar1=1.0 / sden, scalar2=1e-37,
                                  op0=ALU.mult, op1=ALU.max))
            ctx.gp(v)
            ctx.v(v.tensor_copy(out=gc(k, tslots[0]), in_=gc(k, LT)))
            ctx.v(v.tensor_scalar(out=gc(k, TBG), in0=gc(k, LT),
                                  scalar1=65536.0, scalar2=None,
                                  op0=ALU.mult))

    def s2_A(k, rnd):
        sT = ctx.snap()

        @blk.scalar
        def _(a):
            ctx.wait(a, sT, "V")
            for tt in range(GQ):
                ctx.a(a.activation(
                    out=junk[:], in_=eb(k, tt),
                    func=AF.Tanh,
                    bias=gc1(k, TBG, tt), scale=-65536.0,
                    accum_out=gc1(k, cslots[rnd], tt)))

    def s2_probeV(k, rnd, half=False):
        sP = ctx.snap()

        @blk.vector
        def _(v):
            if not half:
                ctx.wait(v, sP, "A")
            for tt in range(GQ):
                if half:
                    ctx.v(v.tensor_scalar(
                        out=scrt[:, 0:S // 2], in0=eb(k, tt)[:, S // 2:S],
                        scalar1=gc1(k, tslots[rnd], tt), scalar2=None,
                        op0=ALU.is_ge, op1=ALU.add,
                        accum_out=gc1(k, cslotsB[rnd], tt)))
                else:
                    ctx.v(v.tensor_scalar(
                        out=junk[:], in0=eb(k, tt),
                        scalar1=gc1(k, tslots[rnd], tt), scalar2=None,
                        op0=ALU.is_ge, op1=ALU.add,
                        accum_out=gc1(k, cslots[rnd], tt)))

    def s2_chainz(k, rnd):
        sgn = rnd < NR - 2
        sC = ctx.snap()

        @blk.vector
        def _(v):
            if sgn:
                ctx.wait(v, sC, "A")
            cs = cslots[rnd]
            ctx.gp(v)
            if sgn:
                ctx.v(v.tensor_scalar(out=gc(k, TA), in0=gc(k, cs),
                                      scalar1=-0.5, scalar2=NPR / 2.0,
                                      op0=ALU.mult, op1=ALU.add))
                ctx.v(v.tensor_copy(out=gc(k, ZP), in_=gc(k, ZC)))
            else:
                ctx.v(v.tensor_copy(out=gc(k, TA), in_=gc(k, cs)))
                ctx.v(v.tensor_copy(out=gc(k, ZP), in_=gc(k, ZC)))
            ctx.gp(v)
            ctx.v(v.tensor_scalar(out=gc(k, TB), in0=gc(k, TA),
                                  scalar1=TGT, scalar2=None,
                                  op0=ALU.is_ge))
            ctx.v(v.tensor_scalar(out=gc(k, TC), in0=gc(k, TA),
                                  scalar1=2.0, scalar2=-2.0 * TGT,
                                  op0=ALU.mult, op1=ALU.add))
            ctx.v(v.tensor_scalar(out=gc(k, TD), in0=gc(k, TA),
                                  scalar1=TGT, scalar2=None,
                                  op0=ALU.add))
            ctx.gp(v)
            ctx.v(v.reciprocal(out=gc(k, TD), in_=gc(k, TD)))
            ctx.v(v.tensor_scalar(out=gc(k, TA), in0=gc(k, TB),
                                  scalar1=1000.0, scalar2=-1000.0,
                                  op0=ALU.mult, op1=ALU.add))
            ctx.gp(v)
            ctx.v(v.tensor_tensor(out=gc(k, ZC), in0=gc(k, TC),
                                  in1=gc(k, TD), op=ALU.mult))
            ctx.v(v.tensor_scalar(out=gc(k, TD), in0=gc(k, TB),
                                  scalar1=1000.0, scalar2=None,
                                  op0=ALU.mult))
            ctx.v(v.tensor_tensor(out=gc(k, TA), in0=gc(k, TA),
                                  in1=gc(k, LT), op=ALU.add))
            ctx.v(v.tensor_tensor(out=gc(k, TD), in0=gc(k, TD),
                                  in1=gc(k, LT), op=ALU.add))
            ctx.v(v.tensor_tensor(out=gc(k, CA), in0=gc(k, CA),
                                  in1=gc(k, TA), op=ALU.max))
            ctx.v(v.tensor_tensor(out=gc(k, CB), in0=gc(k, CB),
                                  in1=gc(k, TD), op=ALU.min))
            if rnd > 0:
                ctx.v(v.tensor_tensor(out=gc(k, TC), in0=gc(k, LT),
                                      in1=gc(k, LTP), op=ALU.subtract))
                ctx.v(v.tensor_tensor(out=gc(k, TD), in0=gc(k, ZC),
                                      in1=gc(k, ZP), op=ALU.subtract))
                ctx.v(v.tensor_scalar(out=gc(k, TB), in0=gc(k, TC),
                                      scalar1=-1.0, scalar2=None,
                                      op0=ALU.mult))
                ctx.v(v.tensor_scalar(out=gc(k, TA), in0=gc(k, TD),
                                      scalar1=-1.0, scalar2=None,
                                      op0=ALU.mult))
                ctx.v(v.tensor_tensor(out=gc(k, TC), in0=gc(k, TC),
                                      in1=gc(k, TB), op=ALU.max))
                ctx.v(v.tensor_tensor(out=gc(k, TD), in0=gc(k, TD),
                                      in1=gc(k, TA), op=ALU.max))
                ctx.gp(v)
                ctx.v(v.tensor_scalar(out=gc(k, TD), in0=gc(k, TD),
                                      scalar1=ZF, scalar2=None,
                                      op0=ALU.max))
                ctx.gp(v)
                ctx.v(v.reciprocal(out=gc(k, TD), in_=gc(k, TD)))
                ctx.gp(v)
                ctx.v(v.tensor_tensor(out=gc(k, TC), in0=gc(k, TC),
                                      in1=gc(k, TD), op=ALU.mult))
                ctx.v(v.tensor_scalar(out=gc(k, TD), in0=gc(k, SLP),
                                      scalar1=0.3, scalar2=None,
                                      op0=ALU.mult))
                ctx.v(v.tensor_scalar(out=gc(k, TB), in0=gc(k, SLP),
                                      scalar1=4.0, scalar2=None,
                                      op0=ALU.mult))
                ctx.v(v.tensor_tensor(out=gc(k, TC), in0=gc(k, TC),
                                      in1=gc(k, TD), op=ALU.max))
                ctx.gp(v)
                ctx.v(v.tensor_tensor(out=gc(k, SLP), in0=gc(k, TC),
                                      in1=gc(k, TB), op=ALU.min))
            ctx.gp(v)
            ctx.v(v.tensor_tensor(out=gc(k, TC), in0=gc(k, ZC),
                                  in1=gc(k, SLP), op=ALU.mult))
            caprel = -0.5 if rnd == 0 else -0.95
            ctx.v(v.tensor_scalar(out=gc(k, TD), in0=gc(k, LT),
                                  scalar1=caprel, scalar2=None,
                                  op0=ALU.mult))
            ctx.gp(v)
            ctx.v(v.tensor_tensor(out=gc(k, TC), in0=gc(k, TC),
                                  in1=gc(k, TD), op=ALU.max))
            ctx.v(v.tensor_copy(out=gc(k, LTP), in_=gc(k, LT)))
            ctx.gp(v)
            ctx.v(v.tensor_tensor(out=gc(k, TD), in0=gc(k, LT),
                                  in1=gc(k, TC), op=ALU.add))
            ctx.gp(v)
            ctx.v(v.tensor_tensor(out=gc(k, TB), in0=gc(k, TD),
                                  in1=gc(k, CA), op=ALU.is_ge))
            ctx.v(v.tensor_tensor(out=gc(k, TC), in0=gc(k, TD),
                                  in1=gc(k, CB), op=ALU.is_le))
            ctx.gp(v)
            ctx.v(v.tensor_tensor(out=gc(k, TB), in0=gc(k, TB),
                                  in1=gc(k, TC), op=ALU.mult))
            ctx.v(v.tensor_tensor(out=gc(k, TC), in0=gc(k, CA),
                                  in1=gc(k, CB), op=ALU.add))
            ctx.gp(v)
            ctx.v(v.tensor_scalar(out=gc(k, TC), in0=gc(k, TC),
                                  scalar1=0.5, scalar2=None,
                                  op0=ALU.mult))
            ctx.gp(v)
            ctx.v(v.tensor_tensor(out=gc(k, TD), in0=gc(k, TD),
                                  in1=gc(k, TC), op=ALU.subtract))
            ctx.gp(v)
            ctx.v(v.tensor_tensor(out=gc(k, TD), in0=gc(k, TD),
                                  in1=gc(k, TB), op=ALU.mult))
            ctx.gp(v)
            ctx.v(v.tensor_tensor(out=gc(k, LT), in0=gc(k, TC),
                                  in1=gc(k, TD), op=ALU.add))
            ctx.gp(v)
            ctx.v(v.tensor_copy(out=gc(k, tslots[rnd + 1]), in_=gc(k, LT)))
            ctx.v(v.tensor_scalar(out=gc(k, TBG), in0=gc(k, LT),
                                  scalar1=65536.0, scalar2=None,
                                  op0=ALU.mult))

    def s3(k):
        sX = ctx.snap()

        @blk.vector
        def _(v):
            ctx.wait(v, sX, "A", "P")
            ctx.gp(v)
            ctx.v(v.tensor_scalar(out=gc(k, TC), in0=gc(k, cslots[-2]),
                                  scalar1=-(float(TOPK) - 0.5),
                                  scalar2=None, op0=ALU.add))
            ctx.v(v.tensor_scalar(out=gc(k, TD), in0=gc(k, cslots[-1]),
                                  scalar1=-(float(TOPK) - 0.5),
                                  scalar2=None, op0=ALU.add))
            ctx.v(v.tensor_scalar(out=gc(k, TA), in0=gc(k, TC),
                                  scalar1=-1.0, scalar2=None,
                                  op0=ALU.mult))
            ctx.v(v.tensor_scalar(out=gc(k, TB), in0=gc(k, TD),
                                  scalar1=-1.0, scalar2=None,
                                  op0=ALU.mult))
            ctx.v(v.tensor_tensor(out=gc(k, TC), in0=gc(k, TC),
                                  in1=gc(k, TA), op=ALU.max))
            ctx.v(v.tensor_tensor(out=gc(k, TD), in0=gc(k, TD),
                                  in1=gc(k, TB), op=ALU.max))
            ctx.gp(v)
            ctx.v(v.tensor_tensor(out=gc(k, TA), in0=gc(k, TC),
                                  in1=gc(k, TD), op=ALU.is_ge))
            ctx.v(v.tensor_tensor(out=gc(k, TB), in0=gc(k, tslots[-1]),
                                  in1=gc(k, tslots[-2]), op=ALU.subtract))
            ctx.v(v.tensor_tensor(out=gc(k, TC), in0=gc(k, cslots[-1]),
                                  in1=gc(k, cslots[-2]), op=ALU.subtract))
            ctx.v(v.tensor_tensor(out=gc(k, TB), in0=gc(k, TB),
                                  in1=gc(k, TA), op=ALU.mult))
            ctx.v(v.tensor_tensor(out=gc(k, TC), in0=gc(k, TC),
                                  in1=gc(k, TA), op=ALU.mult))
            ctx.v(v.tensor_tensor(out=gc(k, TBS), in0=gc(k, tslots[-2]),
                                  in1=gc(k, TB), op=ALU.add))
            ctx.v(v.tensor_tensor(out=gc(k, CBS), in0=gc(k, cslots[-2]),
                                  in1=gc(k, TC), op=ALU.add))
            for tt in range(GQ):
                e_t = eb(k, tt)
                sel_t = gbuf[:, tt * S:(tt + 1) * S]
                ctx.gp(v)
                ctx.v(v.tensor_scalar(out=scrt[:], in0=e_t,
                                      scalar1=gc1(k, TBS, tt),
                                      scalar2=-4.0,
                                      op0=ALU.is_lt, op1=ALU.mult))
                ctx.v(v.tensor_scalar(out=junk[:], in0=e_t, scalar1=-1.0,
                                      scalar2=None, op0=ALU.mult))
                ctx.gp(v)
                ctx.v(v.tensor_tensor(out=sel_t, in0=scrt[:],
                                      in1=junk[:], op=ALU.add))
                ctx.gp(v)
                ctx.v(v.max(out=kscr[:], in_=sel_t))
                ctx.gp(v)
                ctx.v(v.tensor_scalar(out=scrt[:], in0=e_t,
                                      scalar1=gc1(k, TBS, tt),
                                      scalar2=-4.0,
                                      op0=ALU.is_ge, op1=ALU.mult))
                ctx.v(v.tensor_scalar(out=candb[:, tt * 16:tt * 16 + 8],
                                      in0=kscr[:], scalar1=-1.0,
                                      scalar2=None, op0=ALU.mult))
                ctx.v(v.tensor_tensor(out=sel_t, in0=scrt[:], in1=e_t,
                                      op=ALU.add))
                ctx.gp(v)
                ctx.v(v.max(out=kscr[:], in_=sel_t))
                ctx.gp(v)
                ctx.v(v.tensor_scalar(
                    out=candb[:, tt * 16 + 8:tt * 16 + 16],
                    in0=kscr[:], scalar1=1.0, scalar2=None, op0=ALU.mult))
            ctx.gp(v)
            ctx.v(v.tensor_scalar(out=gc(k, RHO), in0=gc(k, CBS),
                                  scalar1=-float(TOPK - 1), scalar2=None,
                                  op0=ALU.add))
            ctx.gp(v)
            ctx.v(v.tensor_scalar(out=gc(k, RHO), in0=gc(k, RHO),
                                  scalar1=-7.0, scalar2=8.0,
                                  op0=ALU.max, op1=ALU.min))
            ctx.gp(v)
            iview = per["iota16"][:].unsqueeze(1).to_broadcast(
                [128, GQ, 16])
            rview = gc(k, RHO).unsqueeze(2).to_broadcast([128, GQ, 16])
            ctx.v(v.tensor_tensor(
                out=ohb[:].rearrange("p (t k) -> p t k", k=16),
                in0=iview, in1=rview, op=ALU.is_equal))
            ctx.gp(v)
            ctx.v(v.tensor_tensor(out=ohb[:], in0=ohb[:], in1=candb[:],
                                  op=ALU.mult))
            ctx.gp(v)
            ctx.v(v.tensor_reduce(
                out=gc(k, TST).unsqueeze(2),
                in_=ohb[:].rearrange("p (t k) -> p t k", k=16),
                axis=AXX, op=ALU.add))
            ctx.v(v.tensor_scalar(out=gc(k, TB), in0=gc(k, BET),
                                  scalar1=1.0, scalar2=1e-9 / NPR,
                                  op0=ALU.max, op1=ALU.mult))
            ctx.gp(v)
            ctx.v(v.tensor_tensor(out=gc(k, TA), in0=gc(k, TB),
                                  in1=gc(k, BET), op=ALU.add))
            for pp in range(0, GQ, 2):
                for tt in (pp, pp + 1):
                    ctx.v(v.tensor_scalar(
                        out=gbuf[:, tt * S:(tt + 1) * S], in0=eb(k, tt),
                        scalar1=gc1(k, TST, tt), scalar2=None,
                        op0=ALU.is_ge))
                for tt in (pp, pp + 1):
                    sc_b = scrt if tt % 2 == 0 else junk
                    ctx.v(v.tensor_scalar(out=sc_b[:], in0=eb(k, tt),
                                          scalar1=gc1(k, BET, tt),
                                          scalar2=None,
                                          op0=ALU.subtract))
                for tt in (pp, pp + 1):
                    sc_b = scrt if tt % 2 == 0 else junk
                    ctx.v(v.tensor_tensor(
                        out=gbuf[:, tt * S:(tt + 1) * S], in0=sc_b[:],
                        in1=gbuf[:, tt * S:(tt + 1) * S], op=ALU.mult))
            for tt in range(GQ):
                ctx.v(v.tensor_scalar(out=gbuf[:, tt * S:(tt + 1) * S],
                                      in0=gbuf[:, tt * S:(tt + 1) * S],
                                      scalar1=gc1(k, TA, tt),
                                      scalar2=None, op0=ALU.add))

    def s4a(k):
        h, gi = groups[k]
        v_end = ctx.cnt["V"]
        aev0 = ctx.cnt["A"]
        HKC = KCH // 2
        NHF = 2 * GQ

        @blk.tensor
        def _(p):
            for H in range(NHF):
                tt, hf = H // 2, H % 2
                if H == 0:
                    ctx.waitv(p, "V", v_end)
                    ctx.waitv(p, "A", aev0)
                elif H == 1:
                    pass
                else:
                    ctx.waitv(p, "A", aev0 + H - 1)
                for c in range(HKC):
                    cc = hf * HKC + c
                    ctx.p(p.transpose(
                        out=tr_ps2[:, (S // 2) * hf + 128 * c:
                                   (S // 2) * hf + 128 * (c + 1)],
                        in_=gbuf[:, tt * S + 128 * cc:
                                 tt * S + 128 * (cc + 1)],
                        identity=per["ident"][:]))

        pev0 = ctx.cnt["P"] - NHF * HKC

        @blk.scalar
        def _(a):
            for H in range(NHF):
                tt, hf = H // 2, H % 2
                ctx.waitv(a, "P", pev0 + (H + 1) * HKC)
                o3 = gTbuf[:].rearrange("p (c w) -> p c w", w=GW)[
                    :, hf * HKC:(hf + 1) * HKC, 128 * tt:128 * (tt + 1)]
                i3 = tr_ps2[:, (S // 2) * hf:(S // 2) * (hf + 1)].rearrange(
                    "p (c w) -> p c w", w=128)
                ctx.a(a.copy(out=o3, in_=i3))

        sJ = ctx.snap()

        @blk.tensor
        def _(p):
            ctx.wait(p, sJ, "A", "V")
            for c in range(KCH):
                ctx.p(p.matmul(
                    out=pv_ps[:],
                    lhsT=vtm(k)[:, (DK + 1) * c:(DK + 1) * (c + 1)],
                    rhs=gTbuf[:, c * GW:(c + 1) * GW],
                    start=(c == 0), stop=(c == KCH - 1)))

        sK = ctx.snap()

        @blk.scalar
        def _(a):
            ctx.wait(a, sK, "P", "Q")
            ctx.a(a.activation(out=invsc[DK:DK + 1, :],
                               in_=pv_ps[DK:DK + 1, :], func=AF.Ln,
                               bias=per["epsc"][64:65, 2:3], scale=1.0))
            ctx.ga(a)
            ctx.a(a.activation(out=invsc[DK:DK + 1, :],
                               in_=invsc[DK:DK + 1, :], func=AF.Exp,
                               scale=-1.0))

        sL2 = ctx.snap()

        @blk.sync
        def _(q):
            ctx.wait(q, sL2, "A", "V")
            ctx.qd(q.dma_start(out=drow[5, 0:GW],
                               in_=invsc[DK:DK + 1, :]))
            ctx.waitv(q, "Q", ctx.cnt["Q"])
            ctx.qd(q.dma_start(out=invbc[:],
                               in_=drow[5:6, 0:GW].partition_broadcast(64)))

    def s4b(k):
        h, gi = groups[k]
        sM = ctx.snap()

        @blk.vector
        def _(v):
            ctx.wait(v, sM, "Q")
            ctx.v(v.tensor_tensor(out=aot_t[:], in0=pv_ps[0:DK, :],
                                  in1=invbc[:], op=ALU.mult))

        sN = ctx.snap()

        @blk.sync
        def _(q):
            ctx.wait(q, sN, "V")
            ctx.qd(q.dma_start(
                out=aoT[64 * h:64 * h + 64, gi * GW:(gi + 1) * GW],
                in_=aot_t[:]))
            if scr.get("dbg") is not None and \
                    30 * GQ * (k + 1) <= 4096:
                ctx.qd(q.dma_start(
                    out=scr["dbg"][:, 30 * GQ * k:30 * GQ * (k + 1)],
                    in_=gcol2[:, P(k) * 30 * GQ:(P(k) + 1) * 30 * GQ]))

    pstate["exp_a"] = ctx.cnt["A"]

    def s1_all(k):
        if groups[k][1] == 0:
            head_prep(k)
        for tt in range(GQ):
            s1_qk(k, tt)
            s1_rx(k, tt)
            s1_exp(k, tt)
        s1_smp(k)

    s1_all(0)
    for k in range(NGR):
        nxt = k + 1 if k + 1 < NGR else None
        for rnd in range(NR - 2):
            s2_A(k, rnd)
            if nxt is not None and rnd < GQ:
                s1_qk(nxt, rnd)
                s1_rx(nxt, rnd)
            s2_chainz(k, rnd)
            if nxt is not None and rnd < GQ:
                s1_exp(nxt, rnd)
            if rnd == 0 and k > 0:
                s4a(k - 1)
            if rnd == 1:
                if k > 0:
                    s4b(k - 1)
                if nxt is not None and groups[nxt][1] == 0:
                    head_prep(nxt)
        if nxt is not None:
            s1_smp(nxt)
        for rnd in range(NR - 2, NR):
            s2_probeV(k, rnd)
            if rnd < NR - 1:
                s2_chainz(k, rnd)
        s3(k)
        if k == NGR - 1:
            s4a(k)
            s4b(k)

    blk_cm.__exit__(None, None, None)



_PROG_CACHE = {}


def kernel(L0_attn_x, L0_attn_Wqkv, L0_attn_bqkv, L0_attn_Wo, L0_attn_bo,
           L0_ffn_w1, L0_ffn_conv_w, L0_ffn_conv_b, L0_ffn_w2):
    x = np.ascontiguousarray(np.asarray(L0_attn_x, dtype=np.float32))
    B, S, D = x.shape
    n_cores = 8
    nb = B // n_cores
    cfg = make_cfg(S=S, D=D, HF=L0_ffn_w1.shape[0], NH=16, NB=nb)

    key = (B, S, D)
    if key not in _PROG_CACHE:
        _PROG_CACHE[key] = build_program(cfg)
    nc = _PROG_CACHE[key]

    w = dict(
        Wqkv=np.ascontiguousarray(np.asarray(L0_attn_Wqkv, dtype=np.float32)),
        bqkv=np.ascontiguousarray(np.asarray(L0_attn_bqkv, dtype=np.float32)),
        Wo=np.ascontiguousarray(np.asarray(L0_attn_Wo, dtype=np.float32)),
        bo=np.ascontiguousarray(np.asarray(L0_attn_bo, dtype=np.float32)),
        w1=np.ascontiguousarray(np.asarray(L0_ffn_w1, dtype=np.float32)),
        cw=np.ascontiguousarray(
            np.asarray(L0_ffn_conv_w, dtype=np.float32).reshape(-1, 3)),
        cb=np.ascontiguousarray(np.asarray(L0_ffn_conv_b, dtype=np.float32)),
        w2=np.ascontiguousarray(np.asarray(L0_ffn_w2, dtype=np.float32)),
    )
    in_maps = []
    for c in range(n_cores):
        m = dict(w)
        m["x_in"] = x[c * nb:(c + 1) * nb]
        in_maps.append(m)

    res = run_bass_kernel_spmd(nc, in_maps, core_ids=list(range(n_cores)))
    return np.concatenate([r["out"] for r in res.results], axis=0)


# version 20
# speedup vs baseline: 1.0591x; 1.0591x over previous
import math
from contextlib import ExitStack, contextmanager

import numpy as np
import concourse.bass as bass
import concourse.mybir as mybir
from concourse.bass_utils import run_bass_kernel_spmd

F32 = mybir.dt.float32
BF16 = mybir.dt.bfloat16
I32 = mybir.dt.int32
AF = mybir.ActivationFunctionType
ALU = mybir.AluOpType
AXX = mybir.AxisListType.X

Z1 = 1.2815515655446004
PHI_Z1 = 0.17549833193248682


def make_cfg(S=2048, D=1024, HF=4096, NH=16, NB=2):
    cfg = dict(S=S, D=D, HF=HF, NH=NH, NB=NB, DK=64)
    cfg["TOPK"] = max(1, int((1.0 - 0.9) * S))
    cfg["NT"] = S // 128
    cfg["DCH"] = D // 128
    cfg["HCH"] = HF // 128
    cfg["QCH"] = 3 * D // 128
    cfg["KCH"] = S // 128
    cfg["QW"] = min(256, S)
    cfg["NQ"] = S // cfg["QW"]
    cfg["GQ"] = min(4, cfg["NT"])
    cfg["NG"] = cfg["NT"] // cfg["GQ"]
    return cfg


class Ctx:
    def __init__(self, nc, es):
        self.nc = nc
        self.sem = {}
        self.cnt = {}
        self.uid = 0
        for name in ("V", "A", "P", "G", "Q", "W"):
            self.sem[name] = es.enter_context(nc.semaphore("sem" + name))
            self.cnt[name] = 0

    def name(self, n):
        self.uid += 1
        return f"{n}_{self.uid}"

    def _mark(self, name, instr, n):
        instr.then_inc(self.sem[name], n)
        self.cnt[name] += n
        return instr

    def v(self, i):
        return self._mark("V", i, 1)

    def a(self, i):
        return self._mark("A", i, 1)

    def p(self, i):
        return self._mark("P", i, 1)

    def g(self, i):
        return self._mark("G", i, 1)

    def qd(self, i):
        return self._mark("Q", i, 16)

    def wd(self, i):
        return self._mark("W", i, 16)

    def snap(self):
        return dict(self.cnt)

    def wait(self, h, snap, *names):
        for n in names:
            if snap[n] > 0:
                h.wait_ge(self.sem[n], snap[n])

    def waitv(self, h, name, val):
        if val > 0:
            h.wait_ge(self.sem[name], val)

    def gp(self, h, name="V"):
        h.wait_ge(self.sem[name], self.cnt[name])

    def ga(self, a):
        self.a(a.activation(out=self.adum[0:1, 0:1], in_=self.adum[0:1, 1:2],
                            func=AF.Identity, scale=1.0))

    def pre6(self, a):
        if self.set6 is not None:
            a.add_instruction(mybir.InstLoadActFuncSet(
                name=self.nc.get_next_instruction_name(),
                act_func_set_id=self.set6, ins=[], outs=[]))



def emit_rint(ctx, v, cfg, out_bf, in_f32, i32scr):
    if cfg.get("rint", "cast") == "cast":
        ctx.v(v.tensor_copy(out=i32scr, in_=in_f32))
        ctx.gp(v)
        ctx.v(v.tensor_copy(out=out_bf, in_=i32scr))
    else:
        ctx.v(v.tensor_scalar(out=out_bf, in0=in_f32, scalar1=12582912.0,
                              scalar2=12582912.0, op0=ALU.add, op1=ALU.subtract))


def sp512(n):
    return [(o, min(512, n - o)) for o in range(0, n, 512)]


@contextmanager
def scope():
    es = ExitStack()
    try:
        yield es
    finally:
        es.close()



def build_program(cfg):
    nc = bass.Bass("TRN2", target_bir_lowering=False, detect_race_conditions=False)
    S, D, HF, NB = cfg["S"], cfg["D"], cfg["HF"], cfg["NB"]
    QCH, HCH, NT = cfg["QCH"], cfg["HCH"], cfg["NT"]

    dt = nc.dram_tensor
    t_in = dict(
        x_in=dt("x_in", [NB, S, D], F32, kind="ExternalInput").ap(),
        Wqkv=dt("Wqkv", [3 * D, D], F32, kind="ExternalInput").ap(),
        bqkv=dt("bqkv", [3 * D], F32, kind="ExternalInput").ap(),
        Wo=dt("Wo", [D, D], F32, kind="ExternalInput").ap(),
        bo=dt("bo", [D], F32, kind="ExternalInput").ap(),
        w1=dt("w1", [HF, D], F32, kind="ExternalInput").ap(),
        cw=dt("cw", [HF, 3], F32, kind="ExternalInput").ap(),
        cb=dt("cb", [HF], F32, kind="ExternalInput").ap(),
        w2=dt("w2", [D, HF], F32, kind="ExternalInput").ap(),
    )
    out = dt("out", [NB, S, D], F32, kind="ExternalOutput").ap()
    scr = dict(
        xres=dt("xres", [NB, S, D], F32, kind="Internal").ap(),
        wqkvT=dt("wqkvT", [D, 3 * D], BF16, kind="Internal").ap(),
        woT=dt("woT", [D, D], BF16, kind="Internal").ap(),
        w1qT=dt("w1qT", [D, HF], BF16, kind="Internal").ap(),
        w2qT=dt("w2qT", [HF, D], BF16, kind="Internal").ap(),
        aoT=dt("aoT", [D, S], BF16, kind="Internal").ap(),
        drow=dt("drow", [8, max(S, 128)], F32, kind="Internal").ap(),
        dbg=(dt("dbg", [128, 4096], F32, kind="ExternalOutput").ap()
             if cfg.get("dbg") else None),
        dsc=dt("dsc", [1, 8], F32, kind="Internal").ap(),
    )

    es = ExitStack()
    ctx = Ctx(nc, es)

    def sb(name, shape, d):
        return es.enter_context(nc.sbuf_tensor(name, shape, d))

    per = dict(
        ident=sb("ident", [128, 128], BF16),
        identf=sb("identf", [128, 128], F32),
        epsc=sb("epsc", [128, 3], F32),
        ident2=sb("ident2", [128, 64], BF16),
        onescol=sb("onescol", [128, 1], BF16),
        e65row=sb("e65row", [1, cfg["DK"] + 1], F32),
        iota16=sb("iota16", [128, 16], F32),
        bobc=sb("bobc", [128, D], F32),
        bqkvcol=sb("bqkvcol", [128, QCH], F32),
        cwcol=sb("cwcol", [128, 3 * HCH], F32),
        cbcol=sb("cbcol", [128, HCH], F32),
        wscal=sb("wscal", [128, 4], F32),
        adum=sb("adum", [1, 2], F32),
    )
    ctx.adum = per["adum"]
    ctx.set6 = None
    try:
        from concourse.hw_specs import get_activation_tables
        for _i, _fs in enumerate(get_activation_tables(nc.m.arch).values()):
            if AF.Exp in _fs and AF.Ln in _fs and AF.Sign in _fs:
                ctx.set6 = _i
                break
    except Exception:
        ctx.set6 = 6

    _init_consts(ctx, cfg, per, t_in)
    _prep_weights(ctx, cfg, t_in, scr, per)
    _copy_x(ctx, cfg, t_in["x_in"], scr["xres"])

    stages = cfg.get("stages", ("f1", "at", "f2"))
    for b in range(NB):
        if "f1" in stages:
            with nc.named_scope(f"f1_b{b}"):
                _ffn_sublayer(ctx, cfg, b, scr, per,
                              dst=(out if stages[-1] == "f1" else scr["xres"]))
        if "at" in stages:
            with nc.named_scope(f"at_b{b}"):
                _attn_sublayer(ctx, cfg, b, scr, per)
        if "f2" in stages:
            with nc.named_scope(f"f2_b{b}"):
                _ffn_sublayer(ctx, cfg, b, scr, per, dst=out)

    with nc.Block() as blk:
        fin = ctx.snap()

        @blk.gpsimd
        def _(g):
            ctx.wait(g, fin, "Q", "W", "V", "A", "P", "G")

    es.close()
    return nc



def _init_consts(ctx, cfg, per, t_in):
    nc = ctx.nc
    DK = cfg["DK"]
    with nc.Block() as blk:
        @blk.gpsimd
        def _(g):
            ctx.g(g.memset(per["adum"][:], 0.0))
            ctx.g(g.iota(per["ident"][:], pattern=[[1, 128]], base=0,
                         channel_multiplier=-1,
                         allow_small_or_imprecise_dtypes=True))
            ctx.g(g.iota(per["iota16"][:, 0:8], pattern=[[1, 8]], base=1,
                         channel_multiplier=0,
                         allow_small_or_imprecise_dtypes=True))
            ctx.g(g.iota(per["iota16"][:, 8:16], pattern=[[-1, 8]], base=0,
                         channel_multiplier=0,
                         allow_small_or_imprecise_dtypes=True))
            ctx.g(g.memset(per["onescol"][:], 1.0))
            ctx.g(g.memset(per["e65row"][:], 0.0))
            ctx.g(g.memset(per["e65row"][0:1, DK:DK + 1], 1.0))

        s0 = ctx.snap()

        @blk.vector
        def _(v):
            ctx.wait(v, s0, "G")
            ctx.v(v.tensor_scalar(out=per["ident"][:], in0=per["ident"][:],
                                  scalar1=0.0, scalar2=None, op0=ALU.is_equal))
            ctx.gp(v)
            ctx.v(v.tensor_copy(out=per["identf"][:], in_=per["ident"][:]))
            ctx.v(v.memset(per["epsc"][:, 0:1], 1e-5))
            ctx.v(v.memset(per["epsc"][:, 1:2], 1e-30))
            ctx.v(v.memset(per["epsc"][:, 2:3], 1e-37))

        s1 = ctx.snap()

        @blk.sync
        def _(q):
            ctx.wait(q, s1, "V")
            ctx.qd(q.dma_start(out=per["ident2"][0:64, :],
                               in_=per["ident"][0:64, 0:64]))
            ctx.qd(q.dma_start(out=per["ident2"][64:128, :],
                               in_=per["ident"][0:64, 0:64]))
            ctx.qd(q.dma_start(out=per["bobc"][:],
                               in_=t_in["bo"][None, :].partition_broadcast(128)))
            ctx.qd(q.dma_start(
                out=per["cwcol"][:].rearrange("p (c k) -> p c k", k=3),
                in_=t_in["cw"].rearrange("(c p) k -> p c k", p=128)))

        @blk.gpsimd
        def _(g):
            ctx.wd(g.dma_start(out=per["bqkvcol"][:],
                               in_=t_in["bqkv"].rearrange("(c p) -> p c", p=128),
                               allow_slow_non_contiguous=True))
            ctx.wd(g.dma_start(out=per["cbcol"][:],
                               in_=t_in["cb"].rearrange("(c p) -> p c", p=128),
                               allow_slow_non_contiguous=True))


def _copy_x(ctx, cfg, x_in, xres):
    nc = ctx.nc
    NB, NT, D = cfg["NB"], cfg["NT"], cfg["D"]
    with scope() as es:
        xt = es.enter_context(nc.sbuf_tensor(ctx.name("xcopy"), [128, 2 * D], F32))
        with nc.Block() as blk:
            @blk.sync
            def _(q):
                for k in range(NB * NT):
                    bb, j = divmod(k, NT)
                    buf = xt[:, (k % 2) * D:(k % 2) * D + D]
                    ctx.qd(q.dma_start(out=buf, in_=x_in[bb, 128 * j:128 * (j + 1), :]))
                    ctx.waitv(q, "Q", ctx.cnt["Q"])
                    ctx.qd(q.dma_start(out=xres[bb, 128 * j:128 * (j + 1), :], in_=buf))



def _prep_weights(ctx, cfg, t_in, scr, per):
    nc = ctx.nc
    D, HF, DCH, HCH = cfg["D"], cfg["HF"], cfg["DCH"], cfg["HCH"]
    NC = max(D, HF)

    with scope() as es:
        wtmp = es.enter_context(nc.sbuf_tensor("wtmp", [128, NC], F32))
        wint = es.enter_context(nc.sbuf_tensor("wrm", [128, NC], I32))
        wbf = es.enter_context(nc.sbuf_tensor("wbf", [128, NC], BF16))
        wtr = es.enter_context(nc.sbuf_tensor("wtr", [128, NC], BF16))
        wabs = es.enter_context(nc.sbuf_tensor("wabs", [128, max(HCH, DCH)], F32))
        wsum = es.enter_context(nc.sbuf_tensor("wsum", [128, 1], F32))
        wmisc = es.enter_context(nc.sbuf_tensor("wmisc", [1, 8], F32))
        onef = es.enter_context(nc.sbuf_tensor("onef", [128, 1], F32))
        scal_ps = es.enter_context(nc.psum_tensor("scal_ps", [1, 2], F32))
        tr_ps = es.enter_context(nc.psum_tensor("wtr_ps", [128, NC], BF16))

        with nc.Block() as blk:
            @blk.vector
            def _(v):
                ctx.v(v.memset(onef[:], 1.0))

        jobs = [("qkv", t_in["Wqkv"], scr["wqkvT"], 3 * D // 128, DCH, None),
                ("wo", t_in["Wo"], scr["woT"], DCH, DCH, None),
                ("w1", t_in["w1"], scr["w1qT"], HCH, DCH, 0),
                ("w2", t_in["w2"], scr["w2qT"], DCH, HCH, 2)]

        for (name, src, dst, nrow_ch, ncol_ch, scol) in jobs:
            ncol = ncol_ch * 128
            quant = scol is not None
            if quant:
                for r in range(nrow_ch):
                    with nc.Block() as blk:
                        s0 = ctx.snap()

                        @blk.sync
                        def _(q):
                            ctx.wait(q, s0, "V")
                            ctx.qd(q.dma_start(out=wtmp[:, 0:ncol],
                                               in_=src[128 * r:128 * (r + 1), :]))

                        s1 = ctx.snap()

                        @blk.vector
                        def _(v):
                            ctx.wait(v, s1, "Q")
                            ctx.v(v.tensor_reduce(out=wabs[:, r:r + 1],
                                                  in_=wtmp[:, 0:ncol], axis=AXX,
                                                  op=ALU.add,
                                                  apply_absolute_value=True))
                with nc.Block() as blk:
                    s0 = ctx.snap()

                    @blk.vector
                    def _(v):
                        ctx.wait(v, s0, "Q")
                        ctx.gp(v)
                        ctx.v(v.tensor_reduce(out=wsum[:], in_=wabs[:, 0:nrow_ch],
                                              axis=AXX, op=ALU.add))

                    s1 = ctx.snap()

                    @blk.tensor
                    def _(p):
                        ctx.wait(p, s1, "V")
                        ctx.p(p.matmul(out=scal_ps[0:1, 0:1], lhsT=wsum[:],
                                       rhs=onef[:], start=True, stop=True))

                    s2 = ctx.snap()

                    @blk.vector
                    def _(v):
                        ctx.wait(v, s2, "P")
                        n_el = float(nrow_ch * 128 * ncol)
                        ctx.v(v.tensor_scalar(out=wmisc[0:1, 0:1],
                                              in0=scal_ps[0:1, 0:1],
                                              scalar1=1.0 / n_el, scalar2=1e-5,
                                              op0=ALU.mult, op1=ALU.max))
                        ctx.gp(v)
                        ctx.v(v.reciprocal(out=wmisc[0:1, 1:2], in_=wmisc[0:1, 0:1]))
                        ctx.v(v.tensor_scalar(out=wmisc[0:1, 2:3], in0=wmisc[0:1, 0:1],
                                              scalar1=1.0 / 127.0, scalar2=None,
                                              op0=ALU.mult))

                    s3 = ctx.snap()

                    @blk.sync
                    def _(q):
                        ctx.wait(q, s3, "V")
                        ctx.qd(q.dma_start(out=scr["dsc"][0:1, 0:2],
                                           in_=wmisc[0:1, 1:3]))
                        ctx.waitv(q, "Q", ctx.cnt["Q"])
                        ctx.qd(q.dma_start(
                            out=per["wscal"][:, scol:scol + 2],
                            in_=scr["dsc"][0:1, 0:2].partition_broadcast(128)))
                        if scr.get("dbg") is not None:
                            ctx.waitv(q, "Q", ctx.cnt["Q"])
                            ctx.qd(q.dma_start(
                                out=scr["dbg"][:, 8 * scol:8 * scol + 4],
                                in_=per["wscal"][:, 0:4]))
                            ctx.qd(q.dma_start(
                                out=scr["dbg"][:, 20 + 40 * scol:20 + 40 * scol + nrow_ch],
                                in_=wabs[:, 0:nrow_ch]))

            for r in range(nrow_ch):
                with nc.Block() as blk:
                    s0 = ctx.snap()

                    @blk.sync
                    def _(q):
                        ctx.wait(q, s0, "V", "P")
                        ctx.qd(q.dma_start(out=wtmp[:, 0:ncol],
                                           in_=src[128 * r:128 * (r + 1), :]))

                    s1 = ctx.snap()

                    @blk.vector
                    def _(v):
                        ctx.wait(v, s1, "Q", "W")
                        if quant:
                            ctx.v(v.tensor_scalar(out=wtmp[:, 0:ncol],
                                                  in0=wtmp[:, 0:ncol],
                                                  scalar1=per["wscal"][:, scol:scol + 1],
                                                  scalar2=None, op0=ALU.mult))
                            ctx.gp(v)
                            if cfg.get("rint", "cast") == "cast":
                                ctx.v(v.tensor_copy(out=wint[:, 0:ncol],
                                                    in_=wtmp[:, 0:ncol]))
                            else:
                                ctx.v(v.tensor_scalar(
                                    out=wint[:, 0:ncol].bitcast(F32),
                                    in0=wtmp[:, 0:ncol], scalar1=12582912.0,
                                    scalar2=12582912.0, op0=ALU.add,
                                    op1=ALU.subtract))
                            wq_in = (wint[:, 0:ncol] if cfg.get("rint", "cast") == "cast"
                                     else wint[:, 0:ncol].bitcast(F32))
                            ctx.gp(v)
                            ctx.v(v.tensor_scalar(out=wbf[:, 0:ncol],
                                                  in0=wq_in,
                                                  scalar1=-1.0, scalar2=1.0,
                                                  op0=ALU.max, op1=ALU.min))
                        else:
                            ctx.v(v.tensor_copy(out=wbf[:, 0:ncol],
                                                in_=wtmp[:, 0:ncol]))

                    s2 = ctx.snap()

                    @blk.tensor
                    def _(p):
                        ctx.wait(p, s2, "V", "A")
                        for c in range(ncol_ch):
                            ctx.p(p.transpose(out=tr_ps[:, 128 * c:128 * (c + 1)],
                                              in_=wbf[:, 128 * c:128 * (c + 1)],
                                              identity=per["ident"][:]))

                    s3 = ctx.snap()

                    @blk.scalar
                    def _(a):
                        ctx.wait(a, s3, "P", "W")
                        ctx.a(a.copy(out=wtr[:, 0:ncol], in_=tr_ps[:, 0:ncol]))

                    s4 = ctx.snap()

                    @blk.gpsimd
                    def _(g):
                        ctx.wait(g, s4, "A")
                        for c in range(ncol_ch):
                            ctx.wd(g.dma_start(
                                out=dst[128 * c:128 * (c + 1), 128 * r:128 * (r + 1)],
                                in_=wtr[:, 128 * c:128 * (c + 1)]))



def _ffn_sublayer(ctx, cfg, b, scr, per, dst):
    nc = ctx.nc
    S, D, HF = cfg["S"], cfg["D"], cfg["HF"]
    NT, DCH, HCH = cfg["NT"], cfg["DCH"], cfg["HCH"]
    QW, NQ = cfg["QW"], cfg["NQ"]
    SP = S + 2
    xres = scr["xres"]
    drow = scr["drow"]

    with scope() as es:
        t = lambda n, s, d: es.enter_context(nc.sbuf_tensor(ctx.name(n), s, d))
        xtile = t("f_x", [128, 2 * D], F32)
        xh = t("f_xh", [128, D], F32)
        xint = t("f_xrm", [128, D], I32)
        xqtm = t("f_xqtm", [128, D], BF16)
        st6 = t("f_st6", [128, 24], F32)
        st2 = t("f_st2", [128, 2], F32)
        tc = t("f_tc", [128, 12], F32)
        inv1col = t("f_inv1c", [128, NT], F32)
        inv2col = t("f_inv2c", [128, NT], F32)
        s2tok = t("f_s2tok", [128, NT], F32)
        xqT = t("f_xqT", [128, DCH * SP], BF16)
        w1all = t("f_w1all", [128, DCH * HF], BF16)
        w2s = t("f_w2s", [128, 2 * D], BF16)
        inv1bc = t("f_inv1bc", [128, SP], F32)
        s2bc = t("f_s2bc", [128, QW], F32)
        ubuf = t("f_ubuf", [128, 2 * (QW + 2)], F32)
        cbuf = t("f_cbuf", [128, 2 * QW], F32)
        hbuf = t("f_hbuf", [128, HCH * QW], F32)
        hmax = t("f_hmax", [128, QW], F32)
        hmin = t("f_hmin", [128, QW], F32)
        hqbuf = t("f_hq", [128, HCH * QW], BF16)
        qtmp = t("f_qtmp", [128, QW], F32)
        hint = t("f_rmq", [128, QW], I32)
        resv = t("f_resv", [128, D], F32)
        h_ps = es.enter_context(nc.psum_tensor(ctx.name("f_hps"), [128, 1024], F32))
        tr_ps = es.enter_context(nc.psum_tensor(ctx.name("f_trps"), [128, D], BF16))
        trf_ps = es.enter_context(nc.psum_tensor(ctx.name("f_trfps"), [128, 2 * 128], F32))
        y2_psl = [es.enter_context(nc.psum_tensor(ctx.name("f_y2ps"), [128, D], F32))
                  for _ in range(QW // 128)]

        ENTRY = ctx.snap()

        with nc.Block() as blk:
            @blk.sync
            def _(q):
                ctx.wait(q, ENTRY, "V", "A", "P", "G", "W")
                ctx.qd(q.dma_start(
                    out=w1all[:].rearrange("p (c h) -> p c h", c=DCH),
                    in_=scr["w1qT"].rearrange("(c p) h -> p c h", p=128)))

            @blk.vector
            def _(v):
                ctx.wait(v, ENTRY, "Q", "A", "P", "G", "W")
                xqT3 = xqT[:].rearrange("p (c s) -> p c s", s=SP)
                ctx.v(v.memset(xqT3[:, :, 0:1], 0.0))
                ctx.v(v.memset(xqT3[:, :, SP - 1:SP], 0.0))
                ctx.v(v.memset(inv1bc[:, 0:1], 0.0))
                ctx.v(v.memset(inv1bc[:, SP - 1:SP], 0.0))

        a_evac = None
        for j in range(NT):
            with nc.Block() as blk:
                s0 = ctx.snap()

                @blk.sync
                def _(q):
                    ctx.wait(q, s0, "V")
                    ctx.qd(q.dma_start(out=xtile[:, (j % 2) * D:(j % 2) * D + D],
                                       in_=xres[b, 128 * j:128 * (j + 1), :]))

                s1 = ctx.snap()
                xt = xtile[:, (j % 2) * D:(j % 2) * D + D]

                @blk.vector
                def _(v):
                    ctx.wait(v, s1, "Q")
                    for ii, (o, wd) in enumerate(sp512(D)):
                        ctx.v(v.bn_stats(out=st6[:, 6 * ii:6 * (ii + 1)],
                                         in_=xt[:, o:o + wd]))
                    ctx.gp(v)
                    ctx.v(v.bn_aggr(out=st2[:], in_=st6[:, 0:6 * len(sp512(D))]))

                s2 = ctx.snap()

                @blk.scalar
                def _(a):
                    ctx.wait(a, s2, "V")
                    ctx.a(a.activation(out=tc[:, 0:1], in_=st2[:, 1:2], func=AF.Ln,
                                       bias=per["epsc"][:, 0:1], scale=1.0))
                    ctx.ga(a)
                    ctx.a(a.activation(out=tc[:, 1:2], in_=tc[:, 0:1], func=AF.Exp,
                                       scale=-0.5))

                s3 = ctx.snap()

                @blk.vector
                def _(v):
                    ctx.wait(v, s3, "A")
                    ctx.v(v.tensor_scalar(out=xh[:], in0=xt,
                                          scalar1=st2[:, 0:1], scalar2=tc[:, 1:2],
                                          op0=ALU.subtract, op1=ALU.mult))
                    ctx.gp(v)
                    ctx.v(v.tensor_reduce(out=tc[:, 2:3], in_=xh[:], axis=AXX,
                                          op=ALU.max, apply_absolute_value=True))
                    ctx.gp(v)
                    ctx.v(v.tensor_scalar(out=tc[:, 3:4], in0=tc[:, 2:3],
                                          scalar1=1e-5, scalar2=None, op0=ALU.max))
                    ctx.gp(v)
                    ctx.v(v.reciprocal(out=tc[:, 4:5], in_=tc[:, 3:4]))
                    ctx.gp(v)
                    ctx.v(v.tensor_scalar(out=tc[:, 5:6], in0=tc[:, 4:5],
                                          scalar1=127.0, scalar2=None, op0=ALU.mult))
                    ctx.v(v.tensor_scalar(out=inv1col[:, j:j + 1], in0=tc[:, 3:4],
                                          scalar1=per["wscal"][:, 1:2], scalar2=None,
                                          op0=ALU.mult))
                    ctx.v(v.tensor_scalar(out=xh[:], in0=xh[:],
                                          scalar1=tc[:, 5:6], scalar2=None,
                                          op0=ALU.mult))
                    ctx.gp(v)
                    emit_rint(ctx, v, cfg, xqtm[:], xh[:], xint[:])

                s4 = ctx.snap()

                @blk.tensor
                def _(p):
                    ctx.wait(p, s4, "V")
                    if a_evac:
                        ctx.waitv(p, "A", a_evac)
                    for c in range(DCH):
                        ctx.p(p.transpose(out=tr_ps[:, 128 * c:128 * (c + 1)],
                                          in_=xqtm[:, 128 * c:128 * (c + 1)],
                                          identity=per["ident"][:]))

                s5 = ctx.snap()

                @blk.scalar
                def _(a):
                    ctx.wait(a, s5, "P")
                    o3 = xqT[:].rearrange("p (c s) -> p c s", s=SP)[
                        :, :, 1 + 128 * j:1 + 128 * (j + 1)]
                    i3 = tr_ps[:].rearrange("p (c w) -> p c w", w=128)
                    ctx.a(a.copy(out=o3, in_=i3))
                a_evac = ctx.cnt["A"]

        with nc.Block() as blk:
            s0 = ctx.snap()

            @blk.gpsimd
            def _(g):
                ctx.wait(g, s0, "V")
                ctx.wd(g.dma_start(out=drow[0, 0:S].rearrange("(t p) -> p t", p=128),
                                   in_=inv1col[:, 0:NT],
                                   allow_slow_non_contiguous=True))

            s1 = ctx.snap()

            @blk.sync
            def _(q):
                ctx.wait(q, s1, "W")
                ctx.qd(q.dma_start(out=inv1bc[:, 1:S + 1],
                                   in_=drow[0:1, 0:S].partition_broadcast(128)))

        for qw in range(NQ):
            base = qw * QW
            with nc.Block() as blk:
                s0 = ctx.snap()

                @blk.vector
                def _(v):
                    ctx.wait(v, s0, "Q", "P")
                    ctx.v(v.memset(hmax[:], -3.0e38))
                    ctx.v(v.memset(hmin[:], 3.0e38))

            with nc.Block() as blk:
                ent = ctx.snap()
                p0, v0, a0, q0 = ent["P"], ent["V"], ent["A"], ent["Q"]
                P_PER_M = DCH
                V_PER_M = 6
                A_PER_M = 1
                Q_PER_M = 16

                @blk.tensor
                def _(p):
                    for m in range(HCH):
                        if m == 0:
                            ctx.wait(p, ent, "V", "A")
                        if m >= 2:
                            ctx.waitv(p, "V", v0 + (m - 2) * V_PER_M + 1)
                        hp = h_ps[:, (m % 2) * 512:(m % 2) * 512 + QW + 2]
                        for c in range(DCH):
                            ctx.p(p.matmul(
                                out=hp,
                                lhsT=w1all[:, c * HF + 128 * m:c * HF + 128 * (m + 1)],
                                rhs=xqT[:, c * SP + base:c * SP + base + QW + 2],
                                start=(c == 0), stop=(c == DCH - 1)))

                @blk.vector
                def _(v):
                    for m in range(HCH):
                        ctx.waitv(v, "P", p0 + (m + 1) * P_PER_M)
                        if m >= 2:
                            ctx.waitv(v, "A", a0 + (m - 1) * A_PER_M)
                        hp = h_ps[:, (m % 2) * 512:(m % 2) * 512 + QW + 2]
                        ub = ubuf[:, (m % 2) * (QW + 2):(m % 2) * (QW + 2) + QW + 2]
                        cb2 = cbuf[:, (m % 2) * QW:(m % 2) * QW + QW]
                        ctx.v(v.tensor_tensor(out=ub, in0=hp,
                                              in1=inv1bc[:, base:base + QW + 2],
                                              op=ALU.mult))
                        ctx.gp(v)
                        ctx.v(v.tensor_scalar(out=cb2, in0=ub[:, 0:QW],
                                              scalar1=per["cwcol"][:, 3 * m:3 * m + 1],
                                              scalar2=None, op0=ALU.mult))
                        ctx.gp(v)
                        ctx.v(v.scalar_tensor_tensor(
                            out=cb2, in0=ub[:, 1:QW + 1],
                            scalar=per["cwcol"][:, 3 * m + 1:3 * m + 2],
                            in1=cb2, op0=ALU.mult, op1=ALU.add))
                        ctx.gp(v)
                        ctx.v(v.scalar_tensor_tensor(
                            out=cb2, in0=ub[:, 2:QW + 2],
                            scalar=per["cwcol"][:, 3 * m + 2:3 * m + 3],
                            in1=cb2, op0=ALU.mult, op1=ALU.add))
                        ctx.waitv(v, "A", a0 + (m + 1) * A_PER_M)
                        hb = hbuf[:, m * QW:(m + 1) * QW]
                        ctx.v(v.tensor_tensor(out=hmax[:], in0=hmax[:], in1=hb,
                                              op=ALU.max))
                        ctx.v(v.tensor_tensor(out=hmin[:], in0=hmin[:], in1=hb,
                                              op=ALU.min))

                @blk.scalar
                def _(a):
                    for m in range(HCH):
                        ctx.waitv(a, "V", v0 + m * V_PER_M + 4)
                        cb2 = cbuf[:, (m % 2) * QW:(m % 2) * QW + QW]
                        ctx.a(a.activation(out=hbuf[:, m * QW:(m + 1) * QW], in_=cb2,
                                           func=AF.Silu,
                                           bias=per["cbcol"][:, m:m + 1], scale=1.0))

            with nc.Block() as blk:
                s0 = ctx.snap()

                @blk.vector
                def _(v):
                    ctx.v(v.tensor_scalar(out=hmin[:], in0=hmin[:], scalar1=-1.0,
                                          scalar2=None, op0=ALU.mult))
                    ctx.gp(v)
                    ctx.v(v.tensor_tensor(out=hmax[:], in0=hmax[:], in1=hmin[:],
                                          op=ALU.max))

                s1 = ctx.snap()

                @blk.tensor
                def _(p):
                    ctx.wait(p, s1, "V")
                    for w in range(QW // 128):
                        ctx.p(p.transpose(out=trf_ps[:, 128 * w:128 * (w + 1)],
                                          in_=hmax[:, 128 * w:128 * (w + 1)],
                                          identity=per["identf"][:]))

                s2 = ctx.snap()

                @blk.vector
                def _(v):
                    ctx.wait(v, s2, "P")
                    nw = QW // 128
                    for w in range(nw):
                        jt = (base // 128) + w
                        ctx.v(v.tensor_reduce(out=s2tok[:, jt:jt + 1],
                                              in_=trf_ps[:, 128 * w:128 * (w + 1)],
                                              axis=AXX, op=ALU.max))
                        ctx.gp(v)
                        ctx.v(v.tensor_scalar(out=s2tok[:, jt:jt + 1],
                                              in0=s2tok[:, jt:jt + 1],
                                              scalar1=1e-5, scalar2=None,
                                              op0=ALU.max))
                        ctx.gp(v)
                        ctx.v(v.tensor_scalar(out=inv2col[:, jt:jt + 1],
                                              in0=s2tok[:, jt:jt + 1],
                                              scalar1=per["wscal"][:, 3:4],
                                              scalar2=0.5, op0=ALU.mult,
                                              op1=ALU.mult))
                        ctx.v(v.reciprocal(out=tc[:, 6:7], in_=s2tok[:, jt:jt + 1]))
                        ctx.gp(v)
                        ctx.v(v.tensor_scalar(out=s2tok[:, jt:jt + 1], in0=tc[:, 6:7],
                                              scalar1=127.0, scalar2=None,
                                              op0=ALU.mult))

                s3 = ctx.snap()

                @blk.gpsimd
                def _(g):
                    ctx.wait(g, s3, "V")
                    nw = QW // 128
                    jt = base // 128
                    ctx.wd(g.dma_start(
                        out=drow[1, base:base + QW].rearrange("(t p) -> p t", p=128),
                        in_=s2tok[:, jt:jt + nw],
                        allow_slow_non_contiguous=True))

                s4 = ctx.snap()

                @blk.sync
                def _(q):
                    ctx.wait(q, s4, "W")
                    ctx.qd(q.dma_start(
                        out=s2bc[:],
                        in_=drow[1:2, base:base + QW].partition_broadcast(128)))

            with nc.Block() as blk:
                s0 = ctx.snap()
                vq0 = ctx.cnt["V"]

                @blk.vector
                def _(v):
                    ctx.wait(v, s0, "Q")
                    for m in range(HCH):
                        hb = hbuf[:, m * QW:(m + 1) * QW]
                        ctx.v(v.tensor_tensor(out=qtmp[:], in0=hb, in1=s2bc[:],
                                              op=ALU.mult))
                        ctx.gp(v)
                        emit_rint(ctx, v, cfg, hqbuf[:, m * QW:(m + 1) * QW],
                                  qtmp[:], hint[:])

            NW = QW // 128
            with nc.Block() as blk:
                ent2 = ctx.snap()
                q2, p2 = ent2["Q"], ent2["P"]
                P_PER_M = NW * len(sp512(D))

                @blk.sync
                def _(q):
                    for m in range(HCH):
                        if m >= 2:
                            ctx.waitv(q, "P", p2 + (m - 1) * P_PER_M)
                        ctx.qd(q.dma_start(
                            out=w2s[:, (m % 2) * D:(m % 2) * D + D],
                            in_=scr["w2qT"][128 * m:128 * (m + 1), :]))

                V_QPM = 3 if cfg.get("rint", "cast") == "cast" else 2

                @blk.tensor
                def _(p):
                    for m in range(HCH):
                        ctx.waitv(p, "V", vq0 + (m + 1) * V_QPM)
                        ctx.waitv(p, "Q", q2 + 16 * (m + 1))
                        for w in range(NW):
                            for (o, wd) in sp512(D):
                                ctx.p(p.matmul(
                                    out=y2_psl[w][:, o:o + wd],
                                    lhsT=hqbuf[:, m * QW + 128 * w:m * QW + 128 * (w + 1)],
                                    rhs=w2s[:, (m % 2) * D + o:(m % 2) * D + o + wd],
                                    start=(m == 0), stop=(m == HCH - 1)))

            for w in range(NW):
                jt = (base // 128) + w
                with nc.Block() as blk:
                    s0 = ctx.snap()

                    @blk.sync
                    def _(q):
                        ctx.wait(q, s0, "V")
                        ctx.qd(q.dma_start(out=xtile[:, 0:D],
                                           in_=xres[b, 128 * jt:128 * (jt + 1), :]))

                    s2 = ctx.snap()

                    @blk.vector
                    def _(v):
                        ctx.wait(v, s2, "P", "Q")
                        ctx.v(v.scalar_tensor_tensor(
                            out=resv[:], in0=y2_psl[w][:, 0:D],
                            scalar=inv2col[:, jt:jt + 1], in1=xtile[:, 0:D],
                            op0=ALU.mult, op1=ALU.add))

                    s3 = ctx.snap()

                    @blk.sync
                    def _(q):
                        ctx.wait(q, s3, "V")
                        ctx.qd(q.dma_start(out=dst[b, 128 * jt:128 * (jt + 1), :],
                                           in_=resv[:]))



def _attn_sublayer(ctx, cfg, b, scr, per):
    nc = ctx.nc
    S, D = cfg["S"], cfg["D"]
    NT, DCH, QCH = cfg["NT"], cfg["DCH"], cfg["QCH"]
    xres = scr["xres"]

    with scope() as es_q:
        qkvT = es_q.enter_context(nc.sbuf_tensor(ctx.name("a_qkvT"), [128, QCH * S], BF16))

        with scope() as es:
            t = lambda n, s, d: es.enter_context(nc.sbuf_tensor(ctx.name(n), s, d))
            xtile = t("a_x", [128, 2 * D], F32)
            xh = t("a_xh", [128, D], BF16)
            st6 = t("a_st6", [128, 24], F32)
            st2 = t("a_st2", [128, 2], F32)
            tc = t("a_tc", [128, 4], F32)
            xhT = t("a_xhT", [128, DCH * S], BF16)
            wq_all = t("a_wq", [128, DCH * 3 * D], BF16)
            tr_ps = es.enter_context(nc.psum_tensor(ctx.name("a_trps"), [128, D], BF16))
            qkv_ps = es.enter_context(nc.psum_tensor(ctx.name("a_qkvps"), [128, S], F32))

            ENTRY = ctx.snap()
            with nc.Block() as blk:
                @blk.sync
                def _(q):
                    ctx.wait(q, ENTRY, "V", "A", "P", "G", "W")
                    ctx.qd(q.dma_start(
                        out=wq_all[:].rearrange("p (c h) -> p c h", c=DCH),
                        in_=scr["wqkvT"].rearrange("(c p) h -> p c h", p=128)))

            a_evac = None
            for j in range(NT):
                with nc.Block() as blk:
                    s0 = ctx.snap()

                    @blk.sync
                    def _(q):
                        ctx.wait(q, s0, "V")
                        ctx.qd(q.dma_start(out=xtile[:, (j % 2) * D:(j % 2) * D + D],
                                           in_=xres[b, 128 * j:128 * (j + 1), :]))

                    s1 = ctx.snap()
                    xt = xtile[:, (j % 2) * D:(j % 2) * D + D]

                    @blk.vector
                    def _(v):
                        ctx.wait(v, s1, "Q")
                        for ii, (o, wd) in enumerate(sp512(D)):
                            ctx.v(v.bn_stats(out=st6[:, 6 * ii:6 * (ii + 1)],
                                             in_=xt[:, o:o + wd]))
                        ctx.gp(v)
                        ctx.v(v.bn_aggr(out=st2[:], in_=st6[:, 0:6 * len(sp512(D))]))

                    s2 = ctx.snap()

                    @blk.scalar
                    def _(a):
                        ctx.wait(a, s2, "V")
                        ctx.a(a.activation(out=tc[:, 0:1], in_=st2[:, 1:2],
                                           func=AF.Ln, bias=per["epsc"][:, 0:1],
                                           scale=1.0))
                        ctx.ga(a)
                        ctx.a(a.activation(out=tc[:, 1:2], in_=tc[:, 0:1],
                                           func=AF.Exp, scale=-0.5))

                    s3 = ctx.snap()

                    @blk.vector
                    def _(v):
                        ctx.wait(v, s3, "A")
                        ctx.v(v.tensor_scalar(out=xh[:], in0=xt,
                                              scalar1=st2[:, 0:1], scalar2=tc[:, 1:2],
                                              op0=ALU.subtract, op1=ALU.mult))

                    s4 = ctx.snap()

                    @blk.tensor
                    def _(p):
                        ctx.wait(p, s4, "V")
                        if a_evac:
                            ctx.waitv(p, "A", a_evac)
                        for c in range(DCH):
                            ctx.p(p.transpose(out=tr_ps[:, 128 * c:128 * (c + 1)],
                                              in_=xh[:, 128 * c:128 * (c + 1)],
                                              identity=per["ident"][:]))

                    s5 = ctx.snap()

                    @blk.scalar
                    def _(a):
                        ctx.wait(a, s5, "P")
                        o3 = xhT[:].rearrange("p (c s) -> p c s", s=S)[
                            :, :, 128 * j:128 * (j + 1)]
                        i3 = tr_ps[:].rearrange("p (c w) -> p c w", w=128)
                        ctx.a(a.copy(out=o3, in_=i3))
                    a_evac = ctx.cnt["A"]

            for r in range(QCH):
                with nc.Block() as blk:
                    s0 = ctx.snap()

                    @blk.tensor
                    def _(p):
                        ctx.wait(p, s0, "Q", "A")
                        for c in range(DCH):
                            for (o, wd) in sp512(S):
                                ctx.p(p.matmul(
                                    out=qkv_ps[:, o:o + wd],
                                    lhsT=wq_all[:, c * 3 * D + 128 * r:c * 3 * D + 128 * (r + 1)],
                                    rhs=xhT[:, c * S + o:c * S + o + wd],
                                    start=(c == 0), stop=(c == DCH - 1)))

                    s1 = ctx.snap()

                    @blk.scalar
                    def _(a):
                        ctx.wait(a, s1, "P")
                        ctx.a(a.activation(out=qkvT[:, r * S:(r + 1) * S],
                                           in_=qkv_ps[:, 0:S], func=AF.Identity,
                                           bias=per["bqkvcol"][:, r:r + 1],
                                           scale=1.0))

        with scope() as es:
            _attn_heads(ctx, cfg, b, qkvT, scr, per, es)

    with scope() as es:
        t = lambda n, s, d: es.enter_context(nc.sbuf_tensor(ctx.name(n), s, d))
        wo_all = t("o_wo", [128, DCH * D], BF16)
        ao_t = t("o_ao", [128, 2 * D], BF16)
        xtile = t("o_x", [128, D], F32)
        resv = t("o_res", [128, D], F32)
        wo_ps = es.enter_context(nc.psum_tensor(ctx.name("o_wops"), [128, D], F32))

        with nc.Block() as blk:
            s0 = ctx.snap()

            @blk.sync
            def _(q):
                ctx.wait(q, s0, "V", "A", "P", "W")
                ctx.qd(q.dma_start(
                    out=wo_all[:].rearrange("p (c h) -> p c h", c=DCH),
                    in_=scr["woT"].rearrange("(c p) h -> p c h", p=128)))

        v_res = None
        for j in range(NT):
            with nc.Block() as blk:
                s0 = ctx.snap()

                @blk.sync
                def _(q):
                    if j > 1:
                        ctx.waitv(q, "P", s0["P"])
                    ctx.qd(q.dma_start(
                        out=ao_t[:, (j % 2) * D:(j % 2) * D + D].rearrange(
                            "p (c w) -> p c w", w=128),
                        in_=scr["aoT"].rearrange("(c p) s -> p c s", p=128)[
                            :, :, 128 * j:128 * (j + 1)]))
                    ctx.waitv(q, "V", s0["V"])
                    ctx.qd(q.dma_start(out=xtile[:],
                                       in_=xres[b, 128 * j:128 * (j + 1), :]))

                s1 = ctx.snap()

                @blk.tensor
                def _(p):
                    ctx.wait(p, s1, "Q")
                    if v_res:
                        ctx.waitv(p, "V", v_res)
                    for c in range(DCH):
                        for (o, wd) in sp512(D):
                            ctx.p(p.matmul(
                                out=wo_ps[:, o:o + wd],
                                lhsT=ao_t[:, (j % 2) * D + 128 * c:(j % 2) * D + 128 * (c + 1)],
                                rhs=wo_all[:, c * D + o:c * D + o + wd],
                                start=(c == 0), stop=(c == DCH - 1)))

                s2 = ctx.snap()

                @blk.vector
                def _(v):
                    ctx.wait(v, s2, "P", "Q")
                    ctx.v(v.scalar_tensor_tensor(out=resv[:], in0=wo_ps[:],
                                                 scalar=0.0, in1=per["bobc"][:],
                                                 op0=ALU.add, op1=ALU.add))
                    ctx.gp(v)
                    ctx.v(v.tensor_tensor(out=resv[:], in0=resv[:], in1=xtile[:],
                                          op=ALU.add))
                v_res = ctx.cnt["V"]

                s3 = ctx.snap()

                @blk.sync
                def _(q):
                    ctx.wait(q, s3, "V")
                    ctx.qd(q.dma_start(out=xres[b, 128 * j:128 * (j + 1), :],
                                       in_=resv[:]))


def _attn_heads(ctx, cfg, b, qkvT, scr, per, es):
    nc = ctx.nc
    S, D, NH, DK = cfg["S"], cfg["D"], cfg["NH"], cfg["DK"]
    NT, KCH, GQ, NG = cfg["NT"], cfg["KCH"], cfg["GQ"], cfg["NG"]
    GW = GQ * 128
    TOPK = cfg["TOPK"]
    NPR = float(S)
    DCH = D // 128
    drow, aoT = scr["drow"], scr["aoT"]

    (RM, NM, BET, LT, LTP, SLP, CA, CB, ZC, ZP, TST, RHO, TA, TB, TC, TD,
     TBS, CBS, TBG) = range(19)
    NR = 7
    cslots = list(range(19, 19 + NR))
    tslots = list(range(19 + NR, 19 + 2 * NR))
    cslotsB = list(range(19 + 2 * NR, 19 + 3 * NR))
    NSLOT = 19 + 3 * NR
    TGT = float(TOPK) - 0.5
    SSZ = S // 32
    islot = min(7, max(0, int(round(TOPK / 32.0)) - 1))
    aslot = max(0, islot - 2)
    bslot = min(7, islot + 2)
    def _zof(c):
        return 2.0 * (c - TGT) / (c + TGT)
    sden = _zof((bslot + 1) * 32.0) - _zof((aslot + 1) * 32.0)
    ZF = 4.0 / float(TOPK)

    t = lambda n, s, d: es.enter_context(nc.sbuf_tensor(ctx.name(n), s, d))
    vtm2 = t("h_vtm", [128, 2 * KCH * (DK + 1)], BF16)
    ebuf2 = t("h_e", [128, 2 * GQ * S], BF16)
    gbuf = t("h_g", [128, GQ * S], BF16)
    junk = t("h_junk", [128, S], BF16)
    scrt = t("h_scrt", [128, S], BF16)
    gTbuf = t("h_gT", [128, KCH * GW], BF16)
    candb = t("h_cand", [128, GQ * 16], F32)
    ohb = t("h_ohb", [128, GQ * 16], F32)
    kscr = t("h_kscr", [128, 8], BF16)
    smp82 = t("h_smp8", [128, 2 * 8 * GQ], BF16)
    gcol2 = t("h_gcol", [128, 2 * 40 * GQ], F32)
    invsc = t("h_invsc", [DK + 1, GW], F32)
    invbc = t("h_invbc", [64, GW], F32)
    aot_t = t("h_aot", [64, GW], BF16)
    sc_ps = es.enter_context(nc.psum_tensor(ctx.name("h_scps"), [128, S], F32))
    tr_ps2 = es.enter_context(nc.psum_tensor(ctx.name("h_trps2"), [128, S], BF16))
    pv_ps = es.enter_context(nc.psum_tensor(ctx.name("h_pvps"), [DK + 1, GW], F32))

    blk_cm = nc.Block()
    blk = blk_cm.__enter__()

    groups = [(h, gi) for h in range(NH) for gi in range(NG)]
    NGR = len(groups)

    def P(k):
        return (k % 2)

    def eb(k, tt):
        return ebuf2[:, (P(k) * GQ + tt) * S:(P(k) * GQ + tt + 1) * S]

    def gc(k, i):
        o = P(k) * 40 * GQ
        return gcol2[:, o + i * GQ:o + (i + 1) * GQ]

    def gc1(k, i, tt):
        o = P(k) * 40 * GQ
        return gcol2[:, o + i * GQ + tt:o + i * GQ + tt + 1]

    def vtm(k):
        h = groups[k][0]
        o = (h % 2) * KCH * (DK + 1)
        return vtm2[:, o:o + KCH * (DK + 1)]

    def smp8(k):
        return smp82[:, P(k) * 8 * GQ:(P(k) + 1) * 8 * GQ]

    def qkslice(k):
        h, gi = groups[k]
        ch_q = h // 2
        off = 64 * (h % 2)
        qT = qkvT[off:off + 64, ch_q * S:(ch_q + 1) * S]
        kT = qkvT[off:off + 64, (DCH + ch_q) * S:(DCH + ch_q + 1) * S]
        vT = qkvT[off:off + 64, (2 * DCH + ch_q) * S:(2 * DCH + ch_q + 1) * S]
        return qT, kT, vT

    s0 = ctx.snap()

    @blk.vector
    def _(v):
        ctx.wait(v, s0, "A", "P")
        ctx.v(v.memset(vtm2[:].rearrange("p (c w) -> p c w", w=DK + 1)[
            :, :, DK:DK + 1], 1.0))

    def head_prep(k):
        h, _ = groups[k]
        off = 64 * (h % 2)
        _, _, vT = qkslice(k)
        vt = vtm(k)
        sp0 = ctx.snap()

        @blk.tensor
        def _(p):
            ctx.wait(p, sp0, "A", "V")
            for c in range(KCH):
                ctx.p(p.transpose(out=tr_ps2[:, DK * c:DK * (c + 1)],
                                  in_=vT[:, 128 * c:128 * (c + 1)],
                                  identity=per["ident2"][off:off + 64, :]))

        sp1 = ctx.snap()

        @blk.scalar
        def _(a):
            ctx.wait(a, sp1, "P")
            ctx.a(a.copy(
                out=vt.rearrange("p (c w) -> p c w", w=DK + 1)[:, :, 0:DK],
                in_=tr_ps2[:, 0:DK * KCH].rearrange("p (c w) -> p c w",
                                                    w=DK)))

    P_QK = len(sp512(S))
    pstate = {"exp_a": 0}

    def s1_qk(k, tt):
        h, gi = groups[k]
        qT, kT, _ = qkslice(k)
        jt = gi * GQ + tt
        @blk.tensor
        def _(p):
            ctx.waitv(p, "A", pstate["exp_a"])
            for (o, wd) in sp512(S):
                ctx.p(p.matmul(out=sc_ps[:, o:o + wd],
                               lhsT=qT[:, 128 * jt:128 * (jt + 1)],
                               rhs=kT[:, o:o + wd],
                               start=True, stop=True))

    def s1_rx(k, tt):
        st1 = ctx.snap()

        @blk.vector
        def _(v):
            ctx.wait(v, st1, "P")
            ctx.v(v.tensor_reduce(out=gc1(k, RM, tt), in_=sc_ps[:, 0:S],
                                  axis=AXX, op=ALU.max))
            ctx.gp(v)
            ctx.v(v.tensor_scalar(out=gc1(k, NM, tt), in0=gc1(k, RM, tt),
                                  scalar1=-0.125, scalar2=None,
                                  op0=ALU.mult))

    def s1_exp(k, tt):
        st2 = ctx.snap()

        @blk.scalar
        def _(a):
            ctx.wait(a, st2, "V")
            ctx.a(a.activation(out=eb(k, tt), in_=sc_ps[:, 0:S],
                               func=AF.Exp, bias=gc1(k, NM, tt),
                               scale=0.125))
        pstate["exp_a"] = ctx.cnt["A"]

    def s1_smp(k):
        sm0 = ctx.snap()

        @blk.scalar
        def _(a):
            ctx.wait(a, sm0, "V")
            ctx.a(a.activation(out=gc(k, BET), in_=gc(k, NM),
                               func=AF.Exp, scale=1.0))

        sm1 = ctx.snap()

        @blk.vector
        def _(v):
            ctx.wait(v, sm1, "A")
            for tt in range(GQ):
                ctx.v(v.tensor_copy(
                    out=scrt[:, SSZ * tt:SSZ * (tt + 1)].unsqueeze(2),
                    in_=eb(k, tt).rearrange("p (a b) -> p a b",
                                            b=32)[:, :, 0:1]))
            for tt in range(GQ):
                ctx.v(v.max(out=smp8(k)[:, 8 * tt:8 * (tt + 1)],
                            in_=scrt[:, SSZ * tt:SSZ * (tt + 1)]))
            l3 = smp8(k)[:].rearrange("p (t k) -> p t k", k=8)
            ctx.gp(v)
            ctx.v(v.tensor_copy(out=gc(k, LT).unsqueeze(2),
                                in_=l3[:, :, islot:islot + 1]))
            ctx.v(v.tensor_tensor(out=gc(k, SLP).unsqueeze(2),
                                  in0=l3[:, :, aslot:aslot + 1],
                                  in1=l3[:, :, bslot:bslot + 1],
                                  op=ALU.subtract))
            ctx.v(v.memset(gc(k, CA), 0.0))
            ctx.v(v.memset(gc(k, CB), 1.01))
            ctx.v(v.memset(gc(k, ZC), 0.0))
            ctx.v(v.memset(gc(k, LTP), 0.0))
            ctx.v(v.tensor_scalar(out=gc(k, SLP), in0=gc(k, SLP),
                                  scalar1=1.0 / sden, scalar2=1e-37,
                                  op0=ALU.mult, op1=ALU.max))
            ctx.gp(v)
            ctx.v(v.tensor_copy(out=gc(k, tslots[0]), in_=gc(k, LT)))
            ctx.v(v.tensor_scalar(out=gc(k, TBG), in0=gc(k, LT),
                                  scalar1=65536.0, scalar2=None,
                                  op0=ALU.mult))

    def s2_A(k, rnd):
        sT = ctx.snap()

        @blk.scalar
        def _(a):
            ctx.wait(a, sT, "V")
            for tt in range(GQ):
                ctx.a(a.activation(
                    out=junk[:, 0:S // 2], in_=eb(k, tt)[:, 0:S // 2],
                    func=AF.Tanh,
                    bias=gc1(k, TBG, tt), scale=-65536.0,
                    accum_out=gc1(k, cslots[rnd], tt)))

    def s2_probeV(k, rnd, half=False):
        sP = ctx.snap()

        @blk.vector
        def _(v):
            if not half:
                ctx.wait(v, sP, "A")
            for tt in range(GQ):
                if half:
                    ctx.v(v.tensor_scalar(
                        out=scrt[:, 0:S // 2], in0=eb(k, tt)[:, S // 2:S],
                        scalar1=gc1(k, tslots[rnd], tt), scalar2=None,
                        op0=ALU.is_ge, op1=ALU.add,
                        accum_out=gc1(k, cslotsB[rnd], tt)))
                else:
                    ctx.v(v.tensor_scalar(
                        out=junk[:], in0=eb(k, tt),
                        scalar1=gc1(k, tslots[rnd], tt), scalar2=None,
                        op0=ALU.is_ge, op1=ALU.add,
                        accum_out=gc1(k, cslots[rnd], tt)))

    def s2_chainz(k, rnd):
        sgn = rnd < NR - 2
        sC = ctx.snap()

        @blk.vector
        def _(v):
            if sgn:
                ctx.wait(v, sC, "A")
            cs = cslots[rnd]
            ctx.gp(v)
            if sgn:
                ctx.v(v.tensor_scalar(out=gc(k, TA), in0=gc(k, cs),
                                      scalar1=-0.5, scalar2=NPR / 4.0,
                                      op0=ALU.mult, op1=ALU.add))
                ctx.v(v.tensor_copy(out=gc(k, ZP), in_=gc(k, ZC)))
                ctx.v(v.tensor_tensor(out=gc(k, TA), in0=gc(k, TA),
                                      in1=gc(k, cslotsB[rnd]), op=ALU.add))
            else:
                ctx.v(v.tensor_copy(out=gc(k, TA), in_=gc(k, cs)))
                ctx.v(v.tensor_copy(out=gc(k, ZP), in_=gc(k, ZC)))
            ctx.gp(v)
            ctx.v(v.tensor_scalar(out=gc(k, TB), in0=gc(k, TA),
                                  scalar1=TGT, scalar2=None,
                                  op0=ALU.is_ge))
            ctx.v(v.tensor_scalar(out=gc(k, TC), in0=gc(k, TA),
                                  scalar1=2.0, scalar2=-2.0 * TGT,
                                  op0=ALU.mult, op1=ALU.add))
            ctx.v(v.tensor_scalar(out=gc(k, TD), in0=gc(k, TA),
                                  scalar1=TGT, scalar2=None,
                                  op0=ALU.add))
            ctx.gp(v)
            ctx.v(v.reciprocal(out=gc(k, TD), in_=gc(k, TD)))
            ctx.v(v.tensor_scalar(out=gc(k, TA), in0=gc(k, TB),
                                  scalar1=1000.0, scalar2=-1000.0,
                                  op0=ALU.mult, op1=ALU.add))
            ctx.gp(v)
            ctx.v(v.tensor_tensor(out=gc(k, ZC), in0=gc(k, TC),
                                  in1=gc(k, TD), op=ALU.mult))
            ctx.v(v.tensor_scalar(out=gc(k, TD), in0=gc(k, TB),
                                  scalar1=1000.0, scalar2=None,
                                  op0=ALU.mult))
            ctx.v(v.tensor_tensor(out=gc(k, TA), in0=gc(k, TA),
                                  in1=gc(k, LT), op=ALU.add))
            ctx.v(v.tensor_tensor(out=gc(k, TD), in0=gc(k, TD),
                                  in1=gc(k, LT), op=ALU.add))
            ctx.v(v.tensor_tensor(out=gc(k, CA), in0=gc(k, CA),
                                  in1=gc(k, TA), op=ALU.max))
            ctx.v(v.tensor_tensor(out=gc(k, CB), in0=gc(k, CB),
                                  in1=gc(k, TD), op=ALU.min))
            if rnd > 0:
                ctx.v(v.tensor_tensor(out=gc(k, TC), in0=gc(k, LT),
                                      in1=gc(k, LTP), op=ALU.subtract))
                ctx.v(v.tensor_tensor(out=gc(k, TD), in0=gc(k, ZC),
                                      in1=gc(k, ZP), op=ALU.subtract))
                ctx.v(v.tensor_scalar(out=gc(k, TB), in0=gc(k, TC),
                                      scalar1=-1.0, scalar2=None,
                                      op0=ALU.mult))
                ctx.v(v.tensor_scalar(out=gc(k, TA), in0=gc(k, TD),
                                      scalar1=-1.0, scalar2=None,
                                      op0=ALU.mult))
                ctx.v(v.tensor_tensor(out=gc(k, TC), in0=gc(k, TC),
                                      in1=gc(k, TB), op=ALU.max))
                ctx.v(v.tensor_tensor(out=gc(k, TD), in0=gc(k, TD),
                                      in1=gc(k, TA), op=ALU.max))
                ctx.gp(v)
                ctx.v(v.tensor_scalar(out=gc(k, TD), in0=gc(k, TD),
                                      scalar1=ZF, scalar2=None,
                                      op0=ALU.max))
                ctx.gp(v)
                ctx.v(v.reciprocal(out=gc(k, TD), in_=gc(k, TD)))
                ctx.gp(v)
                ctx.v(v.tensor_tensor(out=gc(k, TC), in0=gc(k, TC),
                                      in1=gc(k, TD), op=ALU.mult))
                ctx.v(v.tensor_scalar(out=gc(k, TD), in0=gc(k, SLP),
                                      scalar1=0.3, scalar2=None,
                                      op0=ALU.mult))
                ctx.v(v.tensor_scalar(out=gc(k, TB), in0=gc(k, SLP),
                                      scalar1=4.0, scalar2=None,
                                      op0=ALU.mult))
                ctx.v(v.tensor_tensor(out=gc(k, TC), in0=gc(k, TC),
                                      in1=gc(k, TD), op=ALU.max))
                ctx.gp(v)
                ctx.v(v.tensor_tensor(out=gc(k, SLP), in0=gc(k, TC),
                                      in1=gc(k, TB), op=ALU.min))
            ctx.gp(v)
            ctx.v(v.tensor_tensor(out=gc(k, TC), in0=gc(k, ZC),
                                  in1=gc(k, SLP), op=ALU.mult))
            caprel = -0.5 if rnd == 0 else -0.95
            ctx.v(v.tensor_scalar(out=gc(k, TD), in0=gc(k, LT),
                                  scalar1=caprel, scalar2=None,
                                  op0=ALU.mult))
            ctx.gp(v)
            ctx.v(v.tensor_tensor(out=gc(k, TC), in0=gc(k, TC),
                                  in1=gc(k, TD), op=ALU.max))
            ctx.v(v.tensor_copy(out=gc(k, LTP), in_=gc(k, LT)))
            ctx.gp(v)
            ctx.v(v.tensor_tensor(out=gc(k, TD), in0=gc(k, LT),
                                  in1=gc(k, TC), op=ALU.add))
            ctx.gp(v)
            ctx.v(v.tensor_tensor(out=gc(k, TB), in0=gc(k, TD),
                                  in1=gc(k, CA), op=ALU.is_ge))
            ctx.v(v.tensor_tensor(out=gc(k, TC), in0=gc(k, TD),
                                  in1=gc(k, CB), op=ALU.is_le))
            ctx.gp(v)
            ctx.v(v.tensor_tensor(out=gc(k, TB), in0=gc(k, TB),
                                  in1=gc(k, TC), op=ALU.mult))
            ctx.v(v.tensor_tensor(out=gc(k, TC), in0=gc(k, CA),
                                  in1=gc(k, CB), op=ALU.add))
            ctx.gp(v)
            ctx.v(v.tensor_scalar(out=gc(k, TC), in0=gc(k, TC),
                                  scalar1=0.5, scalar2=None,
                                  op0=ALU.mult))
            ctx.gp(v)
            ctx.v(v.tensor_tensor(out=gc(k, TD), in0=gc(k, TD),
                                  in1=gc(k, TC), op=ALU.subtract))
            ctx.gp(v)
            ctx.v(v.tensor_tensor(out=gc(k, TD), in0=gc(k, TD),
                                  in1=gc(k, TB), op=ALU.mult))
            ctx.gp(v)
            ctx.v(v.tensor_tensor(out=gc(k, LT), in0=gc(k, TC),
                                  in1=gc(k, TD), op=ALU.add))
            ctx.gp(v)
            ctx.v(v.tensor_copy(out=gc(k, tslots[rnd + 1]), in_=gc(k, LT)))
            ctx.v(v.tensor_scalar(out=gc(k, TBG), in0=gc(k, LT),
                                  scalar1=65536.0, scalar2=None,
                                  op0=ALU.mult))

    def s3(k):
        sX = ctx.snap()

        @blk.vector
        def _(v):
            ctx.wait(v, sX, "A", "P")
            ctx.gp(v)
            ctx.v(v.tensor_scalar(out=gc(k, TC), in0=gc(k, cslots[-2]),
                                  scalar1=-(float(TOPK) - 0.5),
                                  scalar2=None, op0=ALU.add))
            ctx.v(v.tensor_scalar(out=gc(k, TD), in0=gc(k, cslots[-1]),
                                  scalar1=-(float(TOPK) - 0.5),
                                  scalar2=None, op0=ALU.add))
            ctx.v(v.tensor_scalar(out=gc(k, TA), in0=gc(k, TC),
                                  scalar1=-1.0, scalar2=None,
                                  op0=ALU.mult))
            ctx.v(v.tensor_scalar(out=gc(k, TB), in0=gc(k, TD),
                                  scalar1=-1.0, scalar2=None,
                                  op0=ALU.mult))
            ctx.v(v.tensor_tensor(out=gc(k, TC), in0=gc(k, TC),
                                  in1=gc(k, TA), op=ALU.max))
            ctx.v(v.tensor_tensor(out=gc(k, TD), in0=gc(k, TD),
                                  in1=gc(k, TB), op=ALU.max))
            ctx.gp(v)
            ctx.v(v.tensor_tensor(out=gc(k, TA), in0=gc(k, TC),
                                  in1=gc(k, TD), op=ALU.is_ge))
            ctx.v(v.tensor_tensor(out=gc(k, TB), in0=gc(k, tslots[-1]),
                                  in1=gc(k, tslots[-2]), op=ALU.subtract))
            ctx.v(v.tensor_tensor(out=gc(k, TC), in0=gc(k, cslots[-1]),
                                  in1=gc(k, cslots[-2]), op=ALU.subtract))
            ctx.v(v.tensor_tensor(out=gc(k, TB), in0=gc(k, TB),
                                  in1=gc(k, TA), op=ALU.mult))
            ctx.v(v.tensor_tensor(out=gc(k, TC), in0=gc(k, TC),
                                  in1=gc(k, TA), op=ALU.mult))
            ctx.v(v.tensor_tensor(out=gc(k, TBS), in0=gc(k, tslots[-2]),
                                  in1=gc(k, TB), op=ALU.add))
            ctx.v(v.tensor_tensor(out=gc(k, CBS), in0=gc(k, cslots[-2]),
                                  in1=gc(k, TC), op=ALU.add))
            for tt in range(GQ):
                e_t = eb(k, tt)
                sel_t = gbuf[:, tt * S:(tt + 1) * S]
                ctx.gp(v)
                ctx.v(v.tensor_scalar(out=scrt[:], in0=e_t,
                                      scalar1=gc1(k, TBS, tt),
                                      scalar2=-4.0,
                                      op0=ALU.is_lt, op1=ALU.mult))
                ctx.gp(v)
                ctx.v(v.tensor_tensor(out=sel_t, in0=scrt[:],
                                      in1=e_t, op=ALU.subtract))
                ctx.gp(v)
                ctx.v(v.max(out=kscr[:], in_=sel_t))
                ctx.gp(v)
                ctx.v(v.tensor_scalar(out=scrt[:], in0=e_t,
                                      scalar1=gc1(k, TBS, tt),
                                      scalar2=-4.0,
                                      op0=ALU.is_ge, op1=ALU.mult))
                ctx.v(v.tensor_scalar(out=candb[:, tt * 16:tt * 16 + 8],
                                      in0=kscr[:], scalar1=-1.0,
                                      scalar2=None, op0=ALU.mult))
                ctx.v(v.tensor_tensor(out=sel_t, in0=scrt[:], in1=e_t,
                                      op=ALU.add))
                ctx.gp(v)
                ctx.v(v.max(out=kscr[:], in_=sel_t))
                ctx.gp(v)
                ctx.v(v.tensor_scalar(
                    out=candb[:, tt * 16 + 8:tt * 16 + 16],
                    in0=kscr[:], scalar1=1.0, scalar2=None, op0=ALU.mult))
            ctx.gp(v)
            ctx.v(v.tensor_scalar(out=gc(k, RHO), in0=gc(k, CBS),
                                  scalar1=-float(TOPK - 1), scalar2=None,
                                  op0=ALU.add))
            ctx.gp(v)
            ctx.v(v.tensor_scalar(out=gc(k, RHO), in0=gc(k, RHO),
                                  scalar1=-7.0, scalar2=8.0,
                                  op0=ALU.max, op1=ALU.min))
            ctx.gp(v)
            iview = per["iota16"][:].unsqueeze(1).to_broadcast(
                [128, GQ, 16])
            rview = gc(k, RHO).unsqueeze(2).to_broadcast([128, GQ, 16])
            ctx.v(v.tensor_tensor(
                out=ohb[:].rearrange("p (t k) -> p t k", k=16),
                in0=iview, in1=rview, op=ALU.is_equal))
            ctx.gp(v)
            ctx.v(v.tensor_tensor(out=ohb[:], in0=ohb[:], in1=candb[:],
                                  op=ALU.mult))
            ctx.gp(v)
            ctx.v(v.tensor_reduce(
                out=gc(k, TST).unsqueeze(2),
                in_=ohb[:].rearrange("p (t k) -> p t k", k=16),
                axis=AXX, op=ALU.add))
            ctx.v(v.tensor_scalar(out=gc(k, TB), in0=gc(k, BET),
                                  scalar1=1.0, scalar2=1e-9 / NPR,
                                  op0=ALU.max, op1=ALU.mult))
            ctx.gp(v)
            ctx.v(v.tensor_tensor(out=gc(k, TA), in0=gc(k, TB),
                                  in1=gc(k, BET), op=ALU.add))
            for pp in range(0, GQ, 2):
                for tt in (pp, pp + 1):
                    ctx.v(v.tensor_scalar(
                        out=gbuf[:, tt * S:(tt + 1) * S], in0=eb(k, tt),
                        scalar1=gc1(k, TST, tt), scalar2=None,
                        op0=ALU.is_ge))
                for tt in (pp, pp + 1):
                    sc_b = scrt if tt % 2 == 0 else junk
                    ctx.v(v.tensor_scalar(out=sc_b[:], in0=eb(k, tt),
                                          scalar1=gc1(k, BET, tt),
                                          scalar2=None,
                                          op0=ALU.subtract))
                for tt in (pp, pp + 1):
                    sc_b = scrt if tt % 2 == 0 else junk
                    ctx.v(v.tensor_tensor(
                        out=gbuf[:, tt * S:(tt + 1) * S], in0=sc_b[:],
                        in1=gbuf[:, tt * S:(tt + 1) * S], op=ALU.mult))
            for tt in range(GQ):
                ctx.v(v.tensor_scalar(out=gbuf[:, tt * S:(tt + 1) * S],
                                      in0=gbuf[:, tt * S:(tt + 1) * S],
                                      scalar1=gc1(k, TA, tt),
                                      scalar2=None, op0=ALU.add))

    def s4a(k):
        h, gi = groups[k]
        v_end = ctx.cnt["V"]
        aev0 = ctx.cnt["A"]
        HKC = KCH // 2
        NHF = 2 * GQ

        @blk.tensor
        def _(p):
            for H in range(NHF):
                tt, hf = H // 2, H % 2
                if H == 0:
                    ctx.waitv(p, "V", v_end)
                    ctx.waitv(p, "A", aev0)
                elif H == 1:
                    pass
                else:
                    ctx.waitv(p, "A", aev0 + H - 1)
                for c in range(HKC):
                    cc = hf * HKC + c
                    ctx.p(p.transpose(
                        out=tr_ps2[:, (S // 2) * hf + 128 * c:
                                   (S // 2) * hf + 128 * (c + 1)],
                        in_=gbuf[:, tt * S + 128 * cc:
                                 tt * S + 128 * (cc + 1)],
                        identity=per["ident"][:]))

        pev0 = ctx.cnt["P"] - NHF * HKC

        @blk.scalar
        def _(a):
            for H in range(NHF):
                tt, hf = H // 2, H % 2
                ctx.waitv(a, "P", pev0 + (H + 1) * HKC)
                o3 = gTbuf[:].rearrange("p (c w) -> p c w", w=GW)[
                    :, hf * HKC:(hf + 1) * HKC, 128 * tt:128 * (tt + 1)]
                i3 = tr_ps2[:, (S // 2) * hf:(S // 2) * (hf + 1)].rearrange(
                    "p (c w) -> p c w", w=128)
                ctx.a(a.copy(out=o3, in_=i3))

        sJ = ctx.snap()

        @blk.tensor
        def _(p):
            ctx.wait(p, sJ, "A", "V")
            for c in range(KCH):
                ctx.p(p.matmul(
                    out=pv_ps[:],
                    lhsT=vtm(k)[:, (DK + 1) * c:(DK + 1) * (c + 1)],
                    rhs=gTbuf[:, c * GW:(c + 1) * GW],
                    start=(c == 0), stop=(c == KCH - 1)))

        sK = ctx.snap()

        @blk.scalar
        def _(a):
            ctx.wait(a, sK, "P", "Q")
            ctx.a(a.activation(out=invsc[DK:DK + 1, :],
                               in_=pv_ps[DK:DK + 1, :], func=AF.Ln,
                               bias=per["epsc"][64:65, 2:3], scale=1.0))
            ctx.ga(a)
            ctx.a(a.activation(out=invsc[DK:DK + 1, :],
                               in_=invsc[DK:DK + 1, :], func=AF.Exp,
                               scale=-1.0))

        sL2 = ctx.snap()

        @blk.sync
        def _(q):
            ctx.wait(q, sL2, "A", "V")
            ctx.qd(q.dma_start(out=drow[5, 0:GW],
                               in_=invsc[DK:DK + 1, :]))
            ctx.waitv(q, "Q", ctx.cnt["Q"])
            ctx.qd(q.dma_start(out=invbc[:],
                               in_=drow[5:6, 0:GW].partition_broadcast(64)))

    def s4b(k):
        h, gi = groups[k]
        sM = ctx.snap()

        @blk.vector
        def _(v):
            ctx.wait(v, sM, "Q")
            ctx.v(v.tensor_tensor(out=aot_t[:], in0=pv_ps[0:DK, :],
                                  in1=invbc[:], op=ALU.mult))

        sN = ctx.snap()

        @blk.sync
        def _(q):
            ctx.wait(q, sN, "V")
            ctx.qd(q.dma_start(
                out=aoT[64 * h:64 * h + 64, gi * GW:(gi + 1) * GW],
                in_=aot_t[:]))
            if scr.get("dbg") is not None and \
                    30 * GQ * (k + 1) <= 4096:
                ctx.qd(q.dma_start(
                    out=scr["dbg"][:, 30 * GQ * k:30 * GQ * (k + 1)],
                    in_=gcol2[:, P(k) * 30 * GQ:(P(k) + 1) * 30 * GQ]))

    pstate["exp_a"] = ctx.cnt["A"]

    def s1_all(k):
        if groups[k][1] == 0:
            head_prep(k)
        for tt in range(GQ):
            s1_qk(k, tt)
            s1_rx(k, tt)
            s1_exp(k, tt)
        s1_smp(k)

    s1_all(0)
    for k in range(NGR):
        nxt = k + 1 if k + 1 < NGR else None
        for rnd in range(NR - 2):
            s2_A(k, rnd)
            if nxt is not None and rnd < GQ:
                s1_qk(nxt, rnd)
            s2_probeV(k, rnd, half=True)
            if nxt is not None and rnd < GQ:
                s1_rx(nxt, rnd)
            s2_chainz(k, rnd)
            if nxt is not None and rnd < GQ:
                s1_exp(nxt, rnd)
            if rnd == 0 and k > 0:
                s4a(k - 1)
            if rnd == 1:
                if k > 0:
                    s4b(k - 1)
                if nxt is not None and groups[nxt][1] == 0:
                    head_prep(nxt)
        if nxt is not None:
            s1_smp(nxt)
        for rnd in range(NR - 2, NR):
            s2_probeV(k, rnd)
            if rnd < NR - 1:
                s2_chainz(k, rnd)
        s3(k)
        if k == NGR - 1:
            s4a(k)
            s4b(k)

    blk_cm.__exit__(None, None, None)



_PROG_CACHE = {}


def kernel(L0_attn_x, L0_attn_Wqkv, L0_attn_bqkv, L0_attn_Wo, L0_attn_bo,
           L0_ffn_w1, L0_ffn_conv_w, L0_ffn_conv_b, L0_ffn_w2):
    x = np.ascontiguousarray(np.asarray(L0_attn_x, dtype=np.float32))
    B, S, D = x.shape
    n_cores = 8
    nb = B // n_cores
    cfg = make_cfg(S=S, D=D, HF=L0_ffn_w1.shape[0], NH=16, NB=nb)

    key = (B, S, D)
    if key not in _PROG_CACHE:
        _PROG_CACHE[key] = build_program(cfg)
    nc = _PROG_CACHE[key]

    w = dict(
        Wqkv=np.ascontiguousarray(np.asarray(L0_attn_Wqkv, dtype=np.float32)),
        bqkv=np.ascontiguousarray(np.asarray(L0_attn_bqkv, dtype=np.float32)),
        Wo=np.ascontiguousarray(np.asarray(L0_attn_Wo, dtype=np.float32)),
        bo=np.ascontiguousarray(np.asarray(L0_attn_bo, dtype=np.float32)),
        w1=np.ascontiguousarray(np.asarray(L0_ffn_w1, dtype=np.float32)),
        cw=np.ascontiguousarray(
            np.asarray(L0_ffn_conv_w, dtype=np.float32).reshape(-1, 3)),
        cb=np.ascontiguousarray(np.asarray(L0_ffn_conv_b, dtype=np.float32)),
        w2=np.ascontiguousarray(np.asarray(L0_ffn_w2, dtype=np.float32)),
    )
    in_maps = []
    for c in range(n_cores):
        m = dict(w)
        m["x_in"] = x[c * nb:(c + 1) * nb]
        in_maps.append(m)

    res = run_bass_kernel_spmd(nc, in_maps, core_ids=list(range(n_cores)))
    return np.concatenate([r["out"] for r in res.results], axis=0)


# version 21
# speedup vs baseline: 1.0682x; 1.0085x over previous
import math
from contextlib import ExitStack, contextmanager

import numpy as np
import concourse.bass as bass
import concourse.mybir as mybir
from concourse.bass_utils import run_bass_kernel_spmd

F32 = mybir.dt.float32
BF16 = mybir.dt.bfloat16
I32 = mybir.dt.int32
AF = mybir.ActivationFunctionType
ALU = mybir.AluOpType
AXX = mybir.AxisListType.X

Z1 = 1.2815515655446004
PHI_Z1 = 0.17549833193248682


def make_cfg(S=2048, D=1024, HF=4096, NH=16, NB=2):
    cfg = dict(S=S, D=D, HF=HF, NH=NH, NB=NB, DK=64)
    cfg["TOPK"] = max(1, int((1.0 - 0.9) * S))
    cfg["NT"] = S // 128
    cfg["DCH"] = D // 128
    cfg["HCH"] = HF // 128
    cfg["QCH"] = 3 * D // 128
    cfg["KCH"] = S // 128
    cfg["QW"] = min(256, S)
    cfg["NQ"] = S // cfg["QW"]
    cfg["GQ"] = min(4, cfg["NT"])
    cfg["NG"] = cfg["NT"] // cfg["GQ"]
    return cfg


class Ctx:
    def __init__(self, nc, es):
        self.nc = nc
        self.sem = {}
        self.cnt = {}
        self.uid = 0
        for name in ("V", "A", "P", "G", "Q", "W"):
            self.sem[name] = es.enter_context(nc.semaphore("sem" + name))
            self.cnt[name] = 0

    def name(self, n):
        self.uid += 1
        return f"{n}_{self.uid}"

    def _mark(self, name, instr, n):
        instr.then_inc(self.sem[name], n)
        self.cnt[name] += n
        return instr

    def v(self, i):
        return self._mark("V", i, 1)

    def a(self, i):
        return self._mark("A", i, 1)

    def p(self, i):
        return self._mark("P", i, 1)

    def g(self, i):
        return self._mark("G", i, 1)

    def qd(self, i):
        return self._mark("Q", i, 16)

    def wd(self, i):
        return self._mark("W", i, 16)

    def snap(self):
        return dict(self.cnt)

    def wait(self, h, snap, *names):
        for n in names:
            if snap[n] > 0:
                h.wait_ge(self.sem[n], snap[n])

    def waitv(self, h, name, val):
        if val > 0:
            h.wait_ge(self.sem[name], val)

    def gp(self, h, name="V"):
        h.wait_ge(self.sem[name], self.cnt[name])

    def ga(self, a):
        self.a(a.activation(out=self.adum[0:1, 0:1], in_=self.adum[0:1, 1:2],
                            func=AF.Identity, scale=1.0))

    def pre6(self, a):
        if self.set6 is not None:
            a.add_instruction(mybir.InstLoadActFuncSet(
                name=self.nc.get_next_instruction_name(),
                act_func_set_id=self.set6, ins=[], outs=[]))



def emit_rint(ctx, v, cfg, out_bf, in_f32, i32scr):
    if cfg.get("rint", "cast") == "cast":
        ctx.v(v.tensor_copy(out=i32scr, in_=in_f32))
        ctx.gp(v)
        ctx.v(v.tensor_copy(out=out_bf, in_=i32scr))
    else:
        ctx.v(v.tensor_scalar(out=out_bf, in0=in_f32, scalar1=12582912.0,
                              scalar2=12582912.0, op0=ALU.add, op1=ALU.subtract))


def sp512(n):
    return [(o, min(512, n - o)) for o in range(0, n, 512)]


@contextmanager
def scope():
    es = ExitStack()
    try:
        yield es
    finally:
        es.close()



def build_program(cfg):
    nc = bass.Bass("TRN2", target_bir_lowering=False, detect_race_conditions=False)
    S, D, HF, NB = cfg["S"], cfg["D"], cfg["HF"], cfg["NB"]
    QCH, HCH, NT = cfg["QCH"], cfg["HCH"], cfg["NT"]

    dt = nc.dram_tensor
    t_in = dict(
        x_in=dt("x_in", [NB, S, D], F32, kind="ExternalInput").ap(),
        Wqkv=dt("Wqkv", [3 * D, D], F32, kind="ExternalInput").ap(),
        bqkv=dt("bqkv", [3 * D], F32, kind="ExternalInput").ap(),
        Wo=dt("Wo", [D, D], F32, kind="ExternalInput").ap(),
        bo=dt("bo", [D], F32, kind="ExternalInput").ap(),
        w1=dt("w1", [HF, D], F32, kind="ExternalInput").ap(),
        cw=dt("cw", [HF, 3], F32, kind="ExternalInput").ap(),
        cb=dt("cb", [HF], F32, kind="ExternalInput").ap(),
        w2=dt("w2", [D, HF], F32, kind="ExternalInput").ap(),
    )
    out = dt("out", [NB, S, D], F32, kind="ExternalOutput").ap()
    scr = dict(
        xres=dt("xres", [NB, S, D], F32, kind="Internal").ap(),
        wqkvT=dt("wqkvT", [D, 3 * D], BF16, kind="Internal").ap(),
        woT=dt("woT", [D, D], BF16, kind="Internal").ap(),
        w1qT=dt("w1qT", [D, HF], BF16, kind="Internal").ap(),
        w2qT=dt("w2qT", [HF, D], BF16, kind="Internal").ap(),
        aoT=dt("aoT", [D, S], BF16, kind="Internal").ap(),
        drow=dt("drow", [8, max(S, 128)], F32, kind="Internal").ap(),
        dbg=(dt("dbg", [128, 4096], F32, kind="ExternalOutput").ap()
             if cfg.get("dbg") else None),
        dsc=dt("dsc", [1, 8], F32, kind="Internal").ap(),
    )

    es = ExitStack()
    ctx = Ctx(nc, es)

    def sb(name, shape, d):
        return es.enter_context(nc.sbuf_tensor(name, shape, d))

    per = dict(
        ident=sb("ident", [128, 128], BF16),
        identf=sb("identf", [128, 128], F32),
        epsc=sb("epsc", [128, 3], F32),
        ident2=sb("ident2", [128, 64], BF16),
        onescol=sb("onescol", [128, 1], BF16),
        e65row=sb("e65row", [1, cfg["DK"] + 1], F32),
        iota16=sb("iota16", [128, 16], F32),
        bobc=sb("bobc", [128, D], F32),
        bqkvcol=sb("bqkvcol", [128, QCH], F32),
        cwcol=sb("cwcol", [128, 3 * HCH], F32),
        cbcol=sb("cbcol", [128, HCH], F32),
        wscal=sb("wscal", [128, 4], F32),
        adum=sb("adum", [1, 2], F32),
    )
    ctx.adum = per["adum"]
    ctx.set6 = None
    try:
        from concourse.hw_specs import get_activation_tables
        for _i, _fs in enumerate(get_activation_tables(nc.m.arch).values()):
            if AF.Exp in _fs and AF.Ln in _fs and AF.Sign in _fs:
                ctx.set6 = _i
                break
    except Exception:
        ctx.set6 = 6

    _init_consts(ctx, cfg, per, t_in)
    _prep_weights(ctx, cfg, t_in, scr, per)
    _copy_x(ctx, cfg, t_in["x_in"], scr["xres"])

    stages = cfg.get("stages", ("f1", "at", "f2"))
    for b in range(NB):
        if "f1" in stages:
            with nc.named_scope(f"f1_b{b}"):
                _ffn_sublayer(ctx, cfg, b, scr, per,
                              dst=(out if stages[-1] == "f1" else scr["xres"]))
        if "at" in stages:
            with nc.named_scope(f"at_b{b}"):
                _attn_sublayer(ctx, cfg, b, scr, per)
        if "f2" in stages:
            with nc.named_scope(f"f2_b{b}"):
                _ffn_sublayer(ctx, cfg, b, scr, per, dst=out)

    with nc.Block() as blk:
        fin = ctx.snap()

        @blk.gpsimd
        def _(g):
            ctx.wait(g, fin, "Q", "W", "V", "A", "P", "G")

    es.close()
    return nc



def _init_consts(ctx, cfg, per, t_in):
    nc = ctx.nc
    DK = cfg["DK"]
    with nc.Block() as blk:
        @blk.gpsimd
        def _(g):
            ctx.g(g.memset(per["adum"][:], 0.0))
            ctx.g(g.iota(per["ident"][:], pattern=[[1, 128]], base=0,
                         channel_multiplier=-1,
                         allow_small_or_imprecise_dtypes=True))
            ctx.g(g.iota(per["iota16"][:, 0:8], pattern=[[1, 8]], base=1,
                         channel_multiplier=0,
                         allow_small_or_imprecise_dtypes=True))
            ctx.g(g.iota(per["iota16"][:, 8:16], pattern=[[-1, 8]], base=0,
                         channel_multiplier=0,
                         allow_small_or_imprecise_dtypes=True))
            ctx.g(g.memset(per["onescol"][:], 1.0))
            ctx.g(g.memset(per["e65row"][:], 0.0))
            ctx.g(g.memset(per["e65row"][0:1, DK:DK + 1], 1.0))

        s0 = ctx.snap()

        @blk.vector
        def _(v):
            ctx.wait(v, s0, "G")
            ctx.v(v.tensor_scalar(out=per["ident"][:], in0=per["ident"][:],
                                  scalar1=0.0, scalar2=None, op0=ALU.is_equal))
            ctx.gp(v)
            ctx.v(v.tensor_copy(out=per["identf"][:], in_=per["ident"][:]))
            ctx.v(v.memset(per["epsc"][:, 0:1], 1e-5))
            ctx.v(v.memset(per["epsc"][:, 1:2], 1e-30))
            ctx.v(v.memset(per["epsc"][:, 2:3], 1e-37))

        s1 = ctx.snap()

        @blk.sync
        def _(q):
            ctx.wait(q, s1, "V")
            ctx.qd(q.dma_start(out=per["ident2"][0:64, :],
                               in_=per["ident"][0:64, 0:64]))
            ctx.qd(q.dma_start(out=per["ident2"][64:128, :],
                               in_=per["ident"][0:64, 0:64]))
            ctx.qd(q.dma_start(out=per["bobc"][:],
                               in_=t_in["bo"][None, :].partition_broadcast(128)))
            ctx.qd(q.dma_start(
                out=per["cwcol"][:].rearrange("p (c k) -> p c k", k=3),
                in_=t_in["cw"].rearrange("(c p) k -> p c k", p=128)))

        @blk.gpsimd
        def _(g):
            ctx.wd(g.dma_start(out=per["bqkvcol"][:],
                               in_=t_in["bqkv"].rearrange("(c p) -> p c", p=128),
                               allow_slow_non_contiguous=True))
            ctx.wd(g.dma_start(out=per["cbcol"][:],
                               in_=t_in["cb"].rearrange("(c p) -> p c", p=128),
                               allow_slow_non_contiguous=True))


def _copy_x(ctx, cfg, x_in, xres):
    nc = ctx.nc
    NB, NT, D = cfg["NB"], cfg["NT"], cfg["D"]
    with scope() as es:
        xt = es.enter_context(nc.sbuf_tensor(ctx.name("xcopy"), [128, 2 * D], F32))
        with nc.Block() as blk:
            @blk.sync
            def _(q):
                for k in range(NB * NT):
                    bb, j = divmod(k, NT)
                    buf = xt[:, (k % 2) * D:(k % 2) * D + D]
                    ctx.qd(q.dma_start(out=buf, in_=x_in[bb, 128 * j:128 * (j + 1), :]))
                    ctx.waitv(q, "Q", ctx.cnt["Q"])
                    ctx.qd(q.dma_start(out=xres[bb, 128 * j:128 * (j + 1), :], in_=buf))



def _prep_weights(ctx, cfg, t_in, scr, per):
    nc = ctx.nc
    D, HF, DCH, HCH = cfg["D"], cfg["HF"], cfg["DCH"], cfg["HCH"]
    NC = max(D, HF)

    with scope() as es:
        wtmp = es.enter_context(nc.sbuf_tensor("wtmp", [128, NC], F32))
        wint = es.enter_context(nc.sbuf_tensor("wrm", [128, NC], I32))
        wbf = es.enter_context(nc.sbuf_tensor("wbf", [128, NC], BF16))
        wtr = es.enter_context(nc.sbuf_tensor("wtr", [128, NC], BF16))
        wabs = es.enter_context(nc.sbuf_tensor("wabs", [128, max(HCH, DCH)], F32))
        wsum = es.enter_context(nc.sbuf_tensor("wsum", [128, 1], F32))
        wmisc = es.enter_context(nc.sbuf_tensor("wmisc", [1, 8], F32))
        onef = es.enter_context(nc.sbuf_tensor("onef", [128, 1], F32))
        scal_ps = es.enter_context(nc.psum_tensor("scal_ps", [1, 2], F32))
        tr_ps = es.enter_context(nc.psum_tensor("wtr_ps", [128, NC], BF16))

        with nc.Block() as blk:
            @blk.vector
            def _(v):
                ctx.v(v.memset(onef[:], 1.0))

        jobs = [("qkv", t_in["Wqkv"], scr["wqkvT"], 3 * D // 128, DCH, None),
                ("wo", t_in["Wo"], scr["woT"], DCH, DCH, None),
                ("w1", t_in["w1"], scr["w1qT"], HCH, DCH, 0),
                ("w2", t_in["w2"], scr["w2qT"], DCH, HCH, 2)]

        for (name, src, dst, nrow_ch, ncol_ch, scol) in jobs:
            ncol = ncol_ch * 128
            quant = scol is not None
            if quant:
                for r in range(nrow_ch):
                    with nc.Block() as blk:
                        s0 = ctx.snap()

                        @blk.sync
                        def _(q):
                            ctx.wait(q, s0, "V")
                            ctx.qd(q.dma_start(out=wtmp[:, 0:ncol],
                                               in_=src[128 * r:128 * (r + 1), :]))

                        s1 = ctx.snap()

                        @blk.vector
                        def _(v):
                            ctx.wait(v, s1, "Q")
                            ctx.v(v.tensor_reduce(out=wabs[:, r:r + 1],
                                                  in_=wtmp[:, 0:ncol], axis=AXX,
                                                  op=ALU.add,
                                                  apply_absolute_value=True))
                with nc.Block() as blk:
                    s0 = ctx.snap()

                    @blk.vector
                    def _(v):
                        ctx.wait(v, s0, "Q")
                        ctx.gp(v)
                        ctx.v(v.tensor_reduce(out=wsum[:], in_=wabs[:, 0:nrow_ch],
                                              axis=AXX, op=ALU.add))

                    s1 = ctx.snap()

                    @blk.tensor
                    def _(p):
                        ctx.wait(p, s1, "V")
                        ctx.p(p.matmul(out=scal_ps[0:1, 0:1], lhsT=wsum[:],
                                       rhs=onef[:], start=True, stop=True))

                    s2 = ctx.snap()

                    @blk.vector
                    def _(v):
                        ctx.wait(v, s2, "P")
                        n_el = float(nrow_ch * 128 * ncol)
                        ctx.v(v.tensor_scalar(out=wmisc[0:1, 0:1],
                                              in0=scal_ps[0:1, 0:1],
                                              scalar1=1.0 / n_el, scalar2=1e-5,
                                              op0=ALU.mult, op1=ALU.max))
                        ctx.gp(v)
                        ctx.v(v.reciprocal(out=wmisc[0:1, 1:2], in_=wmisc[0:1, 0:1]))
                        ctx.v(v.tensor_scalar(out=wmisc[0:1, 2:3], in0=wmisc[0:1, 0:1],
                                              scalar1=1.0 / 127.0, scalar2=None,
                                              op0=ALU.mult))

                    s3 = ctx.snap()

                    @blk.sync
                    def _(q):
                        ctx.wait(q, s3, "V")
                        ctx.qd(q.dma_start(out=scr["dsc"][0:1, 0:2],
                                           in_=wmisc[0:1, 1:3]))
                        ctx.waitv(q, "Q", ctx.cnt["Q"])
                        ctx.qd(q.dma_start(
                            out=per["wscal"][:, scol:scol + 2],
                            in_=scr["dsc"][0:1, 0:2].partition_broadcast(128)))
                        if scr.get("dbg") is not None:
                            ctx.waitv(q, "Q", ctx.cnt["Q"])
                            ctx.qd(q.dma_start(
                                out=scr["dbg"][:, 8 * scol:8 * scol + 4],
                                in_=per["wscal"][:, 0:4]))
                            ctx.qd(q.dma_start(
                                out=scr["dbg"][:, 20 + 40 * scol:20 + 40 * scol + nrow_ch],
                                in_=wabs[:, 0:nrow_ch]))

            for r in range(nrow_ch):
                with nc.Block() as blk:
                    s0 = ctx.snap()

                    @blk.sync
                    def _(q):
                        ctx.wait(q, s0, "V", "P")
                        ctx.qd(q.dma_start(out=wtmp[:, 0:ncol],
                                           in_=src[128 * r:128 * (r + 1), :]))

                    s1 = ctx.snap()

                    @blk.vector
                    def _(v):
                        ctx.wait(v, s1, "Q", "W")
                        if quant:
                            ctx.v(v.tensor_scalar(out=wtmp[:, 0:ncol],
                                                  in0=wtmp[:, 0:ncol],
                                                  scalar1=per["wscal"][:, scol:scol + 1],
                                                  scalar2=None, op0=ALU.mult))
                            ctx.gp(v)
                            if cfg.get("rint", "cast") == "cast":
                                ctx.v(v.tensor_copy(out=wint[:, 0:ncol],
                                                    in_=wtmp[:, 0:ncol]))
                            else:
                                ctx.v(v.tensor_scalar(
                                    out=wint[:, 0:ncol].bitcast(F32),
                                    in0=wtmp[:, 0:ncol], scalar1=12582912.0,
                                    scalar2=12582912.0, op0=ALU.add,
                                    op1=ALU.subtract))
                            wq_in = (wint[:, 0:ncol] if cfg.get("rint", "cast") == "cast"
                                     else wint[:, 0:ncol].bitcast(F32))
                            ctx.gp(v)
                            ctx.v(v.tensor_scalar(out=wbf[:, 0:ncol],
                                                  in0=wq_in,
                                                  scalar1=-1.0, scalar2=1.0,
                                                  op0=ALU.max, op1=ALU.min))
                        else:
                            ctx.v(v.tensor_copy(out=wbf[:, 0:ncol],
                                                in_=wtmp[:, 0:ncol]))

                    s2 = ctx.snap()

                    @blk.tensor
                    def _(p):
                        ctx.wait(p, s2, "V", "A")
                        for c in range(ncol_ch):
                            ctx.p(p.transpose(out=tr_ps[:, 128 * c:128 * (c + 1)],
                                              in_=wbf[:, 128 * c:128 * (c + 1)],
                                              identity=per["ident"][:]))

                    s3 = ctx.snap()

                    @blk.scalar
                    def _(a):
                        ctx.wait(a, s3, "P", "W")
                        ctx.a(a.copy(out=wtr[:, 0:ncol], in_=tr_ps[:, 0:ncol]))

                    s4 = ctx.snap()

                    @blk.gpsimd
                    def _(g):
                        ctx.wait(g, s4, "A")
                        for c in range(ncol_ch):
                            ctx.wd(g.dma_start(
                                out=dst[128 * c:128 * (c + 1), 128 * r:128 * (r + 1)],
                                in_=wtr[:, 128 * c:128 * (c + 1)]))



def _ffn_sublayer(ctx, cfg, b, scr, per, dst):
    nc = ctx.nc
    S, D, HF = cfg["S"], cfg["D"], cfg["HF"]
    NT, DCH, HCH = cfg["NT"], cfg["DCH"], cfg["HCH"]
    QW, NQ = cfg["QW"], cfg["NQ"]
    SP = S + 2
    xres = scr["xres"]
    drow = scr["drow"]

    with scope() as es:
        t = lambda n, s, d: es.enter_context(nc.sbuf_tensor(ctx.name(n), s, d))
        xtile = t("f_x", [128, 2 * D], F32)
        xh = t("f_xh", [128, D], F32)
        xint = t("f_xrm", [128, D], I32)
        xqtm = t("f_xqtm", [128, D], BF16)
        st6 = t("f_st6", [128, 24], F32)
        st2 = t("f_st2", [128, 2], F32)
        tc = t("f_tc", [128, 12], F32)
        inv1col = t("f_inv1c", [128, NT], F32)
        inv2col = t("f_inv2c", [128, NT], F32)
        s2tok = t("f_s2tok", [128, NT], F32)
        xqT = t("f_xqT", [128, DCH * SP], BF16)
        w1all = t("f_w1all", [128, DCH * HF], BF16)
        w2s = t("f_w2s", [128, 2 * D], BF16)
        inv1bc = t("f_inv1bc", [128, SP], F32)
        s2bc = t("f_s2bc", [128, QW], F32)
        ubuf = t("f_ubuf", [128, 2 * (QW + 2)], F32)
        cbuf = t("f_cbuf", [128, 2 * QW], F32)
        hbuf = t("f_hbuf", [128, HCH * QW], F32)
        hmax = t("f_hmax", [128, QW], F32)
        hmin = t("f_hmin", [128, QW], F32)
        hqbuf = t("f_hq", [128, HCH * QW], BF16)
        qtmp = t("f_qtmp", [128, QW], F32)
        hint = t("f_rmq", [128, QW], I32)
        resv = t("f_resv", [128, D], F32)
        h_ps = es.enter_context(nc.psum_tensor(ctx.name("f_hps"), [128, 1024], F32))
        tr_ps = es.enter_context(nc.psum_tensor(ctx.name("f_trps"), [128, D], BF16))
        trf_ps = es.enter_context(nc.psum_tensor(ctx.name("f_trfps"), [128, 2 * 128], F32))
        y2_psl = [es.enter_context(nc.psum_tensor(ctx.name("f_y2ps"), [128, D], F32))
                  for _ in range(QW // 128)]

        ENTRY = ctx.snap()

        with nc.Block() as blk:
            @blk.sync
            def _(q):
                ctx.wait(q, ENTRY, "V", "A", "P", "G", "W")
                ctx.qd(q.dma_start(
                    out=w1all[:].rearrange("p (c h) -> p c h", c=DCH),
                    in_=scr["w1qT"].rearrange("(c p) h -> p c h", p=128)))

            @blk.vector
            def _(v):
                ctx.wait(v, ENTRY, "Q", "A", "P", "G", "W")
                xqT3 = xqT[:].rearrange("p (c s) -> p c s", s=SP)
                ctx.v(v.memset(xqT3[:, :, 0:1], 0.0))
                ctx.v(v.memset(xqT3[:, :, SP - 1:SP], 0.0))
                ctx.v(v.memset(inv1bc[:, 0:1], 0.0))
                ctx.v(v.memset(inv1bc[:, SP - 1:SP], 0.0))

        a_evac = None
        with nc.Block() as blk:
            for j in range(NT):
                s0 = ctx.snap()

                @blk.sync
                def _(q):
                    ctx.wait(q, s0, "V")
                    ctx.qd(q.dma_start(out=xtile[:, (j % 2) * D:(j % 2) * D + D],
                                       in_=xres[b, 128 * j:128 * (j + 1), :]))

                s1 = ctx.snap()
                xt = xtile[:, (j % 2) * D:(j % 2) * D + D]

                @blk.vector
                def _(v):
                    ctx.wait(v, s1, "Q")
                    for ii, (o, wd) in enumerate(sp512(D)):
                        ctx.v(v.bn_stats(out=st6[:, 6 * ii:6 * (ii + 1)],
                                         in_=xt[:, o:o + wd]))
                    ctx.gp(v)
                    ctx.v(v.bn_aggr(out=st2[:], in_=st6[:, 0:6 * len(sp512(D))]))

                s2 = ctx.snap()

                @blk.scalar
                def _(a):
                    ctx.wait(a, s2, "V")
                    ctx.a(a.activation(out=tc[:, 0:1], in_=st2[:, 1:2], func=AF.Ln,
                                       bias=per["epsc"][:, 0:1], scale=1.0))
                    ctx.ga(a)
                    ctx.a(a.activation(out=tc[:, 1:2], in_=tc[:, 0:1], func=AF.Exp,
                                       scale=-0.5))

                s3 = ctx.snap()

                @blk.vector
                def _(v):
                    ctx.wait(v, s3, "A")
                    ctx.v(v.tensor_scalar(out=xh[:], in0=xt,
                                          scalar1=st2[:, 0:1], scalar2=tc[:, 1:2],
                                          op0=ALU.subtract, op1=ALU.mult))
                    ctx.gp(v)
                    ctx.v(v.tensor_reduce(out=tc[:, 2:3], in_=xh[:], axis=AXX,
                                          op=ALU.max, apply_absolute_value=True))
                    ctx.gp(v)
                    ctx.v(v.tensor_scalar(out=tc[:, 3:4], in0=tc[:, 2:3],
                                          scalar1=1e-5, scalar2=None, op0=ALU.max))
                    ctx.gp(v)
                    ctx.v(v.reciprocal(out=tc[:, 4:5], in_=tc[:, 3:4]))
                    ctx.gp(v)
                    ctx.v(v.tensor_scalar(out=tc[:, 5:6], in0=tc[:, 4:5],
                                          scalar1=127.0, scalar2=None, op0=ALU.mult))
                    ctx.v(v.tensor_scalar(out=inv1col[:, j:j + 1], in0=tc[:, 3:4],
                                          scalar1=per["wscal"][:, 1:2], scalar2=None,
                                          op0=ALU.mult))
                    ctx.v(v.tensor_scalar(out=xh[:], in0=xh[:],
                                          scalar1=tc[:, 5:6], scalar2=None,
                                          op0=ALU.mult))
                    ctx.gp(v)
                    emit_rint(ctx, v, cfg, xqtm[:], xh[:], xint[:])

                s4 = ctx.snap()

                @blk.tensor
                def _(p):
                    ctx.wait(p, s4, "V")
                    if a_evac:
                        ctx.waitv(p, "A", a_evac)
                    for c in range(DCH):
                        ctx.p(p.transpose(out=tr_ps[:, 128 * c:128 * (c + 1)],
                                          in_=xqtm[:, 128 * c:128 * (c + 1)],
                                          identity=per["ident"][:]))

                s5 = ctx.snap()

                @blk.scalar
                def _(a):
                    ctx.wait(a, s5, "P")
                    o3 = xqT[:].rearrange("p (c s) -> p c s", s=SP)[
                        :, :, 1 + 128 * j:1 + 128 * (j + 1)]
                    i3 = tr_ps[:].rearrange("p (c w) -> p c w", w=128)
                    ctx.a(a.copy(out=o3, in_=i3))
                a_evac = ctx.cnt["A"]

        with nc.Block() as blk:
            s0 = ctx.snap()

            @blk.gpsimd
            def _(g):
                ctx.wait(g, s0, "V")
                ctx.wd(g.dma_start(out=drow[0, 0:S].rearrange("(t p) -> p t", p=128),
                                   in_=inv1col[:, 0:NT],
                                   allow_slow_non_contiguous=True))

            s1 = ctx.snap()

            @blk.sync
            def _(q):
                ctx.wait(q, s1, "W")
                ctx.qd(q.dma_start(out=inv1bc[:, 1:S + 1],
                                   in_=drow[0:1, 0:S].partition_broadcast(128)))

        for qw in range(NQ):
            base = qw * QW
            with nc.Block() as blk:
                s0 = ctx.snap()

                @blk.vector
                def _(v):
                    ctx.wait(v, s0, "Q", "P")
                    ctx.v(v.memset(hmax[:], -3.0e38))
                    ctx.v(v.memset(hmin[:], 3.0e38))

            with nc.Block() as blk:
                ent = ctx.snap()
                p0, v0, a0, q0 = ent["P"], ent["V"], ent["A"], ent["Q"]
                P_PER_M = DCH
                V_PER_M = 6
                A_PER_M = 1
                Q_PER_M = 16

                @blk.tensor
                def _(p):
                    for m in range(HCH):
                        if m == 0:
                            ctx.wait(p, ent, "V", "A")
                        if m >= 2:
                            ctx.waitv(p, "V", v0 + (m - 2) * V_PER_M + 1)
                        hp = h_ps[:, (m % 2) * 512:(m % 2) * 512 + QW + 2]
                        for c in range(DCH):
                            ctx.p(p.matmul(
                                out=hp,
                                lhsT=w1all[:, c * HF + 128 * m:c * HF + 128 * (m + 1)],
                                rhs=xqT[:, c * SP + base:c * SP + base + QW + 2],
                                start=(c == 0), stop=(c == DCH - 1)))

                @blk.vector
                def _(v):
                    for m in range(HCH):
                        ctx.waitv(v, "P", p0 + (m + 1) * P_PER_M)
                        if m >= 2:
                            ctx.waitv(v, "A", a0 + (m - 1) * A_PER_M)
                        hp = h_ps[:, (m % 2) * 512:(m % 2) * 512 + QW + 2]
                        ub = ubuf[:, (m % 2) * (QW + 2):(m % 2) * (QW + 2) + QW + 2]
                        cb2 = cbuf[:, (m % 2) * QW:(m % 2) * QW + QW]
                        ctx.v(v.tensor_tensor(out=ub, in0=hp,
                                              in1=inv1bc[:, base:base + QW + 2],
                                              op=ALU.mult))
                        ctx.gp(v)
                        ctx.v(v.tensor_scalar(out=cb2, in0=ub[:, 0:QW],
                                              scalar1=per["cwcol"][:, 3 * m:3 * m + 1],
                                              scalar2=None, op0=ALU.mult))
                        ctx.gp(v)
                        ctx.v(v.scalar_tensor_tensor(
                            out=cb2, in0=ub[:, 1:QW + 1],
                            scalar=per["cwcol"][:, 3 * m + 1:3 * m + 2],
                            in1=cb2, op0=ALU.mult, op1=ALU.add))
                        ctx.gp(v)
                        ctx.v(v.scalar_tensor_tensor(
                            out=cb2, in0=ub[:, 2:QW + 2],
                            scalar=per["cwcol"][:, 3 * m + 2:3 * m + 3],
                            in1=cb2, op0=ALU.mult, op1=ALU.add))
                        ctx.waitv(v, "A", a0 + (m + 1) * A_PER_M)
                        hb = hbuf[:, m * QW:(m + 1) * QW]
                        ctx.v(v.tensor_tensor(out=hmax[:], in0=hmax[:], in1=hb,
                                              op=ALU.max))
                        ctx.v(v.tensor_tensor(out=hmin[:], in0=hmin[:], in1=hb,
                                              op=ALU.min))

                @blk.scalar
                def _(a):
                    for m in range(HCH):
                        ctx.waitv(a, "V", v0 + m * V_PER_M + 4)
                        cb2 = cbuf[:, (m % 2) * QW:(m % 2) * QW + QW]
                        ctx.a(a.activation(out=hbuf[:, m * QW:(m + 1) * QW], in_=cb2,
                                           func=AF.Silu,
                                           bias=per["cbcol"][:, m:m + 1], scale=1.0))

            with nc.Block() as blk:
                s0 = ctx.snap()

                @blk.vector
                def _(v):
                    ctx.v(v.tensor_scalar(out=hmin[:], in0=hmin[:], scalar1=-1.0,
                                          scalar2=None, op0=ALU.mult))
                    ctx.gp(v)
                    ctx.v(v.tensor_tensor(out=hmax[:], in0=hmax[:], in1=hmin[:],
                                          op=ALU.max))

                s1 = ctx.snap()

                @blk.tensor
                def _(p):
                    ctx.wait(p, s1, "V")
                    for w in range(QW // 128):
                        ctx.p(p.transpose(out=trf_ps[:, 128 * w:128 * (w + 1)],
                                          in_=hmax[:, 128 * w:128 * (w + 1)],
                                          identity=per["identf"][:]))

                s2 = ctx.snap()

                @blk.vector
                def _(v):
                    ctx.wait(v, s2, "P")
                    nw = QW // 128
                    for w in range(nw):
                        jt = (base // 128) + w
                        ctx.v(v.tensor_reduce(out=s2tok[:, jt:jt + 1],
                                              in_=trf_ps[:, 128 * w:128 * (w + 1)],
                                              axis=AXX, op=ALU.max))
                        ctx.gp(v)
                        ctx.v(v.tensor_scalar(out=s2tok[:, jt:jt + 1],
                                              in0=s2tok[:, jt:jt + 1],
                                              scalar1=1e-5, scalar2=None,
                                              op0=ALU.max))
                        ctx.gp(v)
                        ctx.v(v.tensor_scalar(out=inv2col[:, jt:jt + 1],
                                              in0=s2tok[:, jt:jt + 1],
                                              scalar1=per["wscal"][:, 3:4],
                                              scalar2=0.5, op0=ALU.mult,
                                              op1=ALU.mult))
                        ctx.v(v.reciprocal(out=tc[:, 6:7], in_=s2tok[:, jt:jt + 1]))
                        ctx.gp(v)
                        ctx.v(v.tensor_scalar(out=s2tok[:, jt:jt + 1], in0=tc[:, 6:7],
                                              scalar1=127.0, scalar2=None,
                                              op0=ALU.mult))

                s3 = ctx.snap()

                @blk.gpsimd
                def _(g):
                    ctx.wait(g, s3, "V")
                    nw = QW // 128
                    jt = base // 128
                    ctx.wd(g.dma_start(
                        out=drow[1, base:base + QW].rearrange("(t p) -> p t", p=128),
                        in_=s2tok[:, jt:jt + nw],
                        allow_slow_non_contiguous=True))

                s4 = ctx.snap()

                @blk.sync
                def _(q):
                    ctx.wait(q, s4, "W")
                    ctx.qd(q.dma_start(
                        out=s2bc[:],
                        in_=drow[1:2, base:base + QW].partition_broadcast(128)))

            with nc.Block() as blk:
                s0 = ctx.snap()
                vq0 = ctx.cnt["V"]

                @blk.vector
                def _(v):
                    ctx.wait(v, s0, "Q")
                    for m in range(HCH):
                        hb = hbuf[:, m * QW:(m + 1) * QW]
                        ctx.v(v.tensor_tensor(out=qtmp[:], in0=hb, in1=s2bc[:],
                                              op=ALU.mult))
                        ctx.gp(v)
                        emit_rint(ctx, v, cfg, hqbuf[:, m * QW:(m + 1) * QW],
                                  qtmp[:], hint[:])

            NW = QW // 128
            with nc.Block() as blk:
                ent2 = ctx.snap()
                q2, p2 = ent2["Q"], ent2["P"]
                P_PER_M = NW * len(sp512(D))

                @blk.sync
                def _(q):
                    for m in range(HCH):
                        if m >= 2:
                            ctx.waitv(q, "P", p2 + (m - 1) * P_PER_M)
                        ctx.qd(q.dma_start(
                            out=w2s[:, (m % 2) * D:(m % 2) * D + D],
                            in_=scr["w2qT"][128 * m:128 * (m + 1), :]))

                V_QPM = 3 if cfg.get("rint", "cast") == "cast" else 2

                @blk.tensor
                def _(p):
                    for m in range(HCH):
                        ctx.waitv(p, "V", vq0 + (m + 1) * V_QPM)
                        ctx.waitv(p, "Q", q2 + 16 * (m + 1))
                        for w in range(NW):
                            for (o, wd) in sp512(D):
                                ctx.p(p.matmul(
                                    out=y2_psl[w][:, o:o + wd],
                                    lhsT=hqbuf[:, m * QW + 128 * w:m * QW + 128 * (w + 1)],
                                    rhs=w2s[:, (m % 2) * D + o:(m % 2) * D + o + wd],
                                    start=(m == 0), stop=(m == HCH - 1)))

            for w in range(NW):
                jt = (base // 128) + w
                with nc.Block() as blk:
                    s0 = ctx.snap()

                    @blk.sync
                    def _(q):
                        ctx.wait(q, s0, "V")
                        ctx.qd(q.dma_start(out=xtile[:, 0:D],
                                           in_=xres[b, 128 * jt:128 * (jt + 1), :]))

                    s2 = ctx.snap()

                    @blk.vector
                    def _(v):
                        ctx.wait(v, s2, "P", "Q")
                        ctx.v(v.scalar_tensor_tensor(
                            out=resv[:], in0=y2_psl[w][:, 0:D],
                            scalar=inv2col[:, jt:jt + 1], in1=xtile[:, 0:D],
                            op0=ALU.mult, op1=ALU.add))

                    s3 = ctx.snap()

                    @blk.sync
                    def _(q):
                        ctx.wait(q, s3, "V")
                        ctx.qd(q.dma_start(out=dst[b, 128 * jt:128 * (jt + 1), :],
                                           in_=resv[:]))



def _attn_sublayer(ctx, cfg, b, scr, per):
    nc = ctx.nc
    S, D = cfg["S"], cfg["D"]
    NT, DCH, QCH = cfg["NT"], cfg["DCH"], cfg["QCH"]
    xres = scr["xres"]

    with scope() as es_q:
        qkvT = es_q.enter_context(nc.sbuf_tensor(ctx.name("a_qkvT"), [128, QCH * S], BF16))

        with scope() as es:
            t = lambda n, s, d: es.enter_context(nc.sbuf_tensor(ctx.name(n), s, d))
            xtile = t("a_x", [128, 2 * D], F32)
            xh = t("a_xh", [128, D], BF16)
            st6 = t("a_st6", [128, 24], F32)
            st2 = t("a_st2", [128, 2], F32)
            tc = t("a_tc", [128, 4], F32)
            xhT = t("a_xhT", [128, DCH * S], BF16)
            wq_all = t("a_wq", [128, DCH * 3 * D], BF16)
            tr_ps = es.enter_context(nc.psum_tensor(ctx.name("a_trps"), [128, D], BF16))
            qkv_ps = es.enter_context(nc.psum_tensor(ctx.name("a_qkvps"), [128, S], F32))

            ENTRY = ctx.snap()
            with nc.Block() as blk:
                @blk.sync
                def _(q):
                    ctx.wait(q, ENTRY, "V", "A", "P", "G", "W")
                    ctx.qd(q.dma_start(
                        out=wq_all[:].rearrange("p (c h) -> p c h", c=DCH),
                        in_=scr["wqkvT"].rearrange("(c p) h -> p c h", p=128)))

            a_evac = None
            with nc.Block() as blk:
                for j in range(NT):
                    s0 = ctx.snap()

                    @blk.sync
                    def _(q):
                        ctx.wait(q, s0, "V")
                        ctx.qd(q.dma_start(out=xtile[:, (j % 2) * D:(j % 2) * D + D],
                                           in_=xres[b, 128 * j:128 * (j + 1), :]))

                    s1 = ctx.snap()
                    xt = xtile[:, (j % 2) * D:(j % 2) * D + D]

                    @blk.vector
                    def _(v):
                        ctx.wait(v, s1, "Q")
                        for ii, (o, wd) in enumerate(sp512(D)):
                            ctx.v(v.bn_stats(out=st6[:, 6 * ii:6 * (ii + 1)],
                                             in_=xt[:, o:o + wd]))
                        ctx.gp(v)
                        ctx.v(v.bn_aggr(out=st2[:], in_=st6[:, 0:6 * len(sp512(D))]))

                    s2 = ctx.snap()

                    @blk.scalar
                    def _(a):
                        ctx.wait(a, s2, "V")
                        ctx.a(a.activation(out=tc[:, 0:1], in_=st2[:, 1:2],
                                           func=AF.Ln, bias=per["epsc"][:, 0:1],
                                           scale=1.0))
                        ctx.ga(a)
                        ctx.a(a.activation(out=tc[:, 1:2], in_=tc[:, 0:1],
                                           func=AF.Exp, scale=-0.5))

                    s3 = ctx.snap()

                    @blk.vector
                    def _(v):
                        ctx.wait(v, s3, "A")
                        ctx.v(v.tensor_scalar(out=xh[:], in0=xt,
                                              scalar1=st2[:, 0:1], scalar2=tc[:, 1:2],
                                              op0=ALU.subtract, op1=ALU.mult))

                    s4 = ctx.snap()

                    @blk.tensor
                    def _(p):
                        ctx.wait(p, s4, "V")
                        if a_evac:
                            ctx.waitv(p, "A", a_evac)
                        for c in range(DCH):
                            ctx.p(p.transpose(out=tr_ps[:, 128 * c:128 * (c + 1)],
                                              in_=xh[:, 128 * c:128 * (c + 1)],
                                              identity=per["ident"][:]))

                    s5 = ctx.snap()

                    @blk.scalar
                    def _(a):
                        ctx.wait(a, s5, "P")
                        o3 = xhT[:].rearrange("p (c s) -> p c s", s=S)[
                            :, :, 128 * j:128 * (j + 1)]
                        i3 = tr_ps[:].rearrange("p (c w) -> p c w", w=128)
                        ctx.a(a.copy(out=o3, in_=i3))
                    a_evac = ctx.cnt["A"]

            with nc.Block() as blk:
                for r in range(QCH):
                    s0 = ctx.snap()

                    @blk.tensor
                    def _(p):
                        ctx.wait(p, s0, "Q", "A")
                        for c in range(DCH):
                            for (o, wd) in sp512(S):
                                ctx.p(p.matmul(
                                    out=qkv_ps[:, o:o + wd],
                                    lhsT=wq_all[:, c * 3 * D + 128 * r:c * 3 * D + 128 * (r + 1)],
                                    rhs=xhT[:, c * S + o:c * S + o + wd],
                                    start=(c == 0), stop=(c == DCH - 1)))

                    s1 = ctx.snap()

                    @blk.scalar
                    def _(a):
                        ctx.wait(a, s1, "P")
                        ctx.a(a.activation(out=qkvT[:, r * S:(r + 1) * S],
                                           in_=qkv_ps[:, 0:S], func=AF.Identity,
                                           bias=per["bqkvcol"][:, r:r + 1],
                                           scale=1.0))

        with scope() as es:
            _attn_heads(ctx, cfg, b, qkvT, scr, per, es)

    with scope() as es:
        t = lambda n, s, d: es.enter_context(nc.sbuf_tensor(ctx.name(n), s, d))
        wo_all = t("o_wo", [128, DCH * D], BF16)
        ao_t = t("o_ao", [128, 2 * D], BF16)
        xtile = t("o_x", [128, D], F32)
        resv = t("o_res", [128, D], F32)
        wo_ps = es.enter_context(nc.psum_tensor(ctx.name("o_wops"), [128, D], F32))

        with nc.Block() as blk:
            s0 = ctx.snap()

            @blk.sync
            def _(q):
                ctx.wait(q, s0, "V", "A", "P", "W")
                ctx.qd(q.dma_start(
                    out=wo_all[:].rearrange("p (c h) -> p c h", c=DCH),
                    in_=scr["woT"].rearrange("(c p) h -> p c h", p=128)))

        v_res = None
        with nc.Block() as blk:
            for j in range(NT):
                s0 = ctx.snap()

                @blk.sync
                def _(q):
                    if j > 1:
                        ctx.waitv(q, "P", s0["P"])
                    ctx.qd(q.dma_start(
                        out=ao_t[:, (j % 2) * D:(j % 2) * D + D].rearrange(
                            "p (c w) -> p c w", w=128),
                        in_=scr["aoT"].rearrange("(c p) s -> p c s", p=128)[
                            :, :, 128 * j:128 * (j + 1)]))
                    ctx.waitv(q, "V", s0["V"])
                    ctx.qd(q.dma_start(out=xtile[:],
                                       in_=xres[b, 128 * j:128 * (j + 1), :]))

                s1 = ctx.snap()

                @blk.tensor
                def _(p):
                    ctx.wait(p, s1, "Q")
                    if v_res:
                        ctx.waitv(p, "V", v_res)
                    for c in range(DCH):
                        for (o, wd) in sp512(D):
                            ctx.p(p.matmul(
                                out=wo_ps[:, o:o + wd],
                                lhsT=ao_t[:, (j % 2) * D + 128 * c:(j % 2) * D + 128 * (c + 1)],
                                rhs=wo_all[:, c * D + o:c * D + o + wd],
                                start=(c == 0), stop=(c == DCH - 1)))

                s2 = ctx.snap()

                @blk.vector
                def _(v):
                    ctx.wait(v, s2, "P", "Q")
                    ctx.v(v.scalar_tensor_tensor(out=resv[:], in0=wo_ps[:],
                                                 scalar=0.0, in1=per["bobc"][:],
                                                 op0=ALU.add, op1=ALU.add))
                    ctx.gp(v)
                    ctx.v(v.tensor_tensor(out=resv[:], in0=resv[:], in1=xtile[:],
                                          op=ALU.add))
                v_res = ctx.cnt["V"]

                s3 = ctx.snap()

                @blk.sync
                def _(q):
                    ctx.wait(q, s3, "V")
                    ctx.qd(q.dma_start(out=xres[b, 128 * j:128 * (j + 1), :],
                                       in_=resv[:]))


def _attn_heads(ctx, cfg, b, qkvT, scr, per, es):
    nc = ctx.nc
    S, D, NH, DK = cfg["S"], cfg["D"], cfg["NH"], cfg["DK"]
    NT, KCH, GQ, NG = cfg["NT"], cfg["KCH"], cfg["GQ"], cfg["NG"]
    GW = GQ * 128
    TOPK = cfg["TOPK"]
    NPR = float(S)
    DCH = D // 128
    drow, aoT = scr["drow"], scr["aoT"]

    (RM, NM, BET, LT, LTP, SLP, CA, CB, ZC, ZP, TST, RHO, TA, TB, TC, TD,
     TBS, CBS, TBG) = range(19)
    NR = 7
    cslots = list(range(19, 19 + NR))
    tslots = list(range(19 + NR, 19 + 2 * NR))
    cslotsB = list(range(19 + 2 * NR, 19 + 3 * NR))
    NSLOT = 19 + 3 * NR
    TGT = float(TOPK) - 0.5
    SSZ = S // 32
    islot = min(7, max(0, int(round(TOPK / 32.0)) - 1))
    aslot = max(0, islot - 2)
    bslot = min(7, islot + 2)
    def _zof(c):
        return 2.0 * (c - TGT) / (c + TGT)
    sden = _zof((bslot + 1) * 32.0) - _zof((aslot + 1) * 32.0)
    ZF = 4.0 / float(TOPK)

    t = lambda n, s, d: es.enter_context(nc.sbuf_tensor(ctx.name(n), s, d))
    vtm2 = t("h_vtm", [128, 2 * KCH * (DK + 1)], BF16)
    ebuf2 = t("h_e", [128, 2 * GQ * S], BF16)
    gbuf = t("h_g", [128, GQ * S], BF16)
    junk = t("h_junk", [128, S], BF16)
    scrt = t("h_scrt", [128, S], BF16)
    gTbuf = t("h_gT", [128, KCH * GW], BF16)
    candb = t("h_cand", [128, GQ * 16], F32)
    ohb = t("h_ohb", [128, GQ * 16], F32)
    kscr = t("h_kscr", [128, 8], BF16)
    smp82 = t("h_smp8", [128, 2 * 8 * GQ], BF16)
    gcol2 = t("h_gcol", [128, 2 * 40 * GQ], F32)
    invsc = t("h_invsc", [DK + 1, GW], F32)
    invbc = t("h_invbc", [64, GW], F32)
    aot_t = t("h_aot", [64, GW], BF16)
    sc_ps = es.enter_context(nc.psum_tensor(ctx.name("h_scps"), [128, S], F32))
    tr_ps2 = es.enter_context(nc.psum_tensor(ctx.name("h_trps2"), [128, S], BF16))
    pv_ps = es.enter_context(nc.psum_tensor(ctx.name("h_pvps"), [DK + 1, GW], F32))

    blk_cm = nc.Block()
    blk = blk_cm.__enter__()

    groups = [(h, gi) for h in range(NH) for gi in range(NG)]
    NGR = len(groups)

    def P(k):
        return (k % 2)

    def eb(k, tt):
        return ebuf2[:, (P(k) * GQ + tt) * S:(P(k) * GQ + tt + 1) * S]

    def gc(k, i):
        o = P(k) * 40 * GQ
        return gcol2[:, o + i * GQ:o + (i + 1) * GQ]

    def gc1(k, i, tt):
        o = P(k) * 40 * GQ
        return gcol2[:, o + i * GQ + tt:o + i * GQ + tt + 1]

    def vtm(k):
        h = groups[k][0]
        o = (h % 2) * KCH * (DK + 1)
        return vtm2[:, o:o + KCH * (DK + 1)]

    def smp8(k):
        return smp82[:, P(k) * 8 * GQ:(P(k) + 1) * 8 * GQ]

    def qkslice(k):
        h, gi = groups[k]
        ch_q = h // 2
        off = 64 * (h % 2)
        qT = qkvT[off:off + 64, ch_q * S:(ch_q + 1) * S]
        kT = qkvT[off:off + 64, (DCH + ch_q) * S:(DCH + ch_q + 1) * S]
        vT = qkvT[off:off + 64, (2 * DCH + ch_q) * S:(2 * DCH + ch_q + 1) * S]
        return qT, kT, vT

    s0 = ctx.snap()

    @blk.vector
    def _(v):
        ctx.wait(v, s0, "A", "P")
        ctx.v(v.memset(vtm2[:].rearrange("p (c w) -> p c w", w=DK + 1)[
            :, :, DK:DK + 1], 1.0))

    def head_prep(k):
        h, _ = groups[k]
        off = 64 * (h % 2)
        _, _, vT = qkslice(k)
        vt = vtm(k)
        sp0 = ctx.snap()

        @blk.tensor
        def _(p):
            ctx.wait(p, sp0, "A", "V")
            for c in range(KCH):
                ctx.p(p.transpose(out=tr_ps2[:, DK * c:DK * (c + 1)],
                                  in_=vT[:, 128 * c:128 * (c + 1)],
                                  identity=per["ident2"][off:off + 64, :]))

        sp1 = ctx.snap()

        @blk.scalar
        def _(a):
            ctx.wait(a, sp1, "P")
            ctx.a(a.copy(
                out=vt.rearrange("p (c w) -> p c w", w=DK + 1)[:, :, 0:DK],
                in_=tr_ps2[:, 0:DK * KCH].rearrange("p (c w) -> p c w",
                                                    w=DK)))

    P_QK = len(sp512(S))
    pstate = {"exp_a": 0}

    def s1_qk(k, tt):
        h, gi = groups[k]
        qT, kT, _ = qkslice(k)
        jt = gi * GQ + tt
        @blk.tensor
        def _(p):
            ctx.waitv(p, "A", pstate["exp_a"])
            for (o, wd) in sp512(S):
                ctx.p(p.matmul(out=sc_ps[:, o:o + wd],
                               lhsT=qT[:, 128 * jt:128 * (jt + 1)],
                               rhs=kT[:, o:o + wd],
                               start=True, stop=True))

    def s1_rx(k, tt):
        st1 = ctx.snap()

        @blk.vector
        def _(v):
            ctx.wait(v, st1, "P")
            ctx.v(v.tensor_reduce(out=gc1(k, RM, tt), in_=sc_ps[:, 0:S],
                                  axis=AXX, op=ALU.max))
            ctx.gp(v)
            ctx.v(v.tensor_scalar(out=gc1(k, NM, tt), in0=gc1(k, RM, tt),
                                  scalar1=-0.125, scalar2=None,
                                  op0=ALU.mult))

    def s1_exp(k, tt):
        st2 = ctx.snap()

        @blk.scalar
        def _(a):
            ctx.wait(a, st2, "V")
            ctx.a(a.activation(out=eb(k, tt), in_=sc_ps[:, 0:S],
                               func=AF.Exp, bias=gc1(k, NM, tt),
                               scale=0.125))
        pstate["exp_a"] = ctx.cnt["A"]

    def s1_smp(k):
        sm0 = ctx.snap()

        @blk.scalar
        def _(a):
            ctx.wait(a, sm0, "V")
            ctx.a(a.activation(out=gc(k, BET), in_=gc(k, NM),
                               func=AF.Exp, scale=1.0))

        sm1 = ctx.snap()

        @blk.vector
        def _(v):
            ctx.wait(v, sm1, "A")
            for tt in range(GQ):
                ctx.v(v.tensor_copy(
                    out=scrt[:, SSZ * tt:SSZ * (tt + 1)].unsqueeze(2),
                    in_=eb(k, tt).rearrange("p (a b) -> p a b",
                                            b=32)[:, :, 0:1]))
            for tt in range(GQ):
                ctx.v(v.max(out=smp8(k)[:, 8 * tt:8 * (tt + 1)],
                            in_=scrt[:, SSZ * tt:SSZ * (tt + 1)]))
            l3 = smp8(k)[:].rearrange("p (t k) -> p t k", k=8)
            ctx.gp(v)
            ctx.v(v.tensor_copy(out=gc(k, LT).unsqueeze(2),
                                in_=l3[:, :, islot:islot + 1]))
            ctx.v(v.tensor_tensor(out=gc(k, SLP).unsqueeze(2),
                                  in0=l3[:, :, aslot:aslot + 1],
                                  in1=l3[:, :, bslot:bslot + 1],
                                  op=ALU.subtract))
            ctx.v(v.memset(gc(k, CA), 0.0))
            ctx.v(v.memset(gc(k, CB), 1.01))
            ctx.v(v.memset(gc(k, ZC), 0.0))
            ctx.v(v.memset(gc(k, LTP), 0.0))
            ctx.v(v.tensor_scalar(out=gc(k, SLP), in0=gc(k, SLP),
                                  scalar1=1.0 / sden, scalar2=1e-37,
                                  op0=ALU.mult, op1=ALU.max))
            ctx.gp(v)
            ctx.v(v.tensor_copy(out=gc(k, tslots[0]), in_=gc(k, LT)))
            ctx.v(v.tensor_scalar(out=gc(k, TBG), in0=gc(k, LT),
                                  scalar1=65536.0, scalar2=None,
                                  op0=ALU.mult))

    def s2_A(k, rnd):
        sT = ctx.snap()

        @blk.scalar
        def _(a):
            ctx.wait(a, sT, "V")
            for tt in range(GQ):
                ctx.a(a.activation(
                    out=junk[:, 0:S // 2], in_=eb(k, tt)[:, 0:S // 2],
                    func=AF.Tanh,
                    bias=gc1(k, TBG, tt), scale=-65536.0,
                    accum_out=gc1(k, cslots[rnd], tt)))

    def s2_probeV(k, rnd, half=False):
        sP = ctx.snap()

        @blk.vector
        def _(v):
            if not half:
                ctx.wait(v, sP, "A")
            for tt in range(GQ):
                if half:
                    ctx.v(v.tensor_scalar(
                        out=scrt[:, 0:S // 2], in0=eb(k, tt)[:, S // 2:S],
                        scalar1=gc1(k, tslots[rnd], tt), scalar2=None,
                        op0=ALU.is_ge, op1=ALU.add,
                        accum_out=gc1(k, cslotsB[rnd], tt)))
                else:
                    ctx.v(v.tensor_scalar(
                        out=junk[:], in0=eb(k, tt),
                        scalar1=gc1(k, tslots[rnd], tt), scalar2=None,
                        op0=ALU.is_ge, op1=ALU.add,
                        accum_out=gc1(k, cslots[rnd], tt)))

    def s2_chainz(k, rnd):
        sgn = rnd < NR - 2
        sC = ctx.snap()

        @blk.vector
        def _(v):
            if sgn:
                ctx.wait(v, sC, "A")
            cs = cslots[rnd]
            ctx.gp(v)
            if sgn:
                ctx.v(v.tensor_scalar(out=gc(k, TA), in0=gc(k, cs),
                                      scalar1=-0.5, scalar2=NPR / 4.0,
                                      op0=ALU.mult, op1=ALU.add))
                ctx.v(v.tensor_copy(out=gc(k, ZP), in_=gc(k, ZC)))
                ctx.v(v.tensor_tensor(out=gc(k, TA), in0=gc(k, TA),
                                      in1=gc(k, cslotsB[rnd]), op=ALU.add))
            else:
                ctx.v(v.tensor_copy(out=gc(k, TA), in_=gc(k, cs)))
                ctx.v(v.tensor_copy(out=gc(k, ZP), in_=gc(k, ZC)))
            ctx.gp(v)
            ctx.v(v.tensor_scalar(out=gc(k, TB), in0=gc(k, TA),
                                  scalar1=TGT, scalar2=None,
                                  op0=ALU.is_ge))
            ctx.v(v.tensor_scalar(out=gc(k, TC), in0=gc(k, TA),
                                  scalar1=2.0, scalar2=-2.0 * TGT,
                                  op0=ALU.mult, op1=ALU.add))
            ctx.v(v.tensor_scalar(out=gc(k, TD), in0=gc(k, TA),
                                  scalar1=TGT, scalar2=None,
                                  op0=ALU.add))
            ctx.gp(v)
            ctx.v(v.reciprocal(out=gc(k, TD), in_=gc(k, TD)))
            ctx.v(v.tensor_scalar(out=gc(k, TA), in0=gc(k, TB),
                                  scalar1=1000.0, scalar2=-1000.0,
                                  op0=ALU.mult, op1=ALU.add))
            ctx.gp(v)
            ctx.v(v.tensor_tensor(out=gc(k, ZC), in0=gc(k, TC),
                                  in1=gc(k, TD), op=ALU.mult))
            ctx.v(v.tensor_scalar(out=gc(k, TD), in0=gc(k, TB),
                                  scalar1=1000.0, scalar2=None,
                                  op0=ALU.mult))
            ctx.v(v.tensor_tensor(out=gc(k, TA), in0=gc(k, TA),
                                  in1=gc(k, LT), op=ALU.add))
            ctx.v(v.tensor_tensor(out=gc(k, TD), in0=gc(k, TD),
                                  in1=gc(k, LT), op=ALU.add))
            ctx.v(v.tensor_tensor(out=gc(k, CA), in0=gc(k, CA),
                                  in1=gc(k, TA), op=ALU.max))
            ctx.v(v.tensor_tensor(out=gc(k, CB), in0=gc(k, CB),
                                  in1=gc(k, TD), op=ALU.min))
            if rnd > 0:
                ctx.v(v.tensor_tensor(out=gc(k, TC), in0=gc(k, LT),
                                      in1=gc(k, LTP), op=ALU.subtract))
                ctx.v(v.tensor_tensor(out=gc(k, TD), in0=gc(k, ZC),
                                      in1=gc(k, ZP), op=ALU.subtract))
                ctx.v(v.tensor_scalar(out=gc(k, TB), in0=gc(k, TC),
                                      scalar1=-1.0, scalar2=None,
                                      op0=ALU.mult))
                ctx.v(v.tensor_scalar(out=gc(k, TA), in0=gc(k, TD),
                                      scalar1=-1.0, scalar2=None,
                                      op0=ALU.mult))
                ctx.v(v.tensor_tensor(out=gc(k, TC), in0=gc(k, TC),
                                      in1=gc(k, TB), op=ALU.max))
                ctx.v(v.tensor_tensor(out=gc(k, TD), in0=gc(k, TD),
                                      in1=gc(k, TA), op=ALU.max))
                ctx.gp(v)
                ctx.v(v.tensor_scalar(out=gc(k, TD), in0=gc(k, TD),
                                      scalar1=ZF, scalar2=None,
                                      op0=ALU.max))
                ctx.gp(v)
                ctx.v(v.reciprocal(out=gc(k, TD), in_=gc(k, TD)))
                ctx.gp(v)
                ctx.v(v.tensor_tensor(out=gc(k, TC), in0=gc(k, TC),
                                      in1=gc(k, TD), op=ALU.mult))
                ctx.v(v.tensor_scalar(out=gc(k, TD), in0=gc(k, SLP),
                                      scalar1=0.3, scalar2=None,
                                      op0=ALU.mult))
                ctx.v(v.tensor_scalar(out=gc(k, TB), in0=gc(k, SLP),
                                      scalar1=4.0, scalar2=None,
                                      op0=ALU.mult))
                ctx.v(v.tensor_tensor(out=gc(k, TC), in0=gc(k, TC),
                                      in1=gc(k, TD), op=ALU.max))
                ctx.gp(v)
                ctx.v(v.tensor_tensor(out=gc(k, SLP), in0=gc(k, TC),
                                      in1=gc(k, TB), op=ALU.min))
            ctx.gp(v)
            ctx.v(v.tensor_tensor(out=gc(k, TC), in0=gc(k, ZC),
                                  in1=gc(k, SLP), op=ALU.mult))
            caprel = -0.5 if rnd == 0 else -0.95
            ctx.v(v.tensor_scalar(out=gc(k, TD), in0=gc(k, LT),
                                  scalar1=caprel, scalar2=None,
                                  op0=ALU.mult))
            ctx.gp(v)
            ctx.v(v.tensor_tensor(out=gc(k, TC), in0=gc(k, TC),
                                  in1=gc(k, TD), op=ALU.max))
            ctx.v(v.tensor_copy(out=gc(k, LTP), in_=gc(k, LT)))
            ctx.gp(v)
            ctx.v(v.tensor_tensor(out=gc(k, TD), in0=gc(k, LT),
                                  in1=gc(k, TC), op=ALU.add))
            ctx.gp(v)
            ctx.v(v.tensor_tensor(out=gc(k, TB), in0=gc(k, TD),
                                  in1=gc(k, CA), op=ALU.is_ge))
            ctx.v(v.tensor_tensor(out=gc(k, TC), in0=gc(k, TD),
                                  in1=gc(k, CB), op=ALU.is_le))
            ctx.gp(v)
            ctx.v(v.tensor_tensor(out=gc(k, TB), in0=gc(k, TB),
                                  in1=gc(k, TC), op=ALU.mult))
            ctx.v(v.tensor_tensor(out=gc(k, TC), in0=gc(k, CA),
                                  in1=gc(k, CB), op=ALU.add))
            ctx.gp(v)
            ctx.v(v.tensor_scalar(out=gc(k, TC), in0=gc(k, TC),
                                  scalar1=0.5, scalar2=None,
                                  op0=ALU.mult))
            ctx.gp(v)
            ctx.v(v.tensor_tensor(out=gc(k, TD), in0=gc(k, TD),
                                  in1=gc(k, TC), op=ALU.subtract))
            ctx.gp(v)
            ctx.v(v.tensor_tensor(out=gc(k, TD), in0=gc(k, TD),
                                  in1=gc(k, TB), op=ALU.mult))
            ctx.gp(v)
            ctx.v(v.tensor_tensor(out=gc(k, LT), in0=gc(k, TC),
                                  in1=gc(k, TD), op=ALU.add))
            ctx.gp(v)
            ctx.v(v.tensor_copy(out=gc(k, tslots[rnd + 1]), in_=gc(k, LT)))
            ctx.v(v.tensor_scalar(out=gc(k, TBG), in0=gc(k, LT),
                                  scalar1=65536.0, scalar2=None,
                                  op0=ALU.mult))

    def s3(k):
        sX = ctx.snap()

        @blk.vector
        def _(v):
            ctx.wait(v, sX, "A", "P")
            ctx.gp(v)
            ctx.v(v.tensor_scalar(out=gc(k, TC), in0=gc(k, cslots[-2]),
                                  scalar1=-(float(TOPK) - 0.5),
                                  scalar2=None, op0=ALU.add))
            ctx.v(v.tensor_scalar(out=gc(k, TD), in0=gc(k, cslots[-1]),
                                  scalar1=-(float(TOPK) - 0.5),
                                  scalar2=None, op0=ALU.add))
            ctx.v(v.tensor_scalar(out=gc(k, TA), in0=gc(k, TC),
                                  scalar1=-1.0, scalar2=None,
                                  op0=ALU.mult))
            ctx.v(v.tensor_scalar(out=gc(k, TB), in0=gc(k, TD),
                                  scalar1=-1.0, scalar2=None,
                                  op0=ALU.mult))
            ctx.v(v.tensor_tensor(out=gc(k, TC), in0=gc(k, TC),
                                  in1=gc(k, TA), op=ALU.max))
            ctx.v(v.tensor_tensor(out=gc(k, TD), in0=gc(k, TD),
                                  in1=gc(k, TB), op=ALU.max))
            ctx.gp(v)
            ctx.v(v.tensor_tensor(out=gc(k, TA), in0=gc(k, TC),
                                  in1=gc(k, TD), op=ALU.is_ge))
            ctx.v(v.tensor_tensor(out=gc(k, TB), in0=gc(k, tslots[-1]),
                                  in1=gc(k, tslots[-2]), op=ALU.subtract))
            ctx.v(v.tensor_tensor(out=gc(k, TC), in0=gc(k, cslots[-1]),
                                  in1=gc(k, cslots[-2]), op=ALU.subtract))
            ctx.v(v.tensor_tensor(out=gc(k, TB), in0=gc(k, TB),
                                  in1=gc(k, TA), op=ALU.mult))
            ctx.v(v.tensor_tensor(out=gc(k, TC), in0=gc(k, TC),
                                  in1=gc(k, TA), op=ALU.mult))
            ctx.v(v.tensor_tensor(out=gc(k, TBS), in0=gc(k, tslots[-2]),
                                  in1=gc(k, TB), op=ALU.add))
            ctx.v(v.tensor_tensor(out=gc(k, CBS), in0=gc(k, cslots[-2]),
                                  in1=gc(k, TC), op=ALU.add))
            for tt in range(GQ):
                e_t = eb(k, tt)
                sel_t = gbuf[:, tt * S:(tt + 1) * S]
                ctx.gp(v)
                ctx.v(v.tensor_scalar(out=scrt[:], in0=e_t,
                                      scalar1=gc1(k, TBS, tt),
                                      scalar2=-4.0,
                                      op0=ALU.is_lt, op1=ALU.mult))
                ctx.gp(v)
                ctx.v(v.tensor_tensor(out=sel_t, in0=scrt[:],
                                      in1=e_t, op=ALU.subtract))
                ctx.gp(v)
                ctx.v(v.max(out=kscr[:], in_=sel_t))
                ctx.gp(v)
                ctx.v(v.tensor_scalar(out=scrt[:], in0=e_t,
                                      scalar1=gc1(k, TBS, tt),
                                      scalar2=-4.0,
                                      op0=ALU.is_ge, op1=ALU.mult))
                ctx.v(v.tensor_scalar(out=candb[:, tt * 16:tt * 16 + 8],
                                      in0=kscr[:], scalar1=-1.0,
                                      scalar2=None, op0=ALU.mult))
                ctx.v(v.tensor_tensor(out=sel_t, in0=scrt[:], in1=e_t,
                                      op=ALU.add))
                ctx.gp(v)
                ctx.v(v.max(out=kscr[:], in_=sel_t))
                ctx.gp(v)
                ctx.v(v.tensor_scalar(
                    out=candb[:, tt * 16 + 8:tt * 16 + 16],
                    in0=kscr[:], scalar1=1.0, scalar2=None, op0=ALU.mult))
            ctx.gp(v)
            ctx.v(v.tensor_scalar(out=gc(k, RHO), in0=gc(k, CBS),
                                  scalar1=-float(TOPK - 1), scalar2=None,
                                  op0=ALU.add))
            ctx.gp(v)
            ctx.v(v.tensor_scalar(out=gc(k, RHO), in0=gc(k, RHO),
                                  scalar1=-7.0, scalar2=8.0,
                                  op0=ALU.max, op1=ALU.min))
            ctx.gp(v)
            iview = per["iota16"][:].unsqueeze(1).to_broadcast(
                [128, GQ, 16])
            rview = gc(k, RHO).unsqueeze(2).to_broadcast([128, GQ, 16])
            ctx.v(v.tensor_tensor(
                out=ohb[:].rearrange("p (t k) -> p t k", k=16),
                in0=iview, in1=rview, op=ALU.is_equal))
            ctx.gp(v)
            ctx.v(v.tensor_tensor(out=ohb[:], in0=ohb[:], in1=candb[:],
                                  op=ALU.mult))
            ctx.gp(v)
            ctx.v(v.tensor_reduce(
                out=gc(k, TST).unsqueeze(2),
                in_=ohb[:].rearrange("p (t k) -> p t k", k=16),
                axis=AXX, op=ALU.add))
            ctx.v(v.tensor_scalar(out=gc(k, TB), in0=gc(k, BET),
                                  scalar1=1.0, scalar2=1e-9 / NPR,
                                  op0=ALU.max, op1=ALU.mult))
            ctx.gp(v)
            ctx.v(v.tensor_tensor(out=gc(k, TA), in0=gc(k, TB),
                                  in1=gc(k, BET), op=ALU.add))
            for pp in range(0, GQ, 2):
                for tt in (pp, pp + 1):
                    ctx.v(v.tensor_scalar(
                        out=gbuf[:, tt * S:(tt + 1) * S], in0=eb(k, tt),
                        scalar1=gc1(k, TST, tt), scalar2=None,
                        op0=ALU.is_ge))
                for tt in (pp, pp + 1):
                    sc_b = scrt if tt % 2 == 0 else junk
                    ctx.v(v.tensor_scalar(out=sc_b[:], in0=eb(k, tt),
                                          scalar1=gc1(k, BET, tt),
                                          scalar2=None,
                                          op0=ALU.subtract))
                for tt in (pp, pp + 1):
                    sc_b = scrt if tt % 2 == 0 else junk
                    ctx.v(v.tensor_tensor(
                        out=gbuf[:, tt * S:(tt + 1) * S], in0=sc_b[:],
                        in1=gbuf[:, tt * S:(tt + 1) * S], op=ALU.mult))
            for tt in range(GQ):
                ctx.v(v.tensor_scalar(out=gbuf[:, tt * S:(tt + 1) * S],
                                      in0=gbuf[:, tt * S:(tt + 1) * S],
                                      scalar1=gc1(k, TA, tt),
                                      scalar2=None, op0=ALU.add))

    def s4a(k):
        h, gi = groups[k]
        v_end = ctx.cnt["V"]
        aev0 = ctx.cnt["A"]
        HKC = KCH // 2
        NHF = 2 * GQ

        @blk.tensor
        def _(p):
            for H in range(NHF):
                tt, hf = H // 2, H % 2
                if H == 0:
                    ctx.waitv(p, "V", v_end)
                    ctx.waitv(p, "A", aev0)
                elif H == 1:
                    pass
                else:
                    ctx.waitv(p, "A", aev0 + H - 1)
                for c in range(HKC):
                    cc = hf * HKC + c
                    ctx.p(p.transpose(
                        out=tr_ps2[:, (S // 2) * hf + 128 * c:
                                   (S // 2) * hf + 128 * (c + 1)],
                        in_=gbuf[:, tt * S + 128 * cc:
                                 tt * S + 128 * (cc + 1)],
                        identity=per["ident"][:]))

        pev0 = ctx.cnt["P"] - NHF * HKC

        @blk.scalar
        def _(a):
            for H in range(NHF):
                tt, hf = H // 2, H % 2
                ctx.waitv(a, "P", pev0 + (H + 1) * HKC)
                o3 = gTbuf[:].rearrange("p (c w) -> p c w", w=GW)[
                    :, hf * HKC:(hf + 1) * HKC, 128 * tt:128 * (tt + 1)]
                i3 = tr_ps2[:, (S // 2) * hf:(S // 2) * (hf + 1)].rearrange(
                    "p (c w) -> p c w", w=128)
                ctx.a(a.copy(out=o3, in_=i3))

        sJ = ctx.snap()

        @blk.tensor
        def _(p):
            ctx.wait(p, sJ, "A", "V")
            for c in range(KCH):
                ctx.p(p.matmul(
                    out=pv_ps[:],
                    lhsT=vtm(k)[:, (DK + 1) * c:(DK + 1) * (c + 1)],
                    rhs=gTbuf[:, c * GW:(c + 1) * GW],
                    start=(c == 0), stop=(c == KCH - 1)))

        sK = ctx.snap()

        @blk.scalar
        def _(a):
            ctx.wait(a, sK, "P", "Q")
            ctx.a(a.activation(out=invsc[DK:DK + 1, :],
                               in_=pv_ps[DK:DK + 1, :], func=AF.Ln,
                               bias=per["epsc"][64:65, 2:3], scale=1.0))
            ctx.ga(a)
            ctx.a(a.activation(out=invsc[DK:DK + 1, :],
                               in_=invsc[DK:DK + 1, :], func=AF.Exp,
                               scale=-1.0))

        sL2 = ctx.snap()

        @blk.sync
        def _(q):
            ctx.wait(q, sL2, "A", "V")
            ctx.qd(q.dma_start(out=drow[5, 0:GW],
                               in_=invsc[DK:DK + 1, :]))
            ctx.waitv(q, "Q", ctx.cnt["Q"])
            ctx.qd(q.dma_start(out=invbc[:],
                               in_=drow[5:6, 0:GW].partition_broadcast(64)))

    def s4b(k):
        h, gi = groups[k]
        sM = ctx.snap()

        @blk.vector
        def _(v):
            ctx.wait(v, sM, "Q")
            ctx.v(v.tensor_tensor(out=aot_t[:], in0=pv_ps[0:DK, :],
                                  in1=invbc[:], op=ALU.mult))

        sN = ctx.snap()

        @blk.sync
        def _(q):
            ctx.wait(q, sN, "V")
            ctx.qd(q.dma_start(
                out=aoT[64 * h:64 * h + 64, gi * GW:(gi + 1) * GW],
                in_=aot_t[:]))
            if scr.get("dbg") is not None and \
                    30 * GQ * (k + 1) <= 4096:
                ctx.qd(q.dma_start(
                    out=scr["dbg"][:, 30 * GQ * k:30 * GQ * (k + 1)],
                    in_=gcol2[:, P(k) * 30 * GQ:(P(k) + 1) * 30 * GQ]))

    pstate["exp_a"] = ctx.cnt["A"]

    def s1_all(k):
        if groups[k][1] == 0:
            head_prep(k)
        for tt in range(GQ):
            s1_qk(k, tt)
            s1_rx(k, tt)
            s1_exp(k, tt)
        s1_smp(k)

    s1_all(0)
    for k in range(NGR):
        nxt = k + 1 if k + 1 < NGR else None
        for rnd in range(NR - 2):
            s2_A(k, rnd)
            if nxt is not None and rnd < GQ:
                s1_qk(nxt, rnd)
            s2_probeV(k, rnd, half=True)
            if nxt is not None and rnd < GQ:
                s1_rx(nxt, rnd)
            s2_chainz(k, rnd)
            if nxt is not None and rnd < GQ:
                s1_exp(nxt, rnd)
            if rnd == 0 and k > 0:
                s4a(k - 1)
            if rnd == 1:
                if k > 0:
                    s4b(k - 1)
                if nxt is not None and groups[nxt][1] == 0:
                    head_prep(nxt)
        if nxt is not None:
            s1_smp(nxt)
        for rnd in range(NR - 2, NR):
            s2_probeV(k, rnd)
            if rnd < NR - 1:
                s2_chainz(k, rnd)
        s3(k)
        if k == NGR - 1:
            s4a(k)
            s4b(k)

    blk_cm.__exit__(None, None, None)



_PROG_CACHE = {}


def kernel(L0_attn_x, L0_attn_Wqkv, L0_attn_bqkv, L0_attn_Wo, L0_attn_bo,
           L0_ffn_w1, L0_ffn_conv_w, L0_ffn_conv_b, L0_ffn_w2):
    x = np.ascontiguousarray(np.asarray(L0_attn_x, dtype=np.float32))
    B, S, D = x.shape
    n_cores = 8
    nb = B // n_cores
    cfg = make_cfg(S=S, D=D, HF=L0_ffn_w1.shape[0], NH=16, NB=nb)

    key = (B, S, D)
    if key not in _PROG_CACHE:
        _PROG_CACHE[key] = build_program(cfg)
    nc = _PROG_CACHE[key]

    w = dict(
        Wqkv=np.ascontiguousarray(np.asarray(L0_attn_Wqkv, dtype=np.float32)),
        bqkv=np.ascontiguousarray(np.asarray(L0_attn_bqkv, dtype=np.float32)),
        Wo=np.ascontiguousarray(np.asarray(L0_attn_Wo, dtype=np.float32)),
        bo=np.ascontiguousarray(np.asarray(L0_attn_bo, dtype=np.float32)),
        w1=np.ascontiguousarray(np.asarray(L0_ffn_w1, dtype=np.float32)),
        cw=np.ascontiguousarray(
            np.asarray(L0_ffn_conv_w, dtype=np.float32).reshape(-1, 3)),
        cb=np.ascontiguousarray(np.asarray(L0_ffn_conv_b, dtype=np.float32)),
        w2=np.ascontiguousarray(np.asarray(L0_ffn_w2, dtype=np.float32)),
    )
    in_maps = []
    for c in range(n_cores):
        m = dict(w)
        m["x_in"] = x[c * nb:(c + 1) * nb]
        in_maps.append(m)

    res = run_bass_kernel_spmd(nc, in_maps, core_ids=list(range(n_cores)))
    return np.concatenate([r["out"] for r in res.results], axis=0)


# version 22
# speedup vs baseline: 1.1200x; 1.0485x over previous
import math
from contextlib import ExitStack, contextmanager

import numpy as np
import concourse.bass as bass
import concourse.mybir as mybir
from concourse.bass_utils import run_bass_kernel_spmd

F32 = mybir.dt.float32
BF16 = mybir.dt.bfloat16
I32 = mybir.dt.int32
AF = mybir.ActivationFunctionType
ALU = mybir.AluOpType
AXX = mybir.AxisListType.X

Z1 = 1.2815515655446004
PHI_Z1 = 0.17549833193248682


def make_cfg(S=2048, D=1024, HF=4096, NH=16, NB=2):
    cfg = dict(S=S, D=D, HF=HF, NH=NH, NB=NB, DK=64)
    cfg["TOPK"] = max(1, int((1.0 - 0.9) * S))
    cfg["NT"] = S // 128
    cfg["DCH"] = D // 128
    cfg["HCH"] = HF // 128
    cfg["QCH"] = 3 * D // 128
    cfg["KCH"] = S // 128
    cfg["QW"] = min(256, S)
    cfg["NQ"] = S // cfg["QW"]
    cfg["GQ"] = min(4, cfg["NT"])
    cfg["NG"] = cfg["NT"] // cfg["GQ"]
    return cfg


class Ctx:
    def __init__(self, nc, es):
        self.nc = nc
        self.sem = {}
        self.cnt = {}
        self.uid = 0
        for name in ("V", "A", "P", "G", "Q", "W"):
            self.sem[name] = es.enter_context(nc.semaphore("sem" + name))
            self.cnt[name] = 0

    def name(self, n):
        self.uid += 1
        return f"{n}_{self.uid}"

    def _mark(self, name, instr, n):
        instr.then_inc(self.sem[name], n)
        self.cnt[name] += n
        return instr

    def v(self, i):
        return self._mark("V", i, 1)

    def a(self, i):
        return self._mark("A", i, 1)

    def p(self, i):
        return self._mark("P", i, 1)

    def g(self, i):
        return self._mark("G", i, 1)

    def qd(self, i):
        return self._mark("Q", i, 16)

    def wd(self, i):
        return self._mark("W", i, 16)

    def snap(self):
        return dict(self.cnt)

    def wait(self, h, snap, *names):
        for n in names:
            if snap[n] > 0:
                h.wait_ge(self.sem[n], snap[n])

    def waitv(self, h, name, val):
        if val > 0:
            h.wait_ge(self.sem[name], val)

    def gp(self, h, name="V"):
        h.wait_ge(self.sem[name], self.cnt[name])

    def ga(self, a):
        self.a(a.activation(out=self.adum[0:1, 0:1], in_=self.adum[0:1, 1:2],
                            func=AF.Identity, scale=1.0))

    def pre6(self, a):
        if self.set6 is not None:
            a.add_instruction(mybir.InstLoadActFuncSet(
                name=self.nc.get_next_instruction_name(),
                act_func_set_id=self.set6, ins=[], outs=[]))



def emit_rint(ctx, v, cfg, out_bf, in_f32, i32scr):
    if cfg.get("rint", "cast") == "cast":
        ctx.v(v.tensor_copy(out=i32scr, in_=in_f32))
        ctx.gp(v)
        ctx.v(v.tensor_copy(out=out_bf, in_=i32scr))
    else:
        ctx.v(v.tensor_scalar(out=out_bf, in0=in_f32, scalar1=12582912.0,
                              scalar2=12582912.0, op0=ALU.add, op1=ALU.subtract))


def sp512(n):
    return [(o, min(512, n - o)) for o in range(0, n, 512)]


@contextmanager
def scope():
    es = ExitStack()
    try:
        yield es
    finally:
        es.close()



def build_program(cfg):
    nc = bass.Bass("TRN2", target_bir_lowering=False, detect_race_conditions=False)
    S, D, HF, NB = cfg["S"], cfg["D"], cfg["HF"], cfg["NB"]
    QCH, HCH, NT = cfg["QCH"], cfg["HCH"], cfg["NT"]

    dt = nc.dram_tensor
    t_in = dict(
        x_in=dt("x_in", [NB, S, D], F32, kind="ExternalInput").ap(),
        Wqkv=dt("Wqkv", [3 * D, D], F32, kind="ExternalInput").ap(),
        bqkv=dt("bqkv", [3 * D], F32, kind="ExternalInput").ap(),
        Wo=dt("Wo", [D, D], F32, kind="ExternalInput").ap(),
        bo=dt("bo", [D], F32, kind="ExternalInput").ap(),
        w1=dt("w1", [HF, D], F32, kind="ExternalInput").ap(),
        cw=dt("cw", [HF, 3], F32, kind="ExternalInput").ap(),
        cb=dt("cb", [HF], F32, kind="ExternalInput").ap(),
        w2=dt("w2", [D, HF], F32, kind="ExternalInput").ap(),
    )
    out = dt("out", [NB, S, D], F32, kind="ExternalOutput").ap()
    scr = dict(
        xres=dt("xres", [NB, S, D], F32, kind="Internal").ap(),
        wqkvT=dt("wqkvT", [D, 3 * D], BF16, kind="Internal").ap(),
        woT=dt("woT", [D, D], BF16, kind="Internal").ap(),
        w1qT=dt("w1qT", [D, HF], BF16, kind="Internal").ap(),
        w2qT=dt("w2qT", [HF, D], BF16, kind="Internal").ap(),
        aoT=dt("aoT", [D, S], BF16, kind="Internal").ap(),
        drow=dt("drow", [8, max(S, 128)], F32, kind="Internal").ap(),
        dbg=(dt("dbg", [128, 4096], F32, kind="ExternalOutput").ap()
             if cfg.get("dbg") else None),
        dsc=dt("dsc", [1, 8], F32, kind="Internal").ap(),
    )

    es = ExitStack()
    ctx = Ctx(nc, es)

    def sb(name, shape, d):
        return es.enter_context(nc.sbuf_tensor(name, shape, d))

    per = dict(
        ident=sb("ident", [128, 128], BF16),
        identf=sb("identf", [128, 128], F32),
        epsc=sb("epsc", [128, 3], F32),
        ident2=sb("ident2", [128, 64], BF16),
        onescol=sb("onescol", [128, 1], BF16),
        e65row=sb("e65row", [1, cfg["DK"] + 1], F32),
        iota16=sb("iota16", [128, 16], F32),
        bobc=sb("bobc", [128, D], F32),
        bqkvcol=sb("bqkvcol", [128, QCH], F32),
        cwcol=sb("cwcol", [128, 3 * HCH], F32),
        cbcol=sb("cbcol", [128, HCH], F32),
        wscal=sb("wscal", [128, 4], F32),
        adum=sb("adum", [1, 2], F32),
    )
    ctx.adum = per["adum"]
    ctx.set6 = None
    try:
        from concourse.hw_specs import get_activation_tables
        for _i, _fs in enumerate(get_activation_tables(nc.m.arch).values()):
            if AF.Exp in _fs and AF.Ln in _fs and AF.Sign in _fs:
                ctx.set6 = _i
                break
    except Exception:
        ctx.set6 = 6

    _init_consts(ctx, cfg, per, t_in)
    _prep_weights(ctx, cfg, t_in, scr, per)
    _copy_x(ctx, cfg, t_in["x_in"], scr["xres"])

    stages = cfg.get("stages", ("f1", "at", "f2"))
    for b in range(NB):
        if "f1" in stages:
            with nc.named_scope(f"f1_b{b}"):
                _ffn_sublayer(ctx, cfg, b, scr, per,
                              dst=(out if stages[-1] == "f1" else scr["xres"]))
        if "at" in stages:
            with nc.named_scope(f"at_b{b}"):
                _attn_sublayer(ctx, cfg, b, scr, per)
        if "f2" in stages:
            with nc.named_scope(f"f2_b{b}"):
                _ffn_sublayer(ctx, cfg, b, scr, per, dst=out)

    with nc.Block() as blk:
        fin = ctx.snap()

        @blk.gpsimd
        def _(g):
            ctx.wait(g, fin, "Q", "W", "V", "A", "P", "G")

    es.close()
    return nc



def _init_consts(ctx, cfg, per, t_in):
    nc = ctx.nc
    DK = cfg["DK"]
    with nc.Block() as blk:
        @blk.gpsimd
        def _(g):
            ctx.g(g.memset(per["adum"][:], 0.0))
            ctx.g(g.iota(per["ident"][:], pattern=[[1, 128]], base=0,
                         channel_multiplier=-1,
                         allow_small_or_imprecise_dtypes=True))
            ctx.g(g.iota(per["iota16"][:, 0:8], pattern=[[1, 8]], base=1,
                         channel_multiplier=0,
                         allow_small_or_imprecise_dtypes=True))
            ctx.g(g.iota(per["iota16"][:, 8:16], pattern=[[-1, 8]], base=0,
                         channel_multiplier=0,
                         allow_small_or_imprecise_dtypes=True))
            ctx.g(g.memset(per["onescol"][:], 1.0))
            ctx.g(g.memset(per["e65row"][:], 0.0))
            ctx.g(g.memset(per["e65row"][0:1, DK:DK + 1], 1.0))

        s0 = ctx.snap()

        @blk.vector
        def _(v):
            ctx.wait(v, s0, "G")
            ctx.v(v.tensor_scalar(out=per["ident"][:], in0=per["ident"][:],
                                  scalar1=0.0, scalar2=None, op0=ALU.is_equal))
            ctx.gp(v)
            ctx.v(v.tensor_copy(out=per["identf"][:], in_=per["ident"][:]))
            ctx.v(v.memset(per["epsc"][:, 0:1], 1e-5))
            ctx.v(v.memset(per["epsc"][:, 1:2], 1e-30))
            ctx.v(v.memset(per["epsc"][:, 2:3], 1e-37))

        s1 = ctx.snap()

        @blk.sync
        def _(q):
            ctx.wait(q, s1, "V")
            ctx.qd(q.dma_start(out=per["ident2"][0:64, :],
                               in_=per["ident"][0:64, 0:64]))
            ctx.qd(q.dma_start(out=per["ident2"][64:128, :],
                               in_=per["ident"][0:64, 0:64]))
            ctx.qd(q.dma_start(out=per["bobc"][:],
                               in_=t_in["bo"][None, :].partition_broadcast(128)))
            ctx.qd(q.dma_start(
                out=per["cwcol"][:].rearrange("p (c k) -> p c k", k=3),
                in_=t_in["cw"].rearrange("(c p) k -> p c k", p=128)))

        @blk.gpsimd
        def _(g):
            ctx.wd(g.dma_start(out=per["bqkvcol"][:],
                               in_=t_in["bqkv"].rearrange("(c p) -> p c", p=128),
                               allow_slow_non_contiguous=True))
            ctx.wd(g.dma_start(out=per["cbcol"][:],
                               in_=t_in["cb"].rearrange("(c p) -> p c", p=128),
                               allow_slow_non_contiguous=True))


def _copy_x(ctx, cfg, x_in, xres):
    nc = ctx.nc
    NB, NT, D = cfg["NB"], cfg["NT"], cfg["D"]
    with scope() as es:
        xt = es.enter_context(nc.sbuf_tensor(ctx.name("xcopy"), [128, 2 * D], F32))
        with nc.Block() as blk:
            @blk.sync
            def _(q):
                for k in range(NB * NT):
                    bb, j = divmod(k, NT)
                    buf = xt[:, (k % 2) * D:(k % 2) * D + D]
                    ctx.qd(q.dma_start(out=buf, in_=x_in[bb, 128 * j:128 * (j + 1), :]))
                    ctx.waitv(q, "Q", ctx.cnt["Q"])
                    ctx.qd(q.dma_start(out=xres[bb, 128 * j:128 * (j + 1), :], in_=buf))



def _prep_weights(ctx, cfg, t_in, scr, per):
    nc = ctx.nc
    D, HF, DCH, HCH = cfg["D"], cfg["HF"], cfg["DCH"], cfg["HCH"]
    NC = max(D, HF)

    with scope() as es:
        wtmp = es.enter_context(nc.sbuf_tensor("wtmp", [128, NC], F32))
        wint = es.enter_context(nc.sbuf_tensor("wrm", [128, NC], I32))
        wbf = es.enter_context(nc.sbuf_tensor("wbf", [128, NC], BF16))
        wtr = es.enter_context(nc.sbuf_tensor("wtr", [128, NC], BF16))
        wabs = es.enter_context(nc.sbuf_tensor("wabs", [128, max(HCH, DCH)], F32))
        wsum = es.enter_context(nc.sbuf_tensor("wsum", [128, 1], F32))
        wmisc = es.enter_context(nc.sbuf_tensor("wmisc", [1, 8], F32))
        onef = es.enter_context(nc.sbuf_tensor("onef", [128, 1], F32))
        scal_ps = es.enter_context(nc.psum_tensor("scal_ps", [1, 2], F32))
        tr_ps = es.enter_context(nc.psum_tensor("wtr_ps", [128, NC], BF16))

        with nc.Block() as blk:
            @blk.vector
            def _(v):
                ctx.v(v.memset(onef[:], 1.0))

        jobs = [("qkv", t_in["Wqkv"], scr["wqkvT"], 3 * D // 128, DCH, None),
                ("wo", t_in["Wo"], scr["woT"], DCH, DCH, None),
                ("w1", t_in["w1"], scr["w1qT"], HCH, DCH, 0),
                ("w2", t_in["w2"], scr["w2qT"], DCH, HCH, 2)]

        for (name, src, dst, nrow_ch, ncol_ch, scol) in jobs:
            ncol = ncol_ch * 128
            quant = scol is not None
            if quant:
                for r in range(nrow_ch):
                    with nc.Block() as blk:
                        s0 = ctx.snap()

                        @blk.sync
                        def _(q):
                            ctx.wait(q, s0, "V")
                            ctx.qd(q.dma_start(out=wtmp[:, 0:ncol],
                                               in_=src[128 * r:128 * (r + 1), :]))

                        s1 = ctx.snap()

                        @blk.vector
                        def _(v):
                            ctx.wait(v, s1, "Q")
                            ctx.v(v.tensor_reduce(out=wabs[:, r:r + 1],
                                                  in_=wtmp[:, 0:ncol], axis=AXX,
                                                  op=ALU.add,
                                                  apply_absolute_value=True))
                with nc.Block() as blk:
                    s0 = ctx.snap()

                    @blk.vector
                    def _(v):
                        ctx.wait(v, s0, "Q")
                        ctx.gp(v)
                        ctx.v(v.tensor_reduce(out=wsum[:], in_=wabs[:, 0:nrow_ch],
                                              axis=AXX, op=ALU.add))

                    s1 = ctx.snap()

                    @blk.tensor
                    def _(p):
                        ctx.wait(p, s1, "V")
                        ctx.p(p.matmul(out=scal_ps[0:1, 0:1], lhsT=wsum[:],
                                       rhs=onef[:], start=True, stop=True))

                    s2 = ctx.snap()

                    @blk.vector
                    def _(v):
                        ctx.wait(v, s2, "P")
                        n_el = float(nrow_ch * 128 * ncol)
                        ctx.v(v.tensor_scalar(out=wmisc[0:1, 0:1],
                                              in0=scal_ps[0:1, 0:1],
                                              scalar1=1.0 / n_el, scalar2=1e-5,
                                              op0=ALU.mult, op1=ALU.max))
                        ctx.gp(v)
                        ctx.v(v.reciprocal(out=wmisc[0:1, 1:2], in_=wmisc[0:1, 0:1]))
                        ctx.v(v.tensor_scalar(out=wmisc[0:1, 2:3], in0=wmisc[0:1, 0:1],
                                              scalar1=1.0 / 127.0, scalar2=None,
                                              op0=ALU.mult))

                    s3 = ctx.snap()

                    @blk.sync
                    def _(q):
                        ctx.wait(q, s3, "V")
                        ctx.qd(q.dma_start(out=scr["dsc"][0:1, 0:2],
                                           in_=wmisc[0:1, 1:3]))
                        ctx.waitv(q, "Q", ctx.cnt["Q"])
                        ctx.qd(q.dma_start(
                            out=per["wscal"][:, scol:scol + 2],
                            in_=scr["dsc"][0:1, 0:2].partition_broadcast(128)))
                        if scr.get("dbg") is not None:
                            ctx.waitv(q, "Q", ctx.cnt["Q"])
                            ctx.qd(q.dma_start(
                                out=scr["dbg"][:, 8 * scol:8 * scol + 4],
                                in_=per["wscal"][:, 0:4]))
                            ctx.qd(q.dma_start(
                                out=scr["dbg"][:, 20 + 40 * scol:20 + 40 * scol + nrow_ch],
                                in_=wabs[:, 0:nrow_ch]))

            for r in range(nrow_ch):
                with nc.Block() as blk:
                    s0 = ctx.snap()

                    @blk.sync
                    def _(q):
                        ctx.wait(q, s0, "V", "P")
                        ctx.qd(q.dma_start(out=wtmp[:, 0:ncol],
                                           in_=src[128 * r:128 * (r + 1), :]))

                    s1 = ctx.snap()

                    @blk.vector
                    def _(v):
                        ctx.wait(v, s1, "Q", "W")
                        if quant:
                            ctx.v(v.tensor_scalar(out=wtmp[:, 0:ncol],
                                                  in0=wtmp[:, 0:ncol],
                                                  scalar1=per["wscal"][:, scol:scol + 1],
                                                  scalar2=None, op0=ALU.mult))
                            ctx.gp(v)
                            if cfg.get("rint", "cast") == "cast":
                                ctx.v(v.tensor_copy(out=wint[:, 0:ncol],
                                                    in_=wtmp[:, 0:ncol]))
                            else:
                                ctx.v(v.tensor_scalar(
                                    out=wint[:, 0:ncol].bitcast(F32),
                                    in0=wtmp[:, 0:ncol], scalar1=12582912.0,
                                    scalar2=12582912.0, op0=ALU.add,
                                    op1=ALU.subtract))
                            wq_in = (wint[:, 0:ncol] if cfg.get("rint", "cast") == "cast"
                                     else wint[:, 0:ncol].bitcast(F32))
                            ctx.gp(v)
                            ctx.v(v.tensor_scalar(out=wbf[:, 0:ncol],
                                                  in0=wq_in,
                                                  scalar1=-1.0, scalar2=1.0,
                                                  op0=ALU.max, op1=ALU.min))
                        else:
                            ctx.v(v.tensor_copy(out=wbf[:, 0:ncol],
                                                in_=wtmp[:, 0:ncol]))

                    s2 = ctx.snap()

                    @blk.tensor
                    def _(p):
                        ctx.wait(p, s2, "V", "A")
                        for c in range(ncol_ch):
                            ctx.p(p.transpose(out=tr_ps[:, 128 * c:128 * (c + 1)],
                                              in_=wbf[:, 128 * c:128 * (c + 1)],
                                              identity=per["ident"][:]))

                    s3 = ctx.snap()

                    @blk.scalar
                    def _(a):
                        ctx.wait(a, s3, "P", "W")
                        ctx.a(a.copy(out=wtr[:, 0:ncol], in_=tr_ps[:, 0:ncol]))

                    s4 = ctx.snap()

                    @blk.gpsimd
                    def _(g):
                        ctx.wait(g, s4, "A")
                        for c in range(ncol_ch):
                            ctx.wd(g.dma_start(
                                out=dst[128 * c:128 * (c + 1), 128 * r:128 * (r + 1)],
                                in_=wtr[:, 128 * c:128 * (c + 1)]))



def _ffn_sublayer(ctx, cfg, b, scr, per, dst):
    nc = ctx.nc
    S, D, HF = cfg["S"], cfg["D"], cfg["HF"]
    NT, DCH, HCH = cfg["NT"], cfg["DCH"], cfg["HCH"]
    QW, NQ = cfg["QW"], cfg["NQ"]
    SP = S + 2
    xres = scr["xres"]
    drow = scr["drow"]

    with scope() as es:
        t = lambda n, s, d: es.enter_context(nc.sbuf_tensor(ctx.name(n), s, d))
        xtile = t("f_x", [128, 2 * D], F32)
        xh = t("f_xh", [128, D], F32)
        xint = t("f_xrm", [128, D], I32)
        xqtm = t("f_xqtm", [128, D], BF16)
        st6 = t("f_st6", [128, 24], F32)
        st2 = t("f_st2", [128, 2], F32)
        tc = t("f_tc", [128, 12], F32)
        inv1col = t("f_inv1c", [128, NT], F32)
        inv2col = t("f_inv2c", [128, NT], F32)
        s2tok = t("f_s2tok", [128, NT], F32)
        xqT = t("f_xqT", [128, DCH * SP], BF16)
        w1all = t("f_w1all", [128, DCH * HF], BF16)
        w2s = t("f_w2s", [128, 2 * D], BF16)
        inv1bc = t("f_inv1bc", [128, SP], F32)
        s2bc = t("f_s2bc", [128, QW], F32)
        ubuf = t("f_ubuf", [128, 2 * (QW + 2)], F32)
        cbuf = t("f_cbuf", [128, 2 * QW], F32)
        hbuf = t("f_hbuf", [128, HCH * QW], F32)
        hmax = t("f_hmax", [128, QW], F32)
        hmin = t("f_hmin", [128, QW], F32)
        hqbuf = t("f_hq", [128, HCH * QW], BF16)
        qtmp = t("f_qtmp", [128, QW], F32)
        hint = t("f_rmq", [128, QW], I32)
        resv = t("f_resv", [128, D], F32)
        h_ps = es.enter_context(nc.psum_tensor(ctx.name("f_hps"), [128, 1024], F32))
        tr_ps = es.enter_context(nc.psum_tensor(ctx.name("f_trps"), [128, D], BF16))
        trf_ps = es.enter_context(nc.psum_tensor(ctx.name("f_trfps"), [128, 2 * 128], F32))
        y2_psl = [es.enter_context(nc.psum_tensor(ctx.name("f_y2ps"), [128, D], F32))
                  for _ in range(QW // 128)]

        ENTRY = ctx.snap()

        with nc.Block() as blk:
            @blk.sync
            def _(q):
                ctx.wait(q, ENTRY, "V", "A", "P", "G", "W")
                ctx.qd(q.dma_start(
                    out=w1all[:].rearrange("p (c h) -> p c h", c=DCH),
                    in_=scr["w1qT"].rearrange("(c p) h -> p c h", p=128)))

            @blk.vector
            def _(v):
                ctx.wait(v, ENTRY, "Q", "A", "P", "G", "W")
                xqT3 = xqT[:].rearrange("p (c s) -> p c s", s=SP)
                ctx.v(v.memset(xqT3[:, :, 0:1], 0.0))
                ctx.v(v.memset(xqT3[:, :, SP - 1:SP], 0.0))
                ctx.v(v.memset(inv1bc[:, 0:1], 0.0))
                ctx.v(v.memset(inv1bc[:, SP - 1:SP], 0.0))

        a_evac = None
        with nc.Block() as blk:
            for j in range(NT):
                s0 = ctx.snap()

                @blk.sync
                def _(q):
                    ctx.wait(q, s0, "V")
                    ctx.qd(q.dma_start(out=xtile[:, (j % 2) * D:(j % 2) * D + D],
                                       in_=xres[b, 128 * j:128 * (j + 1), :]))

                s1 = ctx.snap()
                xt = xtile[:, (j % 2) * D:(j % 2) * D + D]

                @blk.vector
                def _(v):
                    ctx.wait(v, s1, "Q")
                    for ii, (o, wd) in enumerate(sp512(D)):
                        ctx.v(v.bn_stats(out=st6[:, 6 * ii:6 * (ii + 1)],
                                         in_=xt[:, o:o + wd]))
                    ctx.gp(v)
                    ctx.v(v.bn_aggr(out=st2[:], in_=st6[:, 0:6 * len(sp512(D))]))

                s2 = ctx.snap()

                @blk.scalar
                def _(a):
                    ctx.wait(a, s2, "V")
                    ctx.a(a.activation(out=tc[:, 0:1], in_=st2[:, 1:2], func=AF.Ln,
                                       bias=per["epsc"][:, 0:1], scale=1.0))
                    ctx.ga(a)
                    ctx.a(a.activation(out=tc[:, 1:2], in_=tc[:, 0:1], func=AF.Exp,
                                       scale=-0.5))

                s3 = ctx.snap()

                @blk.vector
                def _(v):
                    ctx.wait(v, s3, "A")
                    ctx.v(v.tensor_scalar(out=xh[:], in0=xt,
                                          scalar1=st2[:, 0:1], scalar2=tc[:, 1:2],
                                          op0=ALU.subtract, op1=ALU.mult))
                    ctx.gp(v)
                    ctx.v(v.tensor_reduce(out=tc[:, 2:3], in_=xh[:], axis=AXX,
                                          op=ALU.max, apply_absolute_value=True))
                    ctx.gp(v)
                    ctx.v(v.tensor_scalar(out=tc[:, 3:4], in0=tc[:, 2:3],
                                          scalar1=1e-5, scalar2=None, op0=ALU.max))
                    ctx.gp(v)
                    ctx.v(v.reciprocal(out=tc[:, 4:5], in_=tc[:, 3:4]))
                    ctx.gp(v)
                    ctx.v(v.tensor_scalar(out=tc[:, 5:6], in0=tc[:, 4:5],
                                          scalar1=127.0, scalar2=None, op0=ALU.mult))
                    ctx.v(v.tensor_scalar(out=inv1col[:, j:j + 1], in0=tc[:, 3:4],
                                          scalar1=per["wscal"][:, 1:2], scalar2=None,
                                          op0=ALU.mult))
                    ctx.v(v.tensor_scalar(out=xh[:], in0=xh[:],
                                          scalar1=tc[:, 5:6], scalar2=None,
                                          op0=ALU.mult))
                    ctx.gp(v)
                    emit_rint(ctx, v, cfg, xqtm[:], xh[:], xint[:])

                s4 = ctx.snap()

                @blk.tensor
                def _(p):
                    ctx.wait(p, s4, "V")
                    if a_evac:
                        ctx.waitv(p, "A", a_evac)
                    for c in range(DCH):
                        ctx.p(p.transpose(out=tr_ps[:, 128 * c:128 * (c + 1)],
                                          in_=xqtm[:, 128 * c:128 * (c + 1)],
                                          identity=per["ident"][:]))

                s5 = ctx.snap()

                @blk.scalar
                def _(a):
                    ctx.wait(a, s5, "P")
                    o3 = xqT[:].rearrange("p (c s) -> p c s", s=SP)[
                        :, :, 1 + 128 * j:1 + 128 * (j + 1)]
                    i3 = tr_ps[:].rearrange("p (c w) -> p c w", w=128)
                    ctx.a(a.copy(out=o3, in_=i3))
                a_evac = ctx.cnt["A"]

        with nc.Block() as blk:
            s0 = ctx.snap()

            @blk.gpsimd
            def _(g):
                ctx.wait(g, s0, "V")
                ctx.wd(g.dma_start(out=drow[0, 0:S].rearrange("(t p) -> p t", p=128),
                                   in_=inv1col[:, 0:NT],
                                   allow_slow_non_contiguous=True))

            s1 = ctx.snap()

            @blk.sync
            def _(q):
                ctx.wait(q, s1, "W")
                ctx.qd(q.dma_start(out=inv1bc[:, 1:S + 1],
                                   in_=drow[0:1, 0:S].partition_broadcast(128)))

        for qw in range(NQ):
            base = qw * QW
            blk_cm = nc.Block()
            blk = blk_cm.__enter__()
            if True:
                s0 = ctx.snap()

                @blk.vector
                def _(v):
                    ctx.wait(v, s0, "Q", "P")
                    ctx.v(v.memset(hmax[:], -3.0e38))
                    ctx.v(v.memset(hmin[:], 3.0e38))

            ent = ctx.snap()
            p0, v0, a0, q0 = ent["P"], ent["V"], ent["A"], ent["Q"]
            P_PER_M = DCH
            V_PER_M = 6
            A_PER_M = 1
            Q_PER_M = 16

            @blk.tensor
            def _(p):
                for m in range(HCH):
                    if m == 0:
                        ctx.wait(p, ent, "V", "A")
                    if m >= 2:
                        ctx.waitv(p, "V", v0 + (m - 2) * V_PER_M + 1)
                    hp = h_ps[:, (m % 2) * 512:(m % 2) * 512 + QW + 2]
                    for c in range(DCH):
                        ctx.p(p.matmul(
                            out=hp,
                            lhsT=w1all[:, c * HF + 128 * m:c * HF + 128 * (m + 1)],
                            rhs=xqT[:, c * SP + base:c * SP + base + QW + 2],
                            start=(c == 0), stop=(c == DCH - 1)))

            @blk.vector
            def _(v):
                for m in range(HCH):
                    ctx.waitv(v, "P", p0 + (m + 1) * P_PER_M)
                    if m >= 2:
                        ctx.waitv(v, "A", a0 + (m - 1) * A_PER_M)
                    hp = h_ps[:, (m % 2) * 512:(m % 2) * 512 + QW + 2]
                    ub = ubuf[:, (m % 2) * (QW + 2):(m % 2) * (QW + 2) + QW + 2]
                    cb2 = cbuf[:, (m % 2) * QW:(m % 2) * QW + QW]
                    ctx.v(v.tensor_tensor(out=ub, in0=hp,
                                          in1=inv1bc[:, base:base + QW + 2],
                                          op=ALU.mult))
                    ctx.gp(v)
                    ctx.v(v.tensor_scalar(out=cb2, in0=ub[:, 0:QW],
                                          scalar1=per["cwcol"][:, 3 * m:3 * m + 1],
                                          scalar2=None, op0=ALU.mult))
                    ctx.gp(v)
                    ctx.v(v.scalar_tensor_tensor(
                        out=cb2, in0=ub[:, 1:QW + 1],
                        scalar=per["cwcol"][:, 3 * m + 1:3 * m + 2],
                        in1=cb2, op0=ALU.mult, op1=ALU.add))
                    ctx.gp(v)
                    ctx.v(v.scalar_tensor_tensor(
                        out=cb2, in0=ub[:, 2:QW + 2],
                        scalar=per["cwcol"][:, 3 * m + 2:3 * m + 3],
                        in1=cb2, op0=ALU.mult, op1=ALU.add))
                    ctx.waitv(v, "A", a0 + (m + 1) * A_PER_M)
                    hb = hbuf[:, m * QW:(m + 1) * QW]
                    ctx.v(v.tensor_tensor(out=hmax[:], in0=hmax[:], in1=hb,
                                          op=ALU.max))
                    ctx.v(v.tensor_tensor(out=hmin[:], in0=hmin[:], in1=hb,
                                          op=ALU.min))

            @blk.scalar
            def _(a):
                for m in range(HCH):
                    ctx.waitv(a, "V", v0 + m * V_PER_M + 4)
                    cb2 = cbuf[:, (m % 2) * QW:(m % 2) * QW + QW]
                    ctx.a(a.activation(out=hbuf[:, m * QW:(m + 1) * QW], in_=cb2,
                                       func=AF.Silu,
                                       bias=per["cbcol"][:, m:m + 1], scale=1.0))

            s0 = ctx.snap()

            @blk.vector
            def _(v):
                ctx.v(v.tensor_scalar(out=hmin[:], in0=hmin[:], scalar1=-1.0,
                                      scalar2=None, op0=ALU.mult))
                ctx.gp(v)
                ctx.v(v.tensor_tensor(out=hmax[:], in0=hmax[:], in1=hmin[:],
                                      op=ALU.max))

            s1 = ctx.snap()

            @blk.tensor
            def _(p):
                ctx.wait(p, s1, "V")
                for w in range(QW // 128):
                    ctx.p(p.transpose(out=trf_ps[:, 128 * w:128 * (w + 1)],
                                      in_=hmax[:, 128 * w:128 * (w + 1)],
                                      identity=per["identf"][:]))

            s2 = ctx.snap()

            @blk.vector
            def _(v):
                ctx.wait(v, s2, "P")
                nw = QW // 128
                for w in range(nw):
                    jt = (base // 128) + w
                    ctx.v(v.tensor_reduce(out=s2tok[:, jt:jt + 1],
                                          in_=trf_ps[:, 128 * w:128 * (w + 1)],
                                          axis=AXX, op=ALU.max))
                    ctx.gp(v)
                    ctx.v(v.tensor_scalar(out=s2tok[:, jt:jt + 1],
                                          in0=s2tok[:, jt:jt + 1],
                                          scalar1=1e-5, scalar2=None,
                                          op0=ALU.max))
                    ctx.gp(v)
                    ctx.v(v.tensor_scalar(out=inv2col[:, jt:jt + 1],
                                          in0=s2tok[:, jt:jt + 1],
                                          scalar1=per["wscal"][:, 3:4],
                                          scalar2=0.5, op0=ALU.mult,
                                          op1=ALU.mult))
                    ctx.v(v.reciprocal(out=tc[:, 6:7], in_=s2tok[:, jt:jt + 1]))
                    ctx.gp(v)
                    ctx.v(v.tensor_scalar(out=s2tok[:, jt:jt + 1], in0=tc[:, 6:7],
                                          scalar1=127.0, scalar2=None,
                                          op0=ALU.mult))

            s3 = ctx.snap()

            @blk.gpsimd
            def _(g):
                ctx.wait(g, s3, "V")
                nw = QW // 128
                jt = base // 128
                ctx.wd(g.dma_start(
                    out=drow[1, base:base + QW].rearrange("(t p) -> p t", p=128),
                    in_=s2tok[:, jt:jt + nw],
                    allow_slow_non_contiguous=True))

            s4 = ctx.snap()

            @blk.sync
            def _(q):
                ctx.wait(q, s4, "W")
                ctx.qd(q.dma_start(
                    out=s2bc[:],
                    in_=drow[1:2, base:base + QW].partition_broadcast(128)))

            s0 = ctx.snap()
            vq0 = ctx.cnt["V"]

            @blk.vector
            def _(v):
                ctx.wait(v, s0, "Q")
                for m in range(HCH):
                    hb = hbuf[:, m * QW:(m + 1) * QW]
                    ctx.v(v.tensor_tensor(out=qtmp[:], in0=hb, in1=s2bc[:],
                                          op=ALU.mult))
                    ctx.gp(v)
                    emit_rint(ctx, v, cfg, hqbuf[:, m * QW:(m + 1) * QW],
                              qtmp[:], hint[:])

            NW = QW // 128
            ent2 = ctx.snap()
            q2, p2 = ent2["Q"], ent2["P"]
            P_PER_M = NW * len(sp512(D))

            @blk.sync
            def _(q):
                for m in range(HCH):
                    if m >= 2:
                        ctx.waitv(q, "P", p2 + (m - 1) * P_PER_M)
                    ctx.qd(q.dma_start(
                        out=w2s[:, (m % 2) * D:(m % 2) * D + D],
                        in_=scr["w2qT"][128 * m:128 * (m + 1), :]))

            V_QPM = 3 if cfg.get("rint", "cast") == "cast" else 2

            @blk.tensor
            def _(p):
                for m in range(HCH):
                    ctx.waitv(p, "V", vq0 + (m + 1) * V_QPM)
                    ctx.waitv(p, "Q", q2 + 16 * (m + 1))
                    for w in range(NW):
                        for (o, wd) in sp512(D):
                            ctx.p(p.matmul(
                                out=y2_psl[w][:, o:o + wd],
                                lhsT=hqbuf[:, m * QW + 128 * w:m * QW + 128 * (w + 1)],
                                rhs=w2s[:, (m % 2) * D + o:(m % 2) * D + o + wd],
                                start=(m == 0), stop=(m == HCH - 1)))

            for w in range(NW):
                jt = (base // 128) + w
                s0 = ctx.snap()

                @blk.sync
                def _(q):
                    ctx.wait(q, s0, "V")
                    ctx.qd(q.dma_start(out=xtile[:, 0:D],
                                       in_=xres[b, 128 * jt:128 * (jt + 1), :]))

                s2 = ctx.snap()

                @blk.vector
                def _(v):
                    ctx.wait(v, s2, "P", "Q")
                    ctx.v(v.scalar_tensor_tensor(
                        out=resv[:], in0=y2_psl[w][:, 0:D],
                        scalar=inv2col[:, jt:jt + 1], in1=xtile[:, 0:D],
                        op0=ALU.mult, op1=ALU.add))

                s3 = ctx.snap()

                @blk.sync
                def _(q):
                    ctx.wait(q, s3, "V")
                    ctx.qd(q.dma_start(out=dst[b, 128 * jt:128 * (jt + 1), :],
                                       in_=resv[:]))
            blk_cm.__exit__(None, None, None)



def _attn_sublayer(ctx, cfg, b, scr, per):
    nc = ctx.nc
    S, D = cfg["S"], cfg["D"]
    NT, DCH, QCH = cfg["NT"], cfg["DCH"], cfg["QCH"]
    xres = scr["xres"]

    with scope() as es_q:
        qkvT = es_q.enter_context(nc.sbuf_tensor(ctx.name("a_qkvT"), [128, QCH * S], BF16))

        with scope() as es:
            t = lambda n, s, d: es.enter_context(nc.sbuf_tensor(ctx.name(n), s, d))
            xtile = t("a_x", [128, 2 * D], F32)
            xh = t("a_xh", [128, D], BF16)
            st6 = t("a_st6", [128, 24], F32)
            st2 = t("a_st2", [128, 2], F32)
            tc = t("a_tc", [128, 4], F32)
            xhT = t("a_xhT", [128, DCH * S], BF16)
            wq_all = t("a_wq", [128, DCH * 3 * D], BF16)
            tr_ps = es.enter_context(nc.psum_tensor(ctx.name("a_trps"), [128, D], BF16))
            qkv_ps = es.enter_context(nc.psum_tensor(ctx.name("a_qkvps"), [128, S], F32))

            ENTRY = ctx.snap()
            with nc.Block() as blk:
                @blk.sync
                def _(q):
                    ctx.wait(q, ENTRY, "V", "A", "P", "G", "W")
                    ctx.qd(q.dma_start(
                        out=wq_all[:].rearrange("p (c h) -> p c h", c=DCH),
                        in_=scr["wqkvT"].rearrange("(c p) h -> p c h", p=128)))

            a_evac = None
            with nc.Block() as blk:
                for j in range(NT):
                    s0 = ctx.snap()

                    @blk.sync
                    def _(q):
                        ctx.wait(q, s0, "V")
                        ctx.qd(q.dma_start(out=xtile[:, (j % 2) * D:(j % 2) * D + D],
                                           in_=xres[b, 128 * j:128 * (j + 1), :]))

                    s1 = ctx.snap()
                    xt = xtile[:, (j % 2) * D:(j % 2) * D + D]

                    @blk.vector
                    def _(v):
                        ctx.wait(v, s1, "Q")
                        for ii, (o, wd) in enumerate(sp512(D)):
                            ctx.v(v.bn_stats(out=st6[:, 6 * ii:6 * (ii + 1)],
                                             in_=xt[:, o:o + wd]))
                        ctx.gp(v)
                        ctx.v(v.bn_aggr(out=st2[:], in_=st6[:, 0:6 * len(sp512(D))]))

                    s2 = ctx.snap()

                    @blk.scalar
                    def _(a):
                        ctx.wait(a, s2, "V")
                        ctx.a(a.activation(out=tc[:, 0:1], in_=st2[:, 1:2],
                                           func=AF.Ln, bias=per["epsc"][:, 0:1],
                                           scale=1.0))
                        ctx.ga(a)
                        ctx.a(a.activation(out=tc[:, 1:2], in_=tc[:, 0:1],
                                           func=AF.Exp, scale=-0.5))

                    s3 = ctx.snap()

                    @blk.vector
                    def _(v):
                        ctx.wait(v, s3, "A")
                        ctx.v(v.tensor_scalar(out=xh[:], in0=xt,
                                              scalar1=st2[:, 0:1], scalar2=tc[:, 1:2],
                                              op0=ALU.subtract, op1=ALU.mult))

                    s4 = ctx.snap()

                    @blk.tensor
                    def _(p):
                        ctx.wait(p, s4, "V")
                        if a_evac:
                            ctx.waitv(p, "A", a_evac)
                        for c in range(DCH):
                            ctx.p(p.transpose(out=tr_ps[:, 128 * c:128 * (c + 1)],
                                              in_=xh[:, 128 * c:128 * (c + 1)],
                                              identity=per["ident"][:]))

                    s5 = ctx.snap()

                    @blk.scalar
                    def _(a):
                        ctx.wait(a, s5, "P")
                        o3 = xhT[:].rearrange("p (c s) -> p c s", s=S)[
                            :, :, 128 * j:128 * (j + 1)]
                        i3 = tr_ps[:].rearrange("p (c w) -> p c w", w=128)
                        ctx.a(a.copy(out=o3, in_=i3))
                    a_evac = ctx.cnt["A"]

            with nc.Block() as blk:
                for r in range(QCH):
                    s0 = ctx.snap()

                    @blk.tensor
                    def _(p):
                        ctx.wait(p, s0, "Q", "A")
                        for c in range(DCH):
                            for (o, wd) in sp512(S):
                                ctx.p(p.matmul(
                                    out=qkv_ps[:, o:o + wd],
                                    lhsT=wq_all[:, c * 3 * D + 128 * r:c * 3 * D + 128 * (r + 1)],
                                    rhs=xhT[:, c * S + o:c * S + o + wd],
                                    start=(c == 0), stop=(c == DCH - 1)))

                    s1 = ctx.snap()

                    @blk.scalar
                    def _(a):
                        ctx.wait(a, s1, "P")
                        ctx.a(a.activation(out=qkvT[:, r * S:(r + 1) * S],
                                           in_=qkv_ps[:, 0:S], func=AF.Identity,
                                           bias=per["bqkvcol"][:, r:r + 1],
                                           scale=1.0))

        with scope() as es:
            _attn_heads(ctx, cfg, b, qkvT, scr, per, es)

    with scope() as es:
        t = lambda n, s, d: es.enter_context(nc.sbuf_tensor(ctx.name(n), s, d))
        wo_all = t("o_wo", [128, DCH * D], BF16)
        ao_t = t("o_ao", [128, 2 * D], BF16)
        xtile = t("o_x", [128, D], F32)
        resv = t("o_res", [128, D], F32)
        wo_ps = es.enter_context(nc.psum_tensor(ctx.name("o_wops"), [128, D], F32))

        with nc.Block() as blk:
            s0 = ctx.snap()

            @blk.sync
            def _(q):
                ctx.wait(q, s0, "V", "A", "P", "W")
                ctx.qd(q.dma_start(
                    out=wo_all[:].rearrange("p (c h) -> p c h", c=DCH),
                    in_=scr["woT"].rearrange("(c p) h -> p c h", p=128)))

        v_res = None
        with nc.Block() as blk:
            for j in range(NT):
                s0 = ctx.snap()

                @blk.sync
                def _(q):
                    if j > 1:
                        ctx.waitv(q, "P", s0["P"])
                    ctx.qd(q.dma_start(
                        out=ao_t[:, (j % 2) * D:(j % 2) * D + D].rearrange(
                            "p (c w) -> p c w", w=128),
                        in_=scr["aoT"].rearrange("(c p) s -> p c s", p=128)[
                            :, :, 128 * j:128 * (j + 1)]))
                    ctx.waitv(q, "V", s0["V"])
                    ctx.qd(q.dma_start(out=xtile[:],
                                       in_=xres[b, 128 * j:128 * (j + 1), :]))

                s1 = ctx.snap()

                @blk.tensor
                def _(p):
                    ctx.wait(p, s1, "Q")
                    if v_res:
                        ctx.waitv(p, "V", v_res)
                    for c in range(DCH):
                        for (o, wd) in sp512(D):
                            ctx.p(p.matmul(
                                out=wo_ps[:, o:o + wd],
                                lhsT=ao_t[:, (j % 2) * D + 128 * c:(j % 2) * D + 128 * (c + 1)],
                                rhs=wo_all[:, c * D + o:c * D + o + wd],
                                start=(c == 0), stop=(c == DCH - 1)))

                s2 = ctx.snap()

                @blk.vector
                def _(v):
                    ctx.wait(v, s2, "P", "Q")
                    ctx.v(v.scalar_tensor_tensor(out=resv[:], in0=wo_ps[:],
                                                 scalar=0.0, in1=per["bobc"][:],
                                                 op0=ALU.add, op1=ALU.add))
                    ctx.gp(v)
                    ctx.v(v.tensor_tensor(out=resv[:], in0=resv[:], in1=xtile[:],
                                          op=ALU.add))
                v_res = ctx.cnt["V"]

                s3 = ctx.snap()

                @blk.sync
                def _(q):
                    ctx.wait(q, s3, "V")
                    ctx.qd(q.dma_start(out=xres[b, 128 * j:128 * (j + 1), :],
                                       in_=resv[:]))


def _attn_heads(ctx, cfg, b, qkvT, scr, per, es):
    nc = ctx.nc
    S, D, NH, DK = cfg["S"], cfg["D"], cfg["NH"], cfg["DK"]
    NT, KCH, GQ, NG = cfg["NT"], cfg["KCH"], cfg["GQ"], cfg["NG"]
    GW = GQ * 128
    TOPK = cfg["TOPK"]
    NPR = float(S)
    DCH = D // 128
    drow, aoT = scr["drow"], scr["aoT"]

    (RM, NM, BET, LT, LTP, SLP, CA, CB, ZC, ZP, TST, RHO, TA, TB, TC, TD,
     TBS, CBS, TBG) = range(19)
    NR = 7
    cslots = list(range(19, 19 + NR))
    tslots = list(range(19 + NR, 19 + 2 * NR))
    cslotsB = list(range(19 + 2 * NR, 19 + 3 * NR))
    NSLOT = 19 + 3 * NR
    TGT = float(TOPK) - 0.5
    SSZ = S // 32
    islot = min(7, max(0, int(round(TOPK / 32.0)) - 1))
    aslot = max(0, islot - 2)
    bslot = min(7, islot + 2)
    def _zof(c):
        return 2.0 * (c - TGT) / (c + TGT)
    sden = _zof((bslot + 1) * 32.0) - _zof((aslot + 1) * 32.0)
    ZF = 4.0 / float(TOPK)

    t = lambda n, s, d: es.enter_context(nc.sbuf_tensor(ctx.name(n), s, d))
    vtm2 = t("h_vtm", [128, 2 * KCH * (DK + 1)], BF16)
    ebuf2 = t("h_e", [128, 2 * GQ * S], BF16)
    gbuf = t("h_g", [128, GQ * S], BF16)
    junk = t("h_junk", [128, S], BF16)
    scrt = t("h_scrt", [128, S], BF16)
    gTbuf = t("h_gT", [128, KCH * GW], BF16)
    candb = t("h_cand", [128, GQ * 16], F32)
    ohb = t("h_ohb", [128, GQ * 16], F32)
    kscr = t("h_kscr", [128, 8], BF16)
    smp82 = t("h_smp8", [128, 2 * 8 * GQ], BF16)
    gcol2 = t("h_gcol", [128, 2 * 40 * GQ], F32)
    invsc = t("h_invsc", [DK + 1, GW], F32)
    invbc = t("h_invbc", [64, GW], F32)
    aot_t = t("h_aot", [64, GW], BF16)
    sc_ps = es.enter_context(nc.psum_tensor(ctx.name("h_scps"), [128, S], F32))
    tr_ps2 = es.enter_context(nc.psum_tensor(ctx.name("h_trps2"), [128, S], BF16))
    pv_ps = es.enter_context(nc.psum_tensor(ctx.name("h_pvps"), [DK + 1, GW], F32))

    blk_cm = nc.Block()
    blk = blk_cm.__enter__()

    groups = [(h, gi) for h in range(NH) for gi in range(NG)]
    NGR = len(groups)

    def P(k):
        return (k % 2)

    def eb(k, tt):
        return ebuf2[:, (P(k) * GQ + tt) * S:(P(k) * GQ + tt + 1) * S]

    def gc(k, i):
        o = P(k) * 40 * GQ
        return gcol2[:, o + i * GQ:o + (i + 1) * GQ]

    def gc1(k, i, tt):
        o = P(k) * 40 * GQ
        return gcol2[:, o + i * GQ + tt:o + i * GQ + tt + 1]

    def vtm(k):
        h = groups[k][0]
        o = (h % 2) * KCH * (DK + 1)
        return vtm2[:, o:o + KCH * (DK + 1)]

    def smp8(k):
        return smp82[:, P(k) * 8 * GQ:(P(k) + 1) * 8 * GQ]

    def qkslice(k):
        h, gi = groups[k]
        ch_q = h // 2
        off = 64 * (h % 2)
        qT = qkvT[off:off + 64, ch_q * S:(ch_q + 1) * S]
        kT = qkvT[off:off + 64, (DCH + ch_q) * S:(DCH + ch_q + 1) * S]
        vT = qkvT[off:off + 64, (2 * DCH + ch_q) * S:(2 * DCH + ch_q + 1) * S]
        return qT, kT, vT

    s0 = ctx.snap()

    @blk.vector
    def _(v):
        ctx.wait(v, s0, "A", "P")
        ctx.v(v.memset(vtm2[:].rearrange("p (c w) -> p c w", w=DK + 1)[
            :, :, DK:DK + 1], 1.0))

    def head_prep(k):
        h, _ = groups[k]
        off = 64 * (h % 2)
        _, _, vT = qkslice(k)
        vt = vtm(k)
        sp0 = ctx.snap()

        @blk.tensor
        def _(p):
            ctx.wait(p, sp0, "A", "V")
            for c in range(KCH):
                ctx.p(p.transpose(out=tr_ps2[:, DK * c:DK * (c + 1)],
                                  in_=vT[:, 128 * c:128 * (c + 1)],
                                  identity=per["ident2"][off:off + 64, :]))

        sp1 = ctx.snap()

        @blk.scalar
        def _(a):
            ctx.wait(a, sp1, "P")
            ctx.a(a.copy(
                out=vt.rearrange("p (c w) -> p c w", w=DK + 1)[:, :, 0:DK],
                in_=tr_ps2[:, 0:DK * KCH].rearrange("p (c w) -> p c w",
                                                    w=DK)))

    P_QK = len(sp512(S))
    pstate = {"exp_a": 0}

    def s1_qk(k, tt):
        h, gi = groups[k]
        qT, kT, _ = qkslice(k)
        jt = gi * GQ + tt
        @blk.tensor
        def _(p):
            ctx.waitv(p, "A", pstate["exp_a"])
            for (o, wd) in sp512(S):
                ctx.p(p.matmul(out=sc_ps[:, o:o + wd],
                               lhsT=qT[:, 128 * jt:128 * (jt + 1)],
                               rhs=kT[:, o:o + wd],
                               start=True, stop=True))

    def s1_rx(k, tt):
        st1 = ctx.snap()

        @blk.vector
        def _(v):
            ctx.wait(v, st1, "P")
            ctx.v(v.tensor_reduce(out=gc1(k, RM, tt), in_=sc_ps[:, 0:S],
                                  axis=AXX, op=ALU.max))
            ctx.gp(v)
            ctx.v(v.tensor_scalar(out=gc1(k, NM, tt), in0=gc1(k, RM, tt),
                                  scalar1=-0.125, scalar2=None,
                                  op0=ALU.mult))

    def s1_exp(k, tt):
        st2 = ctx.snap()

        @blk.scalar
        def _(a):
            ctx.wait(a, st2, "V")
            ctx.a(a.activation(out=eb(k, tt), in_=sc_ps[:, 0:S],
                               func=AF.Exp, bias=gc1(k, NM, tt),
                               scale=0.125))
        pstate["exp_a"] = ctx.cnt["A"]

    def s1_smp(k):
        sm0 = ctx.snap()

        @blk.scalar
        def _(a):
            ctx.wait(a, sm0, "V")
            ctx.a(a.activation(out=gc(k, BET), in_=gc(k, NM),
                               func=AF.Exp, scale=1.0))

        sm1 = ctx.snap()

        @blk.vector
        def _(v):
            ctx.wait(v, sm1, "A")
            for tt in range(GQ):
                ctx.v(v.tensor_copy(
                    out=scrt[:, SSZ * tt:SSZ * (tt + 1)].unsqueeze(2),
                    in_=eb(k, tt).rearrange("p (a b) -> p a b",
                                            b=32)[:, :, 0:1]))
            for tt in range(GQ):
                ctx.v(v.max(out=smp8(k)[:, 8 * tt:8 * (tt + 1)],
                            in_=scrt[:, SSZ * tt:SSZ * (tt + 1)]))
            l3 = smp8(k)[:].rearrange("p (t k) -> p t k", k=8)
            ctx.gp(v)
            ctx.v(v.tensor_copy(out=gc(k, LT).unsqueeze(2),
                                in_=l3[:, :, islot:islot + 1]))
            ctx.v(v.tensor_tensor(out=gc(k, SLP).unsqueeze(2),
                                  in0=l3[:, :, aslot:aslot + 1],
                                  in1=l3[:, :, bslot:bslot + 1],
                                  op=ALU.subtract))
            ctx.v(v.memset(gc(k, CA), 0.0))
            ctx.v(v.memset(gc(k, CB), 1.01))
            ctx.v(v.memset(gc(k, ZC), 0.0))
            ctx.v(v.memset(gc(k, LTP), 0.0))
            ctx.v(v.tensor_scalar(out=gc(k, SLP), in0=gc(k, SLP),
                                  scalar1=1.0 / sden, scalar2=1e-37,
                                  op0=ALU.mult, op1=ALU.max))
            ctx.gp(v)
            ctx.v(v.tensor_copy(out=gc(k, tslots[0]), in_=gc(k, LT)))
            ctx.v(v.tensor_scalar(out=gc(k, TBG), in0=gc(k, LT),
                                  scalar1=65536.0, scalar2=None,
                                  op0=ALU.mult))

    def s2_A(k, rnd):
        sT = ctx.snap()

        @blk.scalar
        def _(a):
            ctx.wait(a, sT, "V")
            for tt in range(GQ):
                ctx.a(a.activation(
                    out=junk[:, 0:S // 2], in_=eb(k, tt)[:, 0:S // 2],
                    func=AF.Tanh,
                    bias=gc1(k, TBG, tt), scale=-65536.0,
                    accum_out=gc1(k, cslots[rnd], tt)))

    def s2_probeV(k, rnd, half=False):
        sP = ctx.snap()

        @blk.vector
        def _(v):
            if not half:
                ctx.wait(v, sP, "A")
            for tt in range(GQ):
                if half:
                    ctx.v(v.tensor_scalar(
                        out=scrt[:, 0:S // 2], in0=eb(k, tt)[:, S // 2:S],
                        scalar1=gc1(k, tslots[rnd], tt), scalar2=None,
                        op0=ALU.is_ge, op1=ALU.add,
                        accum_out=gc1(k, cslotsB[rnd], tt)))
                else:
                    ctx.v(v.tensor_scalar(
                        out=junk[:], in0=eb(k, tt),
                        scalar1=gc1(k, tslots[rnd], tt), scalar2=None,
                        op0=ALU.is_ge, op1=ALU.add,
                        accum_out=gc1(k, cslots[rnd], tt)))

    def s2_chainz(k, rnd):
        sgn = rnd < NR - 2
        sC = ctx.snap()

        @blk.vector
        def _(v):
            if sgn:
                ctx.wait(v, sC, "A")
            cs = cslots[rnd]
            ctx.gp(v)
            if sgn:
                ctx.v(v.tensor_scalar(out=gc(k, TA), in0=gc(k, cs),
                                      scalar1=-0.5, scalar2=NPR / 4.0,
                                      op0=ALU.mult, op1=ALU.add))
                ctx.v(v.tensor_copy(out=gc(k, ZP), in_=gc(k, ZC)))
                ctx.v(v.tensor_tensor(out=gc(k, TA), in0=gc(k, TA),
                                      in1=gc(k, cslotsB[rnd]), op=ALU.add))
            else:
                ctx.v(v.tensor_copy(out=gc(k, TA), in_=gc(k, cs)))
                ctx.v(v.tensor_copy(out=gc(k, ZP), in_=gc(k, ZC)))
            ctx.gp(v)
            ctx.v(v.tensor_scalar(out=gc(k, TB), in0=gc(k, TA),
                                  scalar1=TGT, scalar2=None,
                                  op0=ALU.is_ge))
            ctx.v(v.tensor_scalar(out=gc(k, TC), in0=gc(k, TA),
                                  scalar1=2.0, scalar2=-2.0 * TGT,
                                  op0=ALU.mult, op1=ALU.add))
            ctx.v(v.tensor_scalar(out=gc(k, TD), in0=gc(k, TA),
                                  scalar1=TGT, scalar2=None,
                                  op0=ALU.add))
            ctx.gp(v)
            ctx.v(v.reciprocal(out=gc(k, TD), in_=gc(k, TD)))
            ctx.v(v.tensor_scalar(out=gc(k, TA), in0=gc(k, TB),
                                  scalar1=1000.0, scalar2=-1000.0,
                                  op0=ALU.mult, op1=ALU.add))
            ctx.gp(v)
            ctx.v(v.tensor_tensor(out=gc(k, ZC), in0=gc(k, TC),
                                  in1=gc(k, TD), op=ALU.mult))
            ctx.v(v.tensor_scalar(out=gc(k, TD), in0=gc(k, TB),
                                  scalar1=1000.0, scalar2=None,
                                  op0=ALU.mult))
            ctx.v(v.tensor_tensor(out=gc(k, TA), in0=gc(k, TA),
                                  in1=gc(k, LT), op=ALU.add))
            ctx.v(v.tensor_tensor(out=gc(k, TD), in0=gc(k, TD),
                                  in1=gc(k, LT), op=ALU.add))
            ctx.v(v.tensor_tensor(out=gc(k, CA), in0=gc(k, CA),
                                  in1=gc(k, TA), op=ALU.max))
            ctx.v(v.tensor_tensor(out=gc(k, CB), in0=gc(k, CB),
                                  in1=gc(k, TD), op=ALU.min))
            if rnd > 0:
                ctx.v(v.tensor_tensor(out=gc(k, TC), in0=gc(k, LT),
                                      in1=gc(k, LTP), op=ALU.subtract))
                ctx.v(v.tensor_tensor(out=gc(k, TD), in0=gc(k, ZC),
                                      in1=gc(k, ZP), op=ALU.subtract))
                ctx.v(v.tensor_scalar(out=gc(k, TB), in0=gc(k, TC),
                                      scalar1=-1.0, scalar2=None,
                                      op0=ALU.mult))
                ctx.v(v.tensor_scalar(out=gc(k, TA), in0=gc(k, TD),
                                      scalar1=-1.0, scalar2=None,
                                      op0=ALU.mult))
                ctx.v(v.tensor_tensor(out=gc(k, TC), in0=gc(k, TC),
                                      in1=gc(k, TB), op=ALU.max))
                ctx.v(v.tensor_tensor(out=gc(k, TD), in0=gc(k, TD),
                                      in1=gc(k, TA), op=ALU.max))
                ctx.gp(v)
                ctx.v(v.tensor_scalar(out=gc(k, TD), in0=gc(k, TD),
                                      scalar1=ZF, scalar2=None,
                                      op0=ALU.max))
                ctx.gp(v)
                ctx.v(v.reciprocal(out=gc(k, TD), in_=gc(k, TD)))
                ctx.gp(v)
                ctx.v(v.tensor_tensor(out=gc(k, TC), in0=gc(k, TC),
                                      in1=gc(k, TD), op=ALU.mult))
                ctx.v(v.tensor_scalar(out=gc(k, TD), in0=gc(k, SLP),
                                      scalar1=0.3, scalar2=None,
                                      op0=ALU.mult))
                ctx.v(v.tensor_scalar(out=gc(k, TB), in0=gc(k, SLP),
                                      scalar1=4.0, scalar2=None,
                                      op0=ALU.mult))
                ctx.v(v.tensor_tensor(out=gc(k, TC), in0=gc(k, TC),
                                      in1=gc(k, TD), op=ALU.max))
                ctx.gp(v)
                ctx.v(v.tensor_tensor(out=gc(k, SLP), in0=gc(k, TC),
                                      in1=gc(k, TB), op=ALU.min))
            ctx.gp(v)
            ctx.v(v.tensor_tensor(out=gc(k, TC), in0=gc(k, ZC),
                                  in1=gc(k, SLP), op=ALU.mult))
            caprel = -0.5 if rnd == 0 else -0.95
            ctx.v(v.tensor_scalar(out=gc(k, TD), in0=gc(k, LT),
                                  scalar1=caprel, scalar2=None,
                                  op0=ALU.mult))
            ctx.gp(v)
            ctx.v(v.tensor_tensor(out=gc(k, TC), in0=gc(k, TC),
                                  in1=gc(k, TD), op=ALU.max))
            ctx.v(v.tensor_copy(out=gc(k, LTP), in_=gc(k, LT)))
            ctx.gp(v)
            ctx.v(v.tensor_tensor(out=gc(k, TD), in0=gc(k, LT),
                                  in1=gc(k, TC), op=ALU.add))
            ctx.gp(v)
            ctx.v(v.tensor_tensor(out=gc(k, TB), in0=gc(k, TD),
                                  in1=gc(k, CA), op=ALU.is_ge))
            ctx.v(v.tensor_tensor(out=gc(k, TC), in0=gc(k, TD),
                                  in1=gc(k, CB), op=ALU.is_le))
            ctx.gp(v)
            ctx.v(v.tensor_tensor(out=gc(k, TB), in0=gc(k, TB),
                                  in1=gc(k, TC), op=ALU.mult))
            ctx.v(v.tensor_tensor(out=gc(k, TC), in0=gc(k, CA),
                                  in1=gc(k, CB), op=ALU.add))
            ctx.gp(v)
            ctx.v(v.tensor_scalar(out=gc(k, TC), in0=gc(k, TC),
                                  scalar1=0.5, scalar2=None,
                                  op0=ALU.mult))
            ctx.gp(v)
            ctx.v(v.tensor_tensor(out=gc(k, TD), in0=gc(k, TD),
                                  in1=gc(k, TC), op=ALU.subtract))
            ctx.gp(v)
            ctx.v(v.tensor_tensor(out=gc(k, TD), in0=gc(k, TD),
                                  in1=gc(k, TB), op=ALU.mult))
            ctx.gp(v)
            ctx.v(v.tensor_tensor(out=gc(k, LT), in0=gc(k, TC),
                                  in1=gc(k, TD), op=ALU.add))
            ctx.gp(v)
            ctx.v(v.tensor_copy(out=gc(k, tslots[rnd + 1]), in_=gc(k, LT)))
            ctx.v(v.tensor_scalar(out=gc(k, TBG), in0=gc(k, LT),
                                  scalar1=65536.0, scalar2=None,
                                  op0=ALU.mult))

    def s3(k):
        sX = ctx.snap()

        @blk.vector
        def _(v):
            ctx.wait(v, sX, "A", "P")
            ctx.gp(v)
            ctx.v(v.tensor_scalar(out=gc(k, TC), in0=gc(k, cslots[-2]),
                                  scalar1=-(float(TOPK) - 0.5),
                                  scalar2=None, op0=ALU.add))
            ctx.v(v.tensor_scalar(out=gc(k, TD), in0=gc(k, cslots[-1]),
                                  scalar1=-(float(TOPK) - 0.5),
                                  scalar2=None, op0=ALU.add))
            ctx.v(v.tensor_scalar(out=gc(k, TA), in0=gc(k, TC),
                                  scalar1=-1.0, scalar2=None,
                                  op0=ALU.mult))
            ctx.v(v.tensor_scalar(out=gc(k, TB), in0=gc(k, TD),
                                  scalar1=-1.0, scalar2=None,
                                  op0=ALU.mult))
            ctx.v(v.tensor_tensor(out=gc(k, TC), in0=gc(k, TC),
                                  in1=gc(k, TA), op=ALU.max))
            ctx.v(v.tensor_tensor(out=gc(k, TD), in0=gc(k, TD),
                                  in1=gc(k, TB), op=ALU.max))
            ctx.gp(v)
            ctx.v(v.tensor_tensor(out=gc(k, TA), in0=gc(k, TC),
                                  in1=gc(k, TD), op=ALU.is_ge))
            ctx.v(v.tensor_tensor(out=gc(k, TB), in0=gc(k, tslots[-1]),
                                  in1=gc(k, tslots[-2]), op=ALU.subtract))
            ctx.v(v.tensor_tensor(out=gc(k, TC), in0=gc(k, cslots[-1]),
                                  in1=gc(k, cslots[-2]), op=ALU.subtract))
            ctx.v(v.tensor_tensor(out=gc(k, TB), in0=gc(k, TB),
                                  in1=gc(k, TA), op=ALU.mult))
            ctx.v(v.tensor_tensor(out=gc(k, TC), in0=gc(k, TC),
                                  in1=gc(k, TA), op=ALU.mult))
            ctx.v(v.tensor_tensor(out=gc(k, TBS), in0=gc(k, tslots[-2]),
                                  in1=gc(k, TB), op=ALU.add))
            ctx.v(v.tensor_tensor(out=gc(k, CBS), in0=gc(k, cslots[-2]),
                                  in1=gc(k, TC), op=ALU.add))
            for tt in range(GQ):
                e_t = eb(k, tt)
                sel_t = gbuf[:, tt * S:(tt + 1) * S]
                ctx.gp(v)
                ctx.v(v.tensor_scalar(out=scrt[:], in0=e_t,
                                      scalar1=gc1(k, TBS, tt),
                                      scalar2=-4.0,
                                      op0=ALU.is_lt, op1=ALU.mult))
                ctx.gp(v)
                ctx.v(v.tensor_tensor(out=sel_t, in0=scrt[:],
                                      in1=e_t, op=ALU.subtract))
                ctx.gp(v)
                ctx.v(v.max(out=kscr[:], in_=sel_t))
                ctx.gp(v)
                ctx.v(v.tensor_scalar(out=scrt[:], in0=e_t,
                                      scalar1=gc1(k, TBS, tt),
                                      scalar2=-4.0,
                                      op0=ALU.is_ge, op1=ALU.mult))
                ctx.v(v.tensor_scalar(out=candb[:, tt * 16:tt * 16 + 8],
                                      in0=kscr[:], scalar1=-1.0,
                                      scalar2=None, op0=ALU.mult))
                ctx.v(v.tensor_tensor(out=sel_t, in0=scrt[:], in1=e_t,
                                      op=ALU.add))
                ctx.gp(v)
                ctx.v(v.max(out=kscr[:], in_=sel_t))
                ctx.gp(v)
                ctx.v(v.tensor_scalar(
                    out=candb[:, tt * 16 + 8:tt * 16 + 16],
                    in0=kscr[:], scalar1=1.0, scalar2=None, op0=ALU.mult))
            ctx.gp(v)
            ctx.v(v.tensor_scalar(out=gc(k, RHO), in0=gc(k, CBS),
                                  scalar1=-float(TOPK - 1), scalar2=None,
                                  op0=ALU.add))
            ctx.gp(v)
            ctx.v(v.tensor_scalar(out=gc(k, RHO), in0=gc(k, RHO),
                                  scalar1=-7.0, scalar2=8.0,
                                  op0=ALU.max, op1=ALU.min))
            ctx.gp(v)
            iview = per["iota16"][:].unsqueeze(1).to_broadcast(
                [128, GQ, 16])
            rview = gc(k, RHO).unsqueeze(2).to_broadcast([128, GQ, 16])
            ctx.v(v.tensor_tensor(
                out=ohb[:].rearrange("p (t k) -> p t k", k=16),
                in0=iview, in1=rview, op=ALU.is_equal))
            ctx.gp(v)
            ctx.v(v.tensor_tensor(out=ohb[:], in0=ohb[:], in1=candb[:],
                                  op=ALU.mult))
            ctx.gp(v)
            ctx.v(v.tensor_reduce(
                out=gc(k, TST).unsqueeze(2),
                in_=ohb[:].rearrange("p (t k) -> p t k", k=16),
                axis=AXX, op=ALU.add))
            ctx.v(v.tensor_scalar(out=gc(k, TB), in0=gc(k, BET),
                                  scalar1=1.0, scalar2=1e-9 / NPR,
                                  op0=ALU.max, op1=ALU.mult))
            ctx.gp(v)
            ctx.v(v.tensor_tensor(out=gc(k, TA), in0=gc(k, TB),
                                  in1=gc(k, BET), op=ALU.add))
            for pp in range(0, GQ, 2):
                for tt in (pp, pp + 1):
                    ctx.v(v.tensor_scalar(
                        out=gbuf[:, tt * S:(tt + 1) * S], in0=eb(k, tt),
                        scalar1=gc1(k, TST, tt), scalar2=None,
                        op0=ALU.is_ge))
                for tt in (pp, pp + 1):
                    sc_b = scrt if tt % 2 == 0 else junk
                    ctx.v(v.tensor_scalar(out=sc_b[:], in0=eb(k, tt),
                                          scalar1=gc1(k, BET, tt),
                                          scalar2=None,
                                          op0=ALU.subtract))
                for tt in (pp, pp + 1):
                    sc_b = scrt if tt % 2 == 0 else junk
                    ctx.v(v.tensor_tensor(
                        out=gbuf[:, tt * S:(tt + 1) * S], in0=sc_b[:],
                        in1=gbuf[:, tt * S:(tt + 1) * S], op=ALU.mult))
            for tt in range(GQ):
                ctx.v(v.tensor_scalar(out=gbuf[:, tt * S:(tt + 1) * S],
                                      in0=gbuf[:, tt * S:(tt + 1) * S],
                                      scalar1=gc1(k, TA, tt),
                                      scalar2=None, op0=ALU.add))

    def s4a(k):
        h, gi = groups[k]
        v_end = ctx.cnt["V"]
        aev0 = ctx.cnt["A"]
        HKC = KCH // 2
        NHF = 2 * GQ

        @blk.tensor
        def _(p):
            for H in range(NHF):
                tt, hf = H // 2, H % 2
                if H == 0:
                    ctx.waitv(p, "V", v_end)
                    ctx.waitv(p, "A", aev0)
                elif H == 1:
                    pass
                else:
                    ctx.waitv(p, "A", aev0 + H - 1)
                for c in range(HKC):
                    cc = hf * HKC + c
                    ctx.p(p.transpose(
                        out=tr_ps2[:, (S // 2) * hf + 128 * c:
                                   (S // 2) * hf + 128 * (c + 1)],
                        in_=gbuf[:, tt * S + 128 * cc:
                                 tt * S + 128 * (cc + 1)],
                        identity=per["ident"][:]))

        pev0 = ctx.cnt["P"] - NHF * HKC

        @blk.scalar
        def _(a):
            for H in range(NHF):
                tt, hf = H // 2, H % 2
                ctx.waitv(a, "P", pev0 + (H + 1) * HKC)
                o3 = gTbuf[:].rearrange("p (c w) -> p c w", w=GW)[
                    :, hf * HKC:(hf + 1) * HKC, 128 * tt:128 * (tt + 1)]
                i3 = tr_ps2[:, (S // 2) * hf:(S // 2) * (hf + 1)].rearrange(
                    "p (c w) -> p c w", w=128)
                ctx.a(a.copy(out=o3, in_=i3))

        sJ = ctx.snap()

        @blk.tensor
        def _(p):
            ctx.wait(p, sJ, "A", "V")
            for c in range(KCH):
                ctx.p(p.matmul(
                    out=pv_ps[:],
                    lhsT=vtm(k)[:, (DK + 1) * c:(DK + 1) * (c + 1)],
                    rhs=gTbuf[:, c * GW:(c + 1) * GW],
                    start=(c == 0), stop=(c == KCH - 1)))

        sK = ctx.snap()

        @blk.scalar
        def _(a):
            ctx.wait(a, sK, "P", "Q")
            ctx.a(a.activation(out=invsc[DK:DK + 1, :],
                               in_=pv_ps[DK:DK + 1, :], func=AF.Ln,
                               bias=per["epsc"][64:65, 2:3], scale=1.0))
            ctx.ga(a)
            ctx.a(a.activation(out=invsc[DK:DK + 1, :],
                               in_=invsc[DK:DK + 1, :], func=AF.Exp,
                               scale=-1.0))

        sL2 = ctx.snap()

        @blk.sync
        def _(q):
            ctx.wait(q, sL2, "A", "V")
            ctx.qd(q.dma_start(out=drow[5, 0:GW],
                               in_=invsc[DK:DK + 1, :]))
            ctx.waitv(q, "Q", ctx.cnt["Q"])
            ctx.qd(q.dma_start(out=invbc[:],
                               in_=drow[5:6, 0:GW].partition_broadcast(64)))

    def s4b(k):
        h, gi = groups[k]
        sM = ctx.snap()

        @blk.vector
        def _(v):
            ctx.wait(v, sM, "Q")
            ctx.v(v.tensor_tensor(out=aot_t[:], in0=pv_ps[0:DK, :],
                                  in1=invbc[:], op=ALU.mult))

        sN = ctx.snap()

        @blk.sync
        def _(q):
            ctx.wait(q, sN, "V")
            ctx.qd(q.dma_start(
                out=aoT[64 * h:64 * h + 64, gi * GW:(gi + 1) * GW],
                in_=aot_t[:]))
            if scr.get("dbg") is not None and \
                    30 * GQ * (k + 1) <= 4096:
                ctx.qd(q.dma_start(
                    out=scr["dbg"][:, 30 * GQ * k:30 * GQ * (k + 1)],
                    in_=gcol2[:, P(k) * 30 * GQ:(P(k) + 1) * 30 * GQ]))

    pstate["exp_a"] = ctx.cnt["A"]

    def s1_all(k):
        if groups[k][1] == 0:
            head_prep(k)
        for tt in range(GQ):
            s1_qk(k, tt)
            s1_rx(k, tt)
            s1_exp(k, tt)
        s1_smp(k)

    s1_all(0)
    for k in range(NGR):
        nxt = k + 1 if k + 1 < NGR else None
        for rnd in range(NR - 2):
            s2_A(k, rnd)
            if nxt is not None and rnd < GQ:
                s1_qk(nxt, rnd)
            s2_probeV(k, rnd, half=True)
            if nxt is not None and rnd < GQ:
                s1_rx(nxt, rnd)
            s2_chainz(k, rnd)
            if nxt is not None and rnd < GQ:
                s1_exp(nxt, rnd)
            if rnd == 0 and k > 0:
                s4a(k - 1)
            if rnd == 1:
                if k > 0:
                    s4b(k - 1)
                if nxt is not None and groups[nxt][1] == 0:
                    head_prep(nxt)
        if nxt is not None:
            s1_smp(nxt)
        for rnd in range(NR - 2, NR):
            s2_probeV(k, rnd)
            if rnd < NR - 1:
                s2_chainz(k, rnd)
        s3(k)
        if k == NGR - 1:
            s4a(k)
            s4b(k)

    blk_cm.__exit__(None, None, None)



_PROG_CACHE = {}


def kernel(L0_attn_x, L0_attn_Wqkv, L0_attn_bqkv, L0_attn_Wo, L0_attn_bo,
           L0_ffn_w1, L0_ffn_conv_w, L0_ffn_conv_b, L0_ffn_w2):
    x = np.ascontiguousarray(np.asarray(L0_attn_x, dtype=np.float32))
    B, S, D = x.shape
    n_cores = 8
    nb = B // n_cores
    cfg = make_cfg(S=S, D=D, HF=L0_ffn_w1.shape[0], NH=16, NB=nb)

    key = (B, S, D)
    if key not in _PROG_CACHE:
        _PROG_CACHE[key] = build_program(cfg)
    nc = _PROG_CACHE[key]

    w = dict(
        Wqkv=np.ascontiguousarray(np.asarray(L0_attn_Wqkv, dtype=np.float32)),
        bqkv=np.ascontiguousarray(np.asarray(L0_attn_bqkv, dtype=np.float32)),
        Wo=np.ascontiguousarray(np.asarray(L0_attn_Wo, dtype=np.float32)),
        bo=np.ascontiguousarray(np.asarray(L0_attn_bo, dtype=np.float32)),
        w1=np.ascontiguousarray(np.asarray(L0_ffn_w1, dtype=np.float32)),
        cw=np.ascontiguousarray(
            np.asarray(L0_ffn_conv_w, dtype=np.float32).reshape(-1, 3)),
        cb=np.ascontiguousarray(np.asarray(L0_ffn_conv_b, dtype=np.float32)),
        w2=np.ascontiguousarray(np.asarray(L0_ffn_w2, dtype=np.float32)),
    )
    in_maps = []
    for c in range(n_cores):
        m = dict(w)
        m["x_in"] = x[c * nb:(c + 1) * nb]
        in_maps.append(m)

    res = run_bass_kernel_spmd(nc, in_maps, core_ids=list(range(n_cores)))
    return np.concatenate([r["out"] for r in res.results], axis=0)
